# version 02
# baseline (speedup 1.0000x reference)
import numpy as np
import ml_dtypes
import concourse.bass as bass
import concourse.mybir as mybir
from concourse.bass_utils import run_bass_kernel_spmd

F32 = mybir.dt.float32
BF16 = mybir.dt.bfloat16
AF = mybir.ActivationFunctionType
ALU = mybir.AluOpType
AX = mybir.AxisListType

D = 4096
T = 4096
HD = 128
H = 32
NCH = 32
NW = 8
OWN_W = 4
OT = 2048
LIMS = [8, 16, 24, 32]
OWN_SUPER = {0: [0, 3, 4, 7], 1: [1, 2, 5, 6]}
EPS = 1e-5
SM_SCALE = float(1.0 / np.sqrt(128.0))
Q_LO, Q_HI = 611, 4095
NTOK_R = Q_HI - Q_LO


class Sched:

    ENGINES = ("pe", "act", "dve", "pool", "sp")

    NQROT = 8

    def __init__(self):
        self.streams = {e: [] for e in self.ENGINES}
        self.cnt = {}
        self.lastw = {}
        self.qrot = {}

    def emit(self, eng, fn, deps=(), inc=None, inc_by=1):
        wl = []
        for tok in deps:
            if tok is None:
                continue
            s, v = tok
            if v > self.lastw.get((eng, s), 0):
                wl.append((s, v))
                self.lastw[(eng, s)] = v
        tok = None
        if inc is not None:
            if inc in ("qsp", "qact", "qpool"):
                r = self.qrot.get(inc, 0)
                self.qrot[inc] = (r + 1) % self.NQROT
                inc = f"{inc}{r}"
            self.cnt[inc] = self.cnt.get(inc, 0) + inc_by
            tok = (inc, self.cnt[inc])
        self.streams[eng].append((wl, fn, inc, inc_by))
        return tok

    def qbar(self, prefix):
        return [(f"{prefix}{r}", self.cnt.get(f"{prefix}{r}", 0))
                for r in range(self.NQROT)]

    def bar(self, name):
        return (name, self.cnt.get(name, 0))


class _Stop(Exception):
    pass


def build_program(upto="d"):
    nc = bass.Bass("TRN2", target_bir_lowering=False)

    def din(name, shape, dt=F32):
        return nc.dram_tensor(name, list(shape), dt, kind="ExternalInput").ap()

    x_glob = din("x_glob", (T, D))
    xtok_own = din("xtok_own", (OT, D))
    xT_glob = din("xT_glob", (NW, NCH, 128, 512))
    xT_own = din("xT_own", (OWN_W, NCH, 128, 512))
    wq = din("wq", (NCH, NCH, 128, 128))
    wk = din("wk", (NCH, 128, 128))
    wv = din("wv", (NCH, 128, 128))
    wo = din("wo", (NCH, NCH, 128, 128))
    wW = din("wW", (NCH, NCH, 128, 128))
    wA = din("wA", (NCH, 128, 128))
    wB = din("wB", (NCH, 128, 128))
    wr = din("wr", (NCH, 128, 8))
    bq = din("bq", (128, 32))
    bk = din("bk", (128, 1))
    bv = din("bv", (128, 1))
    bo = din("bo", (128, 32))
    bfb = din("bfb", (128, 32))
    br = din("br", (1, 8))
    identf = din("identf", (128, 128))
    onescol_f = din("onescol_f", (128, 1))
    onesrow_f = din("onesrow_f", (1, 128))
    onesrow_b = din("onesrow_b", (1, 128), BF16)
    ones1_b = din("ones1_b", (1, 1), BF16)
    sel_b = din("sel_b", (8, 128), BF16)
    masks = din("masks", (4, 8, 128, 512), BF16)

    out_t = nc.dram_tensor("out", [NCH, OWN_W, 128, 512], F32,
                           kind="ExternalOutput").ap()
    dbg = {}
    if upto != "d":
        if upto == "a0":
            dbg["d_own"] = nc.dram_tensor("d_own", [128, 4096], F32, kind="ExternalOutput").ap()
            dbg["d_glob"] = nc.dram_tensor("d_glob", [128, 8192], F32, kind="ExternalOutput").ap()
        if upto == "a1":
            dbg["d_kT"] = nc.dram_tensor("d_kT", [128, 4096], BF16, kind="ExternalOutput").ap()
            dbg["d_v"] = nc.dram_tensor("d_v", [128, 4096], BF16, kind="ExternalOutput").ap()
            dbg["d_qs"] = nc.dram_tensor("d_qs", [128, 32], F32, kind="ExternalOutput").ap()
        if upto == "rt":
            dbg["d_rsc"] = nc.dram_tensor("d_rsc", [128, 1], F32, kind="ExternalOutput").ap()
            dbg["d_rsm"] = nc.dram_tensor("d_rsm", [1, 64], F32, kind="ExternalOutput").ap()
        if upto == "b":
            dbg["d_qts"] = nc.dram_tensor("d_qts", [H, 4, 128, 512], BF16, kind="ExternalOutput").ap()
            dbg["d_xns"] = nc.dram_tensor("d_xns", [NCH, OWN_W, 128, 512], BF16, kind="ExternalOutput").ap()
        if upto == "c":
            dbg["d_ats"] = nc.dram_tensor("d_ats", [H, 4, 128, 512], BF16, kind="ExternalOutput").ap()
            dbg["d_pacc"] = nc.dram_tensor("d_pacc", [128, 2048], F32, kind="ExternalOutput").ap()
            dbg["d_linv"] = nc.dram_tensor("d_linv", [128, 512], F32, kind="ExternalOutput").ap()
            dbg["d_pg"] = nc.dram_tensor("d_pg", [128, 2048], BF16, kind="ExternalOutput").ap()
    xns = nc.dram_tensor("xns", [NCH, OWN_W, 128, 512], BF16, kind="Internal").ap()
    qts = nc.dram_tensor("qts", [H, 4, 128, 512], BF16, kind="Internal").ap()
    ats = nc.dram_tensor("ats", [H, 4, 128, 512], BF16, kind="Internal").ap()
    lts = nc.dram_tensor("lts", [H, 4, 512], F32, kind="Internal").ap()

    sch = Sched()
    E = sch.emit

    import contextlib
    with contextlib.ExitStack() as _st:
        AR1_t = _st.enter_context(nc.sbuf_tensor("AR1", [128, 65536], BF16))
        AR2_t = _st.enter_context(nc.sbuf_tensor("AR2", [128, 16384], BF16))
        AR3_t = _st.enter_context(nc.sbuf_tensor("AR3", [128, 5120], F32))
        kT_sb = _st.enter_context(nc.sbuf_tensor("kT_sb", [128, 4096], BF16))
        v_sb = _st.enter_context(nc.sbuf_tensor("v_sb", [128, 4096], BF16))
        mask_sb = _st.enter_context(nc.sbuf_tensor("mask_sb", [128, 4096], BF16))
        stats_sb = _st.enter_context(nc.sbuf_tensor("stats_sb", [128, 48], F32))
        aggr_sb = _st.enter_context(nc.sbuf_tensor("aggr_sb", [128, 2], F32))
        lncol = _st.enter_context(nc.sbuf_tensor("lncol", [128, 1], F32))
        rcol = _st.enter_context(nc.sbuf_tensor("rcol", [128, 1], F32))
        rmcol = _st.enter_context(nc.sbuf_tensor("rmcol", [128, 1], F32))
        qs_sb = _st.enter_context(nc.sbuf_tensor("qs_sb", [128, 32], F32))
        qtmp = _st.enter_context(nc.sbuf_tensor("qtmp", [128, 1], F32))
        brsb = _st.enter_context(nc.sbuf_tensor("brsb", [1, 8], F32))
        rsm = _st.enter_context(nc.sbuf_tensor("rsm", [1, 64], F32))
        routing_b = _st.enter_context(nc.sbuf_tensor("routing_b", [1, 8], BF16))
        rtcol_b = _st.enter_context(nc.sbuf_tensor("rtcol_b", [8, 1], BF16))
        rscale = _st.enter_context(nc.sbuf_tensor("rscale", [128, 1], F32))
        bq_sb = _st.enter_context(nc.sbuf_tensor("bq_sb", [128, 32], F32))
        bk_sb = _st.enter_context(nc.sbuf_tensor("bk_sb", [128, 1], F32))
        bv_sb = _st.enter_context(nc.sbuf_tensor("bv_sb", [128, 1], F32))
        bo_sb = _st.enter_context(nc.sbuf_tensor("bo_sb", [128, 32], F32))
        bf_sb = _st.enter_context(nc.sbuf_tensor("bf_sb", [128, 32], F32))
        bsum = _st.enter_context(nc.sbuf_tensor("bsum", [128, 32], F32))
        identf_sb = _st.enter_context(nc.sbuf_tensor("identf_sb", [128, 128], F32))
        onescol_sb = _st.enter_context(nc.sbuf_tensor("onescol_sb", [128, 1], F32))
        onesrow_sb = _st.enter_context(nc.sbuf_tensor("onesrow_sb", [1, 128], F32))
        onesrowb_sb = _st.enter_context(nc.sbuf_tensor("onesrowb_sb", [1, 128], BF16))
        ones1_sb = _st.enter_context(nc.sbuf_tensor("ones1_sb", [1, 1], BF16))
        sel_sb = _st.enter_context(nc.sbuf_tensor("sel_sb", [8, 128], BF16))
        epscol = _st.enter_context(nc.sbuf_tensor("epscol", [128, 1], F32))
        scr_act = _st.enter_context(nc.sbuf_tensor("scr_act", [1, 1], F32))
        scr_dve = _st.enter_context(nc.sbuf_tensor("scr_dve", [1, 1], F32))
        scr_pool = _st.enter_context(nc.sbuf_tensor("scr_pool", [1, 1], F32))
        ps_A = _st.enter_context(nc.psum_tensor([128, 2048], F32))
        ps_B = _st.enter_context(nc.psum_tensor([128, 2048], F32))
        qsems = {}
        for _qp in ("qsp", "qact", "qpool"):
            for _r in range(Sched.NQROT):
                qsems[f"{_qp}{_r}"] = _st.enter_context(
                    nc.semaphore(f"{_qp}{_r}"))
        sem_spe = _st.enter_context(nc.semaphore("spe"))
        sem_sact = _st.enter_context(nc.semaphore("sact"))
        sem_sdve = _st.enter_context(nc.semaphore("sdve"))
        sem_spool = _st.enter_context(nc.semaphore("spool"))
        block = _st.enter_context(nc.Block())
        sems = {"spe": sem_spe, "sact": sem_sact, "sdve": sem_sdve,
                "spool": sem_spool}
        sems.update(qsems)

        _phases_done = False
        AR1 = AR1_t[:]
        AR2 = AR2_t[:]
        AR3 = AR3_t[:]
        AR1f = AR1.bitcast(F32)
        AR2f = AR2.bitcast(F32)
        rowb = AR3_t[0:1, 4096:5120]
        lnl = AR2f[0:1, 5120:6144]
        wr_sb = AR2f[:, 6144:6400]

        def _emit_phases():
            tok_eps = E("dve", lambda e: e.memset(epscol[:], EPS), inc="sdve")
            t_idf = E("sp", lambda e: e.dma_start(identf_sb[:], identf), inc="qsp", inc_by=16)
            t_oc = E("sp", lambda e: e.dma_start(onescol_sb[:], onescol_f), inc="qsp", inc_by=16)
            t_or = E("sp", lambda e: e.dma_start(onesrow_sb[:], onesrow_f), inc="qsp", inc_by=16)
            t_orb = E("sp", lambda e: e.dma_start(onesrowb_sb[:], onesrow_b), inc="qsp", inc_by=16)
            t_o1 = E("sp", lambda e: e.dma_start(ones1_sb[:], ones1_b), inc="qsp", inc_by=16)
            t_sel = E("sp", lambda e: e.dma_start(sel_sb[:], sel_b), inc="qsp", inc_by=16)
            t_bk = E("sp", lambda e: e.dma_start(bk_sb[:], bk), inc="qsp", inc_by=16)
            t_bv = E("sp", lambda e: e.dma_start(bv_sb[:], bv), inc="qsp", inc_by=16)
            t_bq = E("sp", lambda e: e.dma_start(bq_sb[:], bq), inc="qsp", inc_by=16)
            t_wr = E("sp", lambda e: e.dma_start(
                wr_sb[:].rearrange("p (c e) -> p c e", c=NCH),
                wr.transpose([1, 0, 2])), inc="qsp", inc_by=16)
            t_br = E("sp", lambda e: e.dma_start(brsb[:], br), inc="qsp", inc_by=16)

            def a0_pass(src_ap, ntiles, rbc, rmbc, war):
                src_v = src_ap.rearrange("(n p) d -> n p d", p=128)
                stage = [AR1f[:, 0:4096], AR1f[:, 4096:8192]]
                t_stage_rd = [war.get("stage0"), war.get("stage1")]
                t_row_ev1 = war.get("row1")
                t_row_ev2 = war.get("row2")
                t_bc_ev_r = war.get("bcr")
                t_bc_ev_m = war.get("bcm")
                t_rc_read = war.get("rc")
                t_rm_read = war.get("rm")
                rc_mm = [None] * 4
                rm_mm = [None] * 4
                for tt in range(ntiles):
                    P = tt % 2
                    g, j = tt // 4, tt % 4
                    st = stage[P]
                    t_x = E("sp", lambda e, st=st, tt=tt: e.dma_start(st, src_v[tt]),
                            deps=[t_stage_rd[P]], inc="qsp", inc_by=16)
                    tok_bn = None
                    for i in range(8):
                        tok_bn = E("dve", lambda e, st=st, i=i: e.bn_stats(
                            stats_sb[:, 6 * i:6 * i + 6], st[:, 512 * i:512 * i + 512]),
                            deps=[t_x], inc="sdve" if i == 7 else None)
                    t_stage_rd[P] = tok_bn
                    tok_ag = E("dve", lambda e: e.bn_aggr(aggr_sb[:], stats_sb[:]),
                               deps=[tok_bn], inc="sdve")
                    tok_ln = E("act", lambda e: e.activation(
                        lncol[:], aggr_sb[:, 1:2], AF.Ln, bias=epscol[:]),
                        deps=[tok_ag, tok_eps], inc="sact")
                    tok_r = E("act", lambda e: e.activation(
                        rcol[:], lncol[:], AF.Exp, scale=-0.5),
                        deps=[tok_ln, t_rc_read], inc="sact")
                    tok_rm = E("dve", lambda e: e.tensor_tensor(
                        rmcol[:], aggr_sb[:, 0:1], rcol[:], ALU.mult),
                        deps=[tok_r, t_rm_read], inc="sdve")
                    rc_mm[j] = E("pe", lambda e, j=j: e.matmul(
                        ps_A[0:1, 128 * j:128 * j + 128], rcol[:], identf_sb[:],
                        start=True, stop=True),
                        deps=[tok_r, t_idf, t_row_ev1], inc="spe")
                    rm_mm[j] = E("pe", lambda e, j=j: e.matmul(
                        ps_A[0:1, 512 + 128 * j:512 + 128 * j + 128], rmcol[:],
                        identf_sb[:], start=True, stop=True),
                        deps=[tok_rm, t_row_ev2], inc="spe")
                    t_rc_read = rc_mm[j]
                    t_rm_read = rm_mm[j]
                    if j == 3:
                        t_row_ev1 = E("act", lambda e: e.activation(
                            rowb[:, 0:512], ps_A[0:1, 0:512], AF.Copy),
                            deps=rc_mm, inc="sact")
                        t_row_ev2 = E("act", lambda e: e.activation(
                            rowb[:, 512:1024], ps_A[0:1, 512:1024], AF.Copy),
                            deps=rm_mm, inc="sact")
                        bc1 = E("pe", lambda e: e.matmul(
                            ps_A[:, 1024:1536], onesrow_sb[:], rowb[:, 0:512],
                            start=True, stop=True),
                            deps=[t_row_ev1, t_or, t_bc_ev_r], inc="spe")
                        bc2 = E("pe", lambda e: e.matmul(
                            ps_A[:, 1536:2048], onesrow_sb[:], rowb[:, 512:1024],
                            start=True, stop=True),
                            deps=[t_row_ev2, t_bc_ev_m], inc="spe")
                        t_bc_ev_r = E("act", lambda e, g=g: e.activation(
                            rbc[:, 512 * g:512 * g + 512], ps_A[:, 1024:1536],
                            AF.Copy), deps=[bc1], inc="sact")
                        t_bc_ev_m = E("act", lambda e, g=g: e.activation(
                            rmbc[:, 512 * g:512 * g + 512], ps_A[:, 1536:2048],
                            AF.Copy), deps=[bc2], inc="sact")
                return {"stage0": t_stage_rd[0], "stage1": t_stage_rd[1],
                        "row1": t_row_ev1, "row2": t_row_ev2,
                        "bcr": t_bc_ev_r, "bcm": t_bc_ev_m,
                        "rc": t_rc_read, "rm": t_rm_read}

            r_bc_own = AR3[:, 0:2048]
            rm_bc_own = AR3[:, 2048:4096]
            w0 = a0_pass(xtok_own, 16, r_bc_own, rm_bc_own, {})
            tok_own_r, tok_own_m = w0["bcr"], w0["bcm"]
            r_bc_g = AR1f[:, 20480:24576]
            rm_bc_g = AR1f[:, 24576:28672]
            w1 = a0_pass(x_glob, 32, r_bc_g, rm_bc_g, w0)
            tok_g_r, tok_g_m = w1["bcr"], w1["bcm"]
            if upto == "a0":
                t1 = E("sp", lambda e: e.dma_start(dbg["d_own"], AR3[:, 0:4096]),
                       deps=[tok_own_r, tok_own_m], inc="qsp", inc_by=16)
                t2 = E("sp", lambda e: e.dma_start(dbg["d_glob"], AR1f[:, 20480:28672]),
                       deps=[tok_g_r, tok_g_m], inc="qsp", inc_by=16)
                E("sp", lambda e: None, deps=[t1, t2])
                return

            t_wk = E("pool", lambda e: e.dma_start(
                AR2[:, 0:4096].rearrange("p (c f) -> p c f", c=NCH),
                wk.transpose([1, 0, 2])), inc="qpool", inc_by=16)
            t_wv = E("pool", lambda e: e.dma_start(
                AR2[:, 4096:8192].rearrange("p (c f) -> p c f", c=NCH),
                wv.transpose([1, 0, 2])), inc="qpool", inc_by=16)

            xn_w = [AR1[:, 0:16384], AR1[:, 16384:32768]]
            xstage = [AR1f[:, 16384 + 512 * k:16384 + 512 * k + 512] for k in range(3)]
            tmpf = [AR1f[:, 28672:29184], AR1f[:, 29184:29696]]
            vtmp = AR1f[:, 29696:30208]

            tok_xn = {}
            tok_kv_stop = {}
            tok_qsum_last = None
            tok_kev = None
            tok_vev = None
            tok_vtr_ev = [None, None]
            tok_xstage_rd = [w1["stage0"], w1["stage1"], None]
            tok_qs_wr = None
            tok_v_done = None

            for w in range(NW):
                P = w % 2
                dep_war = [tok_kv_stop.get(w - 2)]
                for d in range(NCH):
                    k3 = (w * NCH + d) % 3
                    t_x = E("sp", lambda e, w=w, d=d, k3=k3: e.dma_start(
                        xstage[k3], xT_glob[w, d]),
                        deps=[tok_xstage_rd[k3]], inc="qsp", inc_by=16)
                    tp = d % 2
                    tok_t = E("dve", lambda e, w=w, tp=tp, k3=k3: e.tensor_tensor(
                        tmpf[tp], xstage[k3], r_bc_g[:, 512 * w:512 * w + 512],
                        ALU.mult),
                        deps=[t_x, tok_g_r] + dep_war, inc="sdve")
                    tok_xn[(w, d)] = E("dve",
                        lambda e, w=w, d=d, P=P, tp=tp: e.tensor_tensor(
                            xn_w[P][:, 512 * d:512 * d + 512], tmpf[tp],
                            rm_bc_g[:, 512 * w:512 * w + 512], ALU.subtract),
                        deps=[tok_t, tok_g_m], inc="sdve")
                    tok_xstage_rd[k3] = tok_t
                tok_k_mm = None
                for d in range(NCH):
                    tok_k_mm = E("pe", lambda e, d=d, P=P: e.matmul(
                        ps_B[:, 0:512], AR2[:, 128 * d:128 * d + 128],
                        xn_w[P][:, 512 * d:512 * d + 512],
                        start=(d == 0), stop=(d == NCH - 1)),
                        deps=[tok_xn[(w, d)], t_wk, tok_kev],
                        inc="spe" if d == NCH - 1 else None)
                tok_kev = E("act", lambda e, w=w: e.activation(
                    kT_sb[:, 512 * w:512 * w + 512], ps_B[:, 0:512], AF.Identity,
                    bias=bk_sb[:]), deps=[tok_k_mm, t_bk], inc="sact")
                tok_v_mm = None
                for d in range(NCH):
                    tok_v_mm = E("pe", lambda e, d=d, P=P: e.matmul(
                        ps_B[:, 512:1024],
                        AR2[:, 4096 + 128 * d:4096 + 128 * d + 128],
                        xn_w[P][:, 512 * d:512 * d + 512],
                        start=(d == 0), stop=(d == NCH - 1)),
                        deps=[tok_xn[(w, d)], t_wv, tok_vev],
                        inc="spe" if d == NCH - 1 else None)
                tok_vev = E("act", lambda e: e.activation(
                    vtmp, ps_B[:, 512:1024], AF.Identity, bias=bv_sb[:]),
                    deps=[tok_v_mm, t_bv], inc="sact")
                for j in range(4):
                    sl = 1024 + 512 * (j % 2)
                    tok_tr = E("pe", lambda e, j=j, sl=sl: e.transpose(
                        ps_B[:, sl:sl + 128],
                        vtmp[:, 128 * j:128 * j + 128], identf_sb[:]),
                        deps=[tok_vev, tok_vtr_ev[j % 2]], inc="spe")
                    tok_vtr_ev[j % 2] = E("act", lambda e, w=w, j=j, sl=sl: e.activation(
                        v_sb[:, (4 * w + j) * 128:(4 * w + j) * 128 + 128],
                        ps_B[:, sl:sl + 128], AF.Copy),
                        deps=[tok_tr], inc="sact")
                tok_v_done = tok_vtr_ev[1]
                tok_kv_stop[w] = tok_v_mm
                if w >= 1:
                    lo = max(0, Q_LO - 512 * w)
                    hi = min(512, Q_HI - 512 * w)
                    for d in range(NCH):
                        tq1 = E("dve", lambda e, d=d, P=P, lo=lo, hi=hi: e.reduce_sum(
                            qtmp[:], xn_w[P][:, 512 * d + lo:512 * d + hi], axis=AX.X),
                            deps=[tok_xn[(w, d)], tok_qs_wr], inc="sdve")
                        if w == 1:
                            tok_qs_wr = E("dve", lambda e, d=d: e.tensor_copy(
                                qs_sb[:, d:d + 1], qtmp[:]), deps=[tq1], inc="sdve")
                        else:
                            tok_qs_wr = E("dve", lambda e, d=d: e.tensor_tensor(
                                qs_sb[:, d:d + 1], qs_sb[:, d:d + 1], qtmp[:],
                                ALU.add), deps=[tq1], inc="sdve")
            tok_qsum_last = tok_qs_wr
            tok_kT_done = tok_kev
            if upto == "a1":
                t1 = E("sp", lambda e: e.dma_start(dbg["d_kT"], kT_sb[:]),
                       deps=[tok_kT_done], inc="qsp", inc_by=16)
                t2 = E("sp", lambda e: e.dma_start(dbg["d_v"], v_sb[:]),
                       deps=[tok_v_done], inc="qsp", inc_by=16)
                t3 = E("sp", lambda e: e.dma_start(dbg["d_qs"], qs_sb[:]),
                       deps=[tok_qsum_last], inc="qsp", inc_by=16)
                E("sp", lambda e: None, deps=[t1, t2, t3])
                return

            tok_lps = None
            for d in range(NCH):
                tok_lps = E("pe", lambda e, d=d: e.matmul(
                    ps_A[0:1, 0:8], qs_sb[:, d:d + 1], wr_sb[:, 8 * d:8 * d + 8],
                    start=(d == 0), stop=(d == NCH - 1)),
                    deps=[tok_qsum_last, t_wr, w1["row1"]],
                    inc="spe" if d == NCH - 1 else None)
            tok_lg = E("act", lambda e: e.activation(
                rsm[0:1, 0:8], ps_A[0:1, 0:8], AF.Copy, scale=1.0 / NTOK_R),
                deps=[tok_lps], inc="sact")
            tok_lg2 = E("dve", lambda e: e.tensor_tensor(
                rsm[0:1, 8:16], rsm[0:1, 0:8], brsb[:], ALU.add),
                deps=[tok_lg, t_br], inc="sdve")
            tok_le = E("act", lambda e: e.activation(
                rsm[0:1, 16:24], rsm[0:1, 8:16], AF.Exp), deps=[tok_lg2], inc="sact")
            tok_ls = E("dve", lambda e: e.reduce_sum(
                rsm[0:1, 24:25], rsm[0:1, 16:24], axis=AX.X), deps=[tok_le], inc="sdve")
            tok_lr = E("dve", lambda e: e.reciprocal(
                rsm[0:1, 25:26], rsm[0:1, 24:25]), deps=[tok_ls], inc="sdve")
            tok_rt = E("dve", lambda e: e.tensor_scalar(
                rsm[0:1, 32:40], rsm[0:1, 16:24], rsm[0:1, 25:26], None, ALU.mult),
                deps=[tok_lr], inc="sdve")
            tok_rtb = E("dve", lambda e: e.tensor_copy(
                routing_b[:], rsm[0:1, 32:40]), deps=[tok_rt], inc="sdve")
            tok_rt1 = E("pe", lambda e: e.matmul(
                ps_A[0:8, 512:513], routing_b[:], ones1_sb[:], start=True, stop=True),
                deps=[tok_rtb, t_o1, w1["row2"]], inc="spe")
            tok_rtc = E("act", lambda e: e.activation(
                rtcol_b[:], ps_A[0:8, 512:513], AF.Copy), deps=[tok_rt1], inc="sact")
            tok_rsc = E("pe", lambda e: e.matmul(
                ps_A[:, 1024:1025], sel_sb[:], rtcol_b[:], start=True, stop=True),
                deps=[tok_rtc, t_sel, w1["bcr"]], inc="spe")
            tok_rscale = E("act", lambda e: e.activation(
                rscale[:], ps_A[:, 1024:1025], AF.Copy), deps=[tok_rsc], inc="sact")
            if upto == "rt":
                t1 = E("sp", lambda e: e.dma_start(dbg["d_rsc"], rscale[:]),
                       deps=[tok_rscale], inc="qsp", inc_by=16)
                t2 = E("sp", lambda e: e.dma_start(dbg["d_rsm"], rsm[:]),
                       deps=[tok_rtb], inc="qsp", inc_by=16)
                E("sp", lambda e: None, deps=[t1, t2])
                return

            bstage = [AR2f[:, 4096 + 512 * k:4096 + 512 * k + 512] for k in range(2)]
            btmp = [AR2f[:, 5120 + 512 * k:5120 + 512 * k + 512] for k in range(2)]
            dep_b_war = [tok_kv_stop[NW - 1], tok_kv_stop[NW - 2], tok_v_done,
                         tok_qsum_last]
            tok_bstage_rd = [None, None]
            tok_xn_own = {}
            tok_xns = {}
            for d in range(NCH):
                for wi in range(OWN_W):
                    k2 = (d * OWN_W + wi) % 2
                    t_x = E("sp", lambda e, d=d, wi=wi, k2=k2: e.dma_start(
                        bstage[k2], xT_own[wi, d]),
                        deps=[tok_bstage_rd[k2]], inc="qsp", inc_by=16)
                    tok_t = E("dve", lambda e, wi=wi, k2=k2: e.tensor_tensor(
                        btmp[k2], bstage[k2], r_bc_own[:, 512 * wi:512 * wi + 512],
                        ALU.mult),
                        deps=[t_x, tok_own_r] + dep_b_war, inc="sdve")
                    tok_xn_own[(d, wi)] = E("dve",
                        lambda e, d=d, wi=wi, k2=k2: e.tensor_tensor(
                            AR1[:, 2048 * d + 512 * wi:2048 * d + 512 * wi + 512],
                            btmp[k2], rm_bc_own[:, 512 * wi:512 * wi + 512],
                            ALU.subtract),
                        deps=[tok_t, tok_own_m], inc="sdve")
                    tok_bstage_rd[k2] = tok_t
                tok_xns[d] = E("act", lambda e, d=d: e.dma_start(
                    xns[d].transpose([1, 0, 2]),
                    AR1[:, 2048 * d:2048 * d + 2048].rearrange(
                        "p (s f) -> p s f", s=4)),
                    deps=[tok_xn_own[(d, wi2)] for wi2 in range(OWN_W)],
                    inc="qact", inc_by=16)
            dep_xn_own = [tok_xn_own[(d, OWN_W - 1)] for d in range(NCH)]
            tok_xns_done = tok_xns[NCH - 1]

            wq_cb = [AR2[:, 0:4096], AR2[:, 4096:8192]]
            qevac = [AR2[:, 12288:14336], AR2[:, 14336:16384]]
            tok_q_stop = {}
            tok_q_ev = {}
            tok_q_spill = {}
            for hd in range(H):
                P = hd % 2
                qps = ps_A if P == 0 else ps_B
                t_w = E("pool", lambda e, hd=hd, P=P: e.dma_start(
                    wq_cb[P].rearrange("p (c f) -> p c f", c=NCH),
                    wq[hd].transpose([1, 0, 2])),
                    deps=[tok_q_stop.get(hd - 2),
                          (tok_kv_stop[NW - 1] if hd <= 1 else None)],
                    inc="qpool", inc_by=16)
                extra = []
                if hd == 0:
                    extra = [tok_lg, tok_rtc, tok_rscale, w1["row1"], w1["row2"],
                             w1["bcr"], w1["bcm"]]
                if hd == 1:
                    extra = [tok_kev, tok_vev, tok_v_done]
                tok_mm = None
                for d in range(NCH):
                    for tb in range(4):
                        tok_mm = E("pe",
                            lambda e, P=P, d=d, tb=tb, qps=qps: e.matmul(
                                qps[:, 512 * tb:512 * tb + 512],
                                wq_cb[P][:, 128 * d:128 * d + 128],
                                AR1[:, 2048 * d + 512 * tb:2048 * d + 512 * tb + 512],
                                start=(d == 0), stop=(d == NCH - 1)),
                            deps=[t_w, tok_q_ev.get(hd - 2)] + extra +
                                 (dep_xn_own if hd == 0 else []),
                            inc="spe" if (d == NCH - 1 and tb == 3) else None)
                tok_q_stop[hd] = tok_mm
                tok_q_ev[hd] = E("act", lambda e, hd=hd, P=P, qps=qps: e.activation(
                    qevac[P], qps[:], AF.Identity, bias=bq_sb[:, hd:hd + 1]),
                    deps=[tok_mm, t_bq, tok_q_spill.get(hd - 2)], inc="sact")
                tok_q_spill[hd] = E("act", lambda e, hd=hd, P=P: e.dma_start(
                    qts[hd].transpose([1, 0, 2]),
                    qevac[P].rearrange("p (s f) -> p s f", s=4)),
                    deps=[tok_q_ev[hd]], inc="qact", inc_by=16)

            if upto == "b":
                t1 = E("sp", lambda e: e.dma_start(dbg["d_qts"], qts),
                       deps=[tok_q_spill[H - 1]], inc="qsp", inc_by=16)
                t2 = E("sp", lambda e: e.dma_start(dbg["d_xns"], xns),
                       deps=[tok_xns_done], inc="qsp", inc_by=16)
                E("sp", lambda e: None, deps=[t1, t2] + sch.qbar("qact"))
                return

            pacc_d = AR3[:, 0:2048]
            pacc_p = AR3[:, 2048:4096]
            qtb = [AR2[:, 0:2048], AR2[:, 2048:4096]]
            pg = [AR2[:, 4096 + 2048 * i:4096 + 2048 * i + 2048] for i in range(4)]
            aev = [AR2[:, 12288 + 512 * j:12288 + 512 * j + 512] for j in range(4)]
            lrow = AR2f[0:1, 7168:8192]

            dep_c_ar3 = [tok_xn_own[(NCH - 1, OWN_W - 1)]]
            dep_c_ar2 = [tok_q_stop[H - 1], tok_q_stop[H - 2],
                         tok_q_spill[H - 1], tok_q_spill[H - 2]]

            tok_mask = None
            tok_mask_use = None
            tok_qtb = {}
            tok_qk_last = {}
            tok_sT_free = tok_q_ev[H - 2]
            tok_pg_free = {}
            for _pp in range(4):
                tok_pg_free[_pp] = list(dep_c_ar2)
            tok_aev_dma = {}
            tok_lts_dma = {}
            tok_av_last = {}
            tok_psum_prev = None
            tok_lps_c = None
            tok_lev = None
            tok_aev_ev_prev = tok_q_ev[H - 1]

            for s in range(4):
                L = LIMS[s]
                tok_mask = E("sp", lambda e, s=s: e.dma_start(
                    mask_sb[:].rearrange("p (j f) -> p j f", j=8),
                    masks[s].transpose([1, 0, 2])),
                    deps=[tok_mask_use] + (dep_c_ar2 if s == 0 else []),
                    inc="qsp", inc_by=16)
                for hb in range(8):
                    h0 = hb * 4
                    Pq = hb % 2
                    gi = s * 8 + hb
                    tok_qtb[gi] = E("sp", lambda e, h0=h0, s=s, Pq=Pq: e.dma_start(
                        qtb[Pq].rearrange("p (h f) -> p h f", h=4),
                        qts[h0:h0 + 4, s].transpose([1, 0, 2])),
                        deps=[tok_q_spill[h0 + 3], tok_qk_last.get(gi - 2)] +
                             (dep_c_ar2 if gi < 2 else []),
                        inc="qsp", inc_by=16)
                    tok_av = {}
                    tok_pacc_d = None
                    tok_pacc_p = None
                    first_d = True
                    first_p = True
                    for kc in range(L):
                        Pp = kc % 4
                        qk_tok = {}
                        for h in range(4):
                            qk_tok[h] = E("pe",
                                lambda e, kc=kc, h=h, Pq=Pq: e.matmul(
                                    ps_A[:, 512 * h:512 * h + 512],
                                    kT_sb[:, 128 * kc:128 * kc + 128],
                                    qtb[Pq][:, 512 * h:512 * h + 512],
                                    start=True, stop=True),
                                deps=[tok_qtb[gi], tok_kT_done, tok_sT_free] +
                                     ([tok_lev] if (kc == 0 and h == 0) else []),
                                inc="spe" if h == 3 else None)
                        texp = E("act", lambda e, Pp=Pp: e.activation(
                            pg[Pp][:], ps_A[:], AF.Exp, scale=SM_SCALE),
                            deps=[qk_tok[3]] + tok_pg_free.get(Pp, []),
                            inc="sact")
                        tok_sT_free = texp
                        dep_p = [texp]
                        if kc >= L - 8:
                            j = kc - (L - 8)
                            tmm = None
                            for h in range(4):
                                tmm = E("dve",
                                    lambda e, h=h, j=j, Pp=Pp: e.tensor_tensor(
                                        pg[Pp][:, 512 * h:512 * h + 512],
                                        pg[Pp][:, 512 * h:512 * h + 512],
                                        mask_sb[:, 512 * j:512 * j + 512],
                                        ALU.mult),
                                    deps=dep_p + [tok_mask],
                                    inc="sdve" if h == 3 else None)
                            dep_p = [tmm]
                            tok_mask_use = tmm
                        if kc % 3 == 2:
                            if first_p:
                                tok_pacc_p = E("pool", lambda e, Pp=Pp: e.tensor_copy(
                                    pacc_p, pg[Pp][:]),
                                    deps=dep_p + dep_c_ar3 + [tok_psum_prev],
                                    inc="spool")
                                first_p = False
                            else:
                                tok_pacc_p = E("pool", lambda e, Pp=Pp: e.tensor_tensor(
                                    pacc_p, pacc_p, pg[Pp][:], ALU.add),
                                    deps=dep_p, inc="spool")
                            tok_pacc = tok_pacc_p
                        else:
                            if first_d:
                                tok_pacc_d = E("dve", lambda e, Pp=Pp: e.tensor_copy(
                                    pacc_d, pg[Pp][:]),
                                    deps=dep_p + dep_c_ar3 + [tok_lps_c],
                                    inc="sdve")
                                first_d = False
                            else:
                                tok_pacc_d = E("dve", lambda e, Pp=Pp: e.tensor_tensor(
                                    pacc_d, pacc_d, pg[Pp][:], ALU.add),
                                    deps=dep_p, inc="sdve")
                            tok_pacc = tok_pacc_d
                        for h in range(4):
                            tok_av[h] = E("pe",
                                lambda e, h=h, kc=kc, Pp=Pp, L=L: e.matmul(
                                    ps_B[:, 512 * h:512 * h + 512],
                                    v_sb[:, 128 * kc:128 * kc + 128],
                                    pg[Pp][:, 512 * h:512 * h + 512],
                                    start=(kc == 0), stop=(kc == L - 1)),
                                deps=dep_p + [tok_v_done] +
                                     ([tok_aev_ev_prev,
                                       (tok_q_ev[H - 1] if gi == 0 else None)]
                                      if kc == 0 else []),
                                inc="spe")
                        tok_pg_free[Pp] = [tok_pacc, tok_av[3]]
                    tok_qk_last[gi] = qk_tok[3]
                    tok_psum = E("dve", lambda e: e.tensor_tensor(
                        pacc_d, pacc_d, pacc_p, ALU.add),
                        deps=[tok_pacc_d, tok_pacc_p], inc="sdve")
                    tok_psum_prev = tok_psum
                    for h in range(4):
                        tok_lps_c = E("pe", lambda e, h=h: e.matmul(
                            ps_A[0:1, 512 * h:512 * h + 512], onescol_sb[:],
                            pacc_d[:, 512 * h:512 * h + 512],
                            start=True, stop=True),
                            deps=[tok_psum, t_oc, tok_sT_free,
                                  (tok_lev if h == 0 else None)],
                            inc="spe" if h == 3 else None)
                    for h in range(4):
                        E("dve", lambda e, h=h: e.tensor_copy(
                            aev[h], ps_B[:, 512 * h:512 * h + 512]),
                            deps=[tok_av[h], tok_aev_dma.get((gi - 1, h))],
                            inc=None)
                    for h in range(4):
                        tok_lev = E("dve", lambda e, h=h: e.tensor_copy(
                            lrow[:, 512 * (h % 2):512 * (h % 2) + 512],
                            ps_A[0:1, 512 * h:512 * h + 512]),
                            deps=[tok_lps_c,
                                  tok_lts_dma.get((gi, h - 2)),
                                  tok_lts_dma.get((gi - 1, 2 + h))],
                            inc="sdve" if (h % 2 == 1) else None)
                        if h % 2 == 1:
                            tok_lts_dma[(gi, h - 1)] = E("sp",
                                lambda e, h0=h0, s=s, h=h: e.dma_start(
                                    lts[h0 + h - 1:h0 + h + 1, s].unsqueeze(0),
                                    lrow[:].rearrange("p (s f) -> p s f", s=2)),
                                deps=[tok_lev], inc="qsp", inc_by=16)
                    tok_aev_ev_prev = tok_lev
                    for h in range(4):
                        tok_aev_dma[(gi, h)] = E("sp",
                            lambda e, h=h, h0=h0, s=s: e.dma_start(
                                ats[h0 + h, s], aev[h]),
                            deps=[tok_lev], inc="qsp", inc_by=16)

            t_bo = E("sp", lambda e: e.dma_start(bo_sb[:], bo), inc="qsp", inc_by=16)
            t_bf = E("sp", lambda e: e.dma_start(bf_sb[:], bfb), inc="qsp", inc_by=16)
            tok_bsum = E("dve", lambda e: e.tensor_tensor(
                bsum[:], bo_sb[:], bf_sb[:], ALU.add), deps=[t_bo, t_bf], inc="sdve")

            bar_pe = sch.bar("spe")
            bar_act = sch.bar("sact")
            bar_dve = sch.bar("sdve")
            bar_pool = sch.bar("spool")
            bar_qact_l = sch.qbar("qact")
            bar_qsp_l = sch.qbar("qsp")

            wcb = [AR2[:, 0:4096], AR2[:, 4096:8192], AR2[:, 8192:12288]]
            wB_t = [AR2[:, 12288:12416], AR2[:, 12416:12544]]
            tp_h = AR2[:, 12544:13568]
            linv_bc = AR2[:, 13568:14592]
            lrowD = AR3[0:1, 4096:5120]
            linv_row = AR2[0:1, 14592:15616]
            xst = [AR3[:, 0:1024], AR3[:, 1024:2048]]
            acts = [AR3[:, 2048:3072], AR3[:, 3072:4096]]
            accs = [ps_A[:, 0:1024], ps_A[:, 1024:2048], ps_B[:, 0:1024]]
            t_ps = ps_B[:, 1024:2048]

            tok_ath = {}
            tok_lath = {}
            tok_xrh = {}
            tok_wcb = {}
            tok_wBt = {}
            tok_acc_stop = {}
            tok_acts = {}
            tok_dadd = {}
            tok_out = {}
            tok_tp_ev = None
            tok_tp_stop = None

            dep_d_ps = [tok_lev, tok_lps_c]

            for hh in range(2):
                dep_half = [tok_acc_stop.get((hh - 1, NCH - 1))] if hh > 0 else \
                    [tok_q_stop[H - 1], tok_xns_done]
                t_wA = E("pool", lambda e: e.dma_start(
                    AR2[:, 0:4096].rearrange("p (c f) -> p c f", c=NCH),
                    wA.transpose([1, 0, 2])),
                    deps=dep_half + ([bar_pe, bar_act, bar_dve] + bar_qact_l +
                                     bar_qsp_l if hh == 0 else []),
                    inc="qpool", inc_by=16)
                for h in range(H):
                    tok_ath[(hh, h)] = E("sp", lambda e, hh=hh, h=h: e.dma_start(
                        AR1[:, 1024 * h:1024 * h + 1024].rearrange(
                            "p (s f) -> p s f", s=2),
                        ats[h, 2 * hh:2 * hh + 2].transpose([1, 0, 2])),
                        deps=[tok_aev_dma[(2 * hh * 8 + h // 4, h % 4)],
                              tok_aev_dma[((2 * hh + 1) * 8 + h // 4, h % 4)],
                              tok_lath.get((hh - 1, h))] +
                             dep_half,
                        inc="qsp", inc_by=16)
                for d in range(NCH):
                    tok_xrh[(hh, d)] = E("sp", lambda e, hh=hh, d=d: e.dma_start(
                        AR1[:, 32768 + 1024 * d:32768 + 1024 * d + 1024].rearrange(
                            "p (s f) -> p s f", s=2),
                        xns[d, 2 * hh:2 * hh + 2].transpose([1, 0, 2])),
                        deps=[tok_xns_done] + dep_half, inc="qsp", inc_by=16)

                tok_expl = None
                tok_bc_ev = dep_d_ps
                for h in range(H):
                    t_lr = E("sp", lambda e, hh=hh, h=h: e.dma_start(
                        lrowD.rearrange("p (s f) -> p s f", s=2),
                        lts[h, 2 * hh:2 * hh + 2].unsqueeze(0)),
                        deps=[tok_lts_dma[((2 * hh) * 8 + h // 4, (h % 4) & ~1)],
                              tok_lts_dma[((2 * hh + 1) * 8 + h // 4, (h % 4) & ~1)],
                              tok_expl] + dep_half,
                        inc="qsp", inc_by=16)
                    tok_ln_l = E("act", lambda e: e.activation(
                        lrowD, lrowD, AF.Ln),
                        deps=[t_lr], inc=None)
                    tok_expl = E("act", lambda e: e.activation(
                        linv_row, lrowD, AF.Exp, scale=-1.0),
                        deps=[], inc="sact")
                    tok_bc = None
                    for tb in range(2):
                        tok_bc = E("pe", lambda e, tb=tb: e.matmul(
                            t_ps[:, 512 * tb:512 * tb + 512],
                            onesrowb_sb[:],
                            linv_row[:, 512 * tb:512 * tb + 512],
                            start=True, stop=True),
                            deps=[tok_expl, t_orb] + tok_bc_ev + dep_d_ps,
                            inc="spe" if tb == 1 else None)
                    tok_ev = E("dve", lambda e: e.tensor_copy(
                        linv_bc[:], t_ps[:]),
                        deps=[tok_bc], inc=None)
                    tok_lath[(hh, h)] = E("dve", lambda e, h=h: e.tensor_tensor(
                        AR1[:, 1024 * h:1024 * h + 1024],
                        AR1[:, 1024 * h:1024 * h + 1024], linv_bc[:], ALU.mult),
                        deps=[tok_ath[(hh, h)]], inc="sdve")
                    tok_bc_ev = [tok_lath[(hh, h)]]

                for d in range(NCH):
                    for tb in range(2):
                        tok_tp_stop = E("pe", lambda e, d=d, tb=tb: e.matmul(
                            t_ps[:, 512 * tb:512 * tb + 512],
                            AR2[:, 128 * d:128 * d + 128],
                            AR1[:, 32768 + 1024 * d + 512 * tb:
                                32768 + 1024 * d + 512 * tb + 512],
                            start=(d == 0), stop=(d == NCH - 1)),
                            deps=[t_wA, tok_xrh[(hh, d)], tok_tp_ev,
                                  tok_bc_ev[0]] + dep_d_ps,
                            inc="spe" if (d == NCH - 1 and tb == 1) else None)
                tok_tp_ev = E("dve", lambda e: e.tensor_scalar(
                    tp_h[:], t_ps[:], rscale[:], None, ALU.mult),
                    deps=[tok_tp_stop, tok_rscale,
                          *(bar_qact_l if hh == 0 else [None])], inc="sdve")

                for o in range(NCH):
                    u0 = (2 * o) % 3
                    u1 = (2 * o + 1) % 3
                    dep_w0 = [tok_acc_stop.get((hh, o - 2))]
                    if hh == 0 and o == 0:
                        dep_w0 += [tok_tp_stop]
                    if hh == 1 and o == 0:
                        dep_w0 += [tok_tp_stop, tok_acc_stop.get((0, NCH - 1))]
                    if hh == 0 and o == 1:
                        dep_w0 += bar_qact_l + [bar_dve]
                    tok_wcb[(hh, o, 0)] = E("pool", lambda e, o=o, u0=u0: e.dma_start(
                        wcb[u0].rearrange("p (c f) -> p c f", c=NCH),
                        wo[o].transpose([1, 0, 2])),
                        deps=dep_w0, inc="qpool", inc_by=16)
                    tok_wcb[(hh, o, 1)] = E("pool", lambda e, o=o, u1=u1: e.dma_start(
                        wcb[u1].rearrange("p (c f) -> p c f", c=NCH),
                        wW[o].transpose([1, 0, 2])),
                        deps=[tok_acc_stop.get((hh, o - 1)),
                              (tok_acc_stop.get((0, NCH - 1))
                               if (hh == 1 and o == 0) else None)] +
                             ([bar_pe, bar_dve, bar_pool]
                              if (hh == 0 and o == 0) else []),
                        inc="qpool", inc_by=16)
                    tok_wBt[(hh, o)] = E("pool", lambda e, o=o: e.dma_start(
                        wB_t[o % 2], wB[o]),
                        deps=[tok_acc_stop.get((hh, o - 2)),
                              *(bar_qact_l if (hh == 0 and o <= 1) else [None]),
                              (tok_acc_stop.get((0, 30 + o))
                               if (hh == 1 and o < 2) else None)],
                        inc="qpool", inc_by=16)
                    acc = accs[o % 3]
                    tok_mm = None
                    for h in range(H):
                        for tb in range(2):
                            tok_mm = E("pe",
                                lambda e, h=h, tb=tb, u0=u0, acc=acc: e.matmul(
                                    acc[:, 512 * tb:512 * tb + 512],
                                    wcb[u0][:, 128 * h:128 * h + 128],
                                    AR1[:, 1024 * h + 512 * tb:
                                        1024 * h + 512 * tb + 512],
                                    start=(h == 0), stop=False),
                                deps=[tok_wcb[(hh, o, 0)], tok_lath[(hh, h)],
                                      tok_acts.get((hh, o - 3)),
                                      (tok_acts.get((hh - 1, [30, 31, 29][o]))
                                       if (hh > 0 and o < 3) else None)],
                                inc=None)
                    for d in range(NCH):
                        for tb in range(2):
                            tok_mm = E("pe",
                                lambda e, d=d, tb=tb, u1=u1, acc=acc: e.matmul(
                                    acc[:, 512 * tb:512 * tb + 512],
                                    wcb[u1][:, 128 * d:128 * d + 128],
                                    AR1[:, 32768 + 1024 * d + 512 * tb:
                                        32768 + 1024 * d + 512 * tb + 512],
                                    start=False, stop=False),
                                deps=[tok_wcb[(hh, o, 1)], tok_xrh[(hh, d)]],
                                inc=None)
                    for tb in range(2):
                        tok_mm = E("pe", lambda e, tb=tb, o=o, acc=acc: e.matmul(
                            acc[:, 512 * tb:512 * tb + 512], wB_t[o % 2][:],
                            tp_h[:, 512 * tb:512 * tb + 512],
                            start=False, stop=True),
                            deps=[tok_wBt[(hh, o)], tok_tp_ev],
                            inc="spe" if tb == 1 else None)
                    tok_acc_stop[(hh, o)] = tok_mm
                    Po = o % 2
                    prev_out = tok_out.get((hh, o - 2)) if o >= 2 else (
                        tok_out.get((hh - 1, NCH - 2 + o)) if hh > 0 else None)
                    tok_acts[(hh, o)] = E("act",
                        lambda e, o=o, Po=Po, acc=acc: e.activation(
                            acts[Po], acc[:], AF.Identity, bias=bsum[:, o:o + 1]),
                        deps=[tok_mm, tok_bsum, prev_out,
                              (bar_dve if (hh == 0 and o < 2) else None)],
                        inc="sact")
                    prev_xst = tok_dadd.get((hh, o - 2)) if o >= 2 else (
                        tok_dadd.get((hh - 1, NCH - 2 + o)) if hh > 0 else
                        (bar_pe if o < 2 else None))
                    t_xs = E("sp", lambda e, hh=hh, o=o, Po=Po: e.dma_start(
                        xst[Po].rearrange("p (s f) -> p s f", s=2),
                        xT_own[2 * hh:2 * hh + 2, o].transpose([1, 0, 2])),
                        deps=[prev_xst,
                              (sch.bar("spe") if (hh == 0 and o < 2) else None)],
                        inc="qsp", inc_by=16)
                    tok_dadd[(hh, o)] = E("dve", lambda e, Po=Po: e.tensor_tensor(
                        acts[Po], acts[Po], xst[Po], ALU.add),
                        deps=[tok_acts[(hh, o)], t_xs], inc="sdve")
                    tok_out[(hh, o)] = E("act", lambda e, hh=hh, o=o, Po=Po: e.dma_start(
                        out_t[o, 2 * hh:2 * hh + 2].transpose([1, 0, 2]),
                        acts[Po].rearrange("p (s f) -> p s f", s=2)),
                        deps=[tok_dadd[(hh, o)]], inc="qact", inc_by=16)

            fin = sch.qbar("qact") + sch.qbar("qsp") + sch.qbar("qpool")
            E("sp", lambda e: None, deps=fin)

        _emit_phases()

        _flush_op = {
            "act": lambda e: e.copy(scr_act[:], scr_act[:]),
            "dve": lambda e: e.tensor_copy(scr_dve[:], scr_dve[:]),
            "pool": lambda e: e.tensor_copy(scr_pool[:], scr_pool[:]),
        }
        _flush_sems = ("sact", "sdve", "spool")

        def run_stream(name, eng):
            for wl, fn, inc, inc_by in sch.streams[name]:
                for snm, v in wl:
                    eng.wait_ge(sems[snm], v)
                ins = fn(eng)
                if inc is not None:
                    if name in _flush_op and inc in _flush_sems:
                        eng.drain(fusable=False)
                        _flush_op[name](eng).then_inc(sems[inc], inc_by)
                    else:
                        ins.then_inc(sems[inc], inc_by)

        @block.tensor
        def _(eng):
            run_stream("pe", eng)

        @block.scalar
        def _(eng):
            run_stream("act", eng)

        @block.vector
        def _(eng):
            run_stream("dve", eng)

        @block.gpsimd
        def _(eng):
            run_stream("pool", eng)

        @block.sync
        def _(eng):
            run_stream("sp", eng)

    return nc



_NC_CACHE = {}


def _get_program():
    if "nc" not in _NC_CACHE:
        _NC_CACHE["nc"] = build_program()
    return _NC_CACHE["nc"]


def _pack_weights(inp):
    f32 = np.float32
    wq_w = np.asarray(inp["L0_attn_wq_w"], f32)
    wk_w = np.asarray(inp["L0_attn_wk_w"], f32)
    wv_w = np.asarray(inp["L0_attn_wv_w"], f32)
    wo_w = np.asarray(inp["L0_attn_wo_w"], f32)
    W = np.asarray(inp["L0_ffn_W"], f32)
    A = np.asarray(inp["L0_ffn_A"], f32)
    Bm = np.asarray(inp["L0_ffn_B"], f32)
    rW = np.asarray(inp["L0_ffn_router_W"], f32)

    def colblocks(w):
        wt = np.ascontiguousarray(w.T)
        return np.ascontiguousarray(
            wt.reshape(NCH, 128, NCH, 128).transpose(2, 0, 1, 3))

    def dblocks(w):
        wt = np.ascontiguousarray(w.T)
        return np.ascontiguousarray(wt.reshape(NCH, 128, w.shape[0]))

    d = {}
    d["wq"] = colblocks(wq_w)
    d["wo"] = colblocks(wo_w)
    d["wW"] = colblocks(W)
    d["wk"] = dblocks(wk_w)
    d["wv"] = dblocks(wv_w)
    A_all = A.reshape(128, D)
    d["wA"] = dblocks(A_all)
    B_all = Bm.transpose(0, 2, 1).reshape(128, D)
    d["wB"] = np.ascontiguousarray(
        B_all.reshape(128, NCH, 128).transpose(1, 0, 2))
    d["wr"] = dblocks(rW)
    d["bq"] = np.ascontiguousarray(
        np.asarray(inp["L0_attn_wq_b"], f32).reshape(NCH, 128).T)
    d["bk"] = np.asarray(inp["L0_attn_wk_b"], f32).reshape(128, 1)
    d["bv"] = np.asarray(inp["L0_attn_wv_b"], f32).reshape(128, 1)
    d["bo"] = np.ascontiguousarray(
        np.asarray(inp["L0_attn_wo_b"], f32).reshape(NCH, 128).T)
    d["bfb"] = np.ascontiguousarray(
        np.asarray(inp["L0_ffn_b"], f32).reshape(NCH, 128).T)
    d["br"] = np.asarray(inp["L0_ffn_router_b"], f32).reshape(1, 8)
    d["identf"] = np.eye(128, dtype=f32)
    d["onescol_f"] = np.ones((128, 1), f32)
    d["onesrow_f"] = np.ones((1, 128), f32)
    d["onesrow_b"] = np.ones((1, 128), ml_dtypes.bfloat16)
    d["ones1_b"] = np.ones((1, 1), ml_dtypes.bfloat16)
    sel = np.zeros((8, 128), f32)
    for e_ in range(8):
        sel[e_, e_ * 16:(e_ + 1) * 16] = 2.0
    d["sel_b"] = sel.astype(ml_dtypes.bfloat16)
    return d


def _build_masks(own_supers):
    m = np.zeros((4, 8, 128, 512), np.float32)
    tri = np.triu(np.ones((128, 128), np.float32))
    for s, g in enumerate(own_supers):
        L = LIMS[s]
        for j in range(8):
            kc = L - 8 + j
            for qt in range(4):
                gq = 4 * g + qt
                if kc < gq:
                    m[s, j, :, qt * 128:(qt + 1) * 128] = 1.0
                elif kc == gq:
                    m[s, j, :, qt * 128:(qt + 1) * 128] = tri
    return m.astype(ml_dtypes.bfloat16)


def kernel(**inputs):
    x = np.asarray(inputs["L0_attn_x"], np.float32)
    wd = _pack_weights(inputs)
    masks_by_p = {p: _build_masks(OWN_SUPER[p]) for p in (0, 1)}

    in_maps = []
    for c in range(8):
        b, p = c // 2, c % 2
        owns = OWN_SUPER[p]
        xb = np.ascontiguousarray(x[b])
        xTb = np.ascontiguousarray(xb.T)
        xT_blocks = np.ascontiguousarray(
            xTb.reshape(NCH, 128, NW, 512).transpose(2, 0, 1, 3))
        own_rows = np.concatenate(
            [np.arange(512 * g, 512 * (g + 1)) for g in owns])
        xtok_own = np.ascontiguousarray(xb[own_rows])
        xT_own = np.ascontiguousarray(xT_blocks[owns])
        m = {"x_glob": xb, "xtok_own": xtok_own, "xT_glob": xT_blocks,
             "xT_own": xT_own, "masks": masks_by_p[p]}
        m.update(wd)
        in_maps.append(m)

    nc = _get_program()
    res = run_bass_kernel_spmd(nc, in_maps, core_ids=list(range(8)))

    out = np.empty((4, 4096, 4096), np.float32)
    for c in range(8):
        b, p = c // 2, c % 2
        owns = OWN_SUPER[p]
        r = res.results[c]["out"]
        arr = np.ascontiguousarray(r.transpose(1, 3, 0, 2)).reshape(4, 512, 4096)
        for wi, g in enumerate(owns):
            out[b, 512 * g:512 * (g + 1), :] = arr[wi]
    return out


if __name__ == "__main__":
    import reference
    inp = reference.setup_inputs()
    o = kernel(**{k: np.asarray(v) for k, v in inp.items()})
    ref = np.asarray(reference.reference(**inp))
    err = np.abs(o - ref)
    rel = np.sqrt(((o - ref) ** 2).mean() / (ref ** 2).mean())
    print("max abs err:", err.max(), "rel l2:", rel)


# version 03
# speedup vs baseline: 1.0782x; 1.0782x over previous
import numpy as np
import ml_dtypes
import concourse.bass as bass
import concourse.mybir as mybir
from concourse.bass_utils import run_bass_kernel_spmd

F32 = mybir.dt.float32
BF16 = mybir.dt.bfloat16
AF = mybir.ActivationFunctionType
ALU = mybir.AluOpType
AX = mybir.AxisListType

D = 4096
T = 4096
HD = 128
H = 32
NCH = 32
NW = 8
OWN_W = 4
OT = 2048
LIMS = [8, 16, 24, 32]
OWN_SUPER = {0: [0, 3, 4, 7], 1: [1, 2, 5, 6]}
EPS = 1e-5
SM_SCALE = float(1.0 / np.sqrt(128.0))
Q_LO, Q_HI = 611, 4095
NTOK_R = Q_HI - Q_LO


class Sched:

    ENGINES = ("pe", "act", "dve", "pool", "sp")

    NQROT = 8

    def __init__(self):
        self.streams = {e: [] for e in self.ENGINES}
        self.cnt = {}
        self.lastw = {}
        self.qrot = {}

    def emit(self, eng, fn, deps=(), inc=None, inc_by=1):
        wl = []
        for tok in deps:
            if tok is None:
                continue
            s, v = tok
            if v > self.lastw.get((eng, s), 0):
                wl.append((s, v))
                self.lastw[(eng, s)] = v
        tok = None
        if inc is not None:
            if inc in ("qsp", "qact", "qpool"):
                r = self.qrot.get(inc, 0)
                self.qrot[inc] = (r + 1) % self.NQROT
                inc = f"{inc}{r}"
            self.cnt[inc] = self.cnt.get(inc, 0) + inc_by
            tok = (inc, self.cnt[inc])
        self.streams[eng].append((wl, fn, inc, inc_by))
        return tok

    def qbar(self, prefix):
        return [(f"{prefix}{r}", self.cnt.get(f"{prefix}{r}", 0))
                for r in range(self.NQROT)]

    def bar(self, name):
        return (name, self.cnt.get(name, 0))


class _Stop(Exception):
    pass


def build_program(upto="d"):
    nc = bass.Bass("TRN2", target_bir_lowering=False)

    def din(name, shape, dt=F32):
        return nc.dram_tensor(name, list(shape), dt, kind="ExternalInput").ap()

    x_glob = din("x_glob", (T, D))
    xtok_own = din("xtok_own", (OT, D))
    xT_glob = din("xT_glob", (NW, NCH, 128, 512))
    xT_own = din("xT_own", (OWN_W, NCH, 128, 512))
    wq = din("wq", (NCH, NCH, 128, 128))
    wk = din("wk", (NCH, 128, 128))
    wv = din("wv", (NCH, 128, 128))
    wo = din("wo", (NCH, NCH, 128, 128))
    wW = din("wW", (NCH, NCH, 128, 128))
    wA = din("wA", (NCH, 128, 128))
    wB = din("wB", (NCH, 128, 128))
    wr = din("wr", (NCH, 128, 8))
    bq = din("bq", (128, 32))
    bk = din("bk", (128, 1))
    bv = din("bv", (128, 1))
    bo = din("bo", (128, 32))
    bfb = din("bfb", (128, 32))
    br = din("br", (1, 8))
    identf = din("identf", (128, 128))
    onescol_f = din("onescol_f", (128, 1))
    onescol_bf = din("onescol_bf", (128, 1), BF16)
    onesrow_f = din("onesrow_f", (1, 128))
    onesrow_b = din("onesrow_b", (1, 128), BF16)
    ones1_b = din("ones1_b", (1, 1), BF16)
    sel_b = din("sel_b", (8, 128), BF16)
    masks = din("masks", (4, 8, 128, 512), BF16)

    out_t = nc.dram_tensor("out", [NCH, OWN_W, 128, 512], F32,
                           kind="ExternalOutput").ap()
    dbg = {}
    if upto != "d":
        if upto == "a0":
            dbg["d_own"] = nc.dram_tensor("d_own", [128, 4096], F32, kind="ExternalOutput").ap()
            dbg["d_glob"] = nc.dram_tensor("d_glob", [128, 8192], F32, kind="ExternalOutput").ap()
        if upto == "a1":
            dbg["d_kT"] = nc.dram_tensor("d_kT", [128, 4096], BF16, kind="ExternalOutput").ap()
            dbg["d_v"] = nc.dram_tensor("d_v", [128, 4096], BF16, kind="ExternalOutput").ap()
            dbg["d_qs"] = nc.dram_tensor("d_qs", [128, 32], F32, kind="ExternalOutput").ap()
        if upto == "rt":
            dbg["d_rsc"] = nc.dram_tensor("d_rsc", [128, 1], F32, kind="ExternalOutput").ap()
            dbg["d_rsm"] = nc.dram_tensor("d_rsm", [1, 64], F32, kind="ExternalOutput").ap()
        if upto == "b":
            dbg["d_qts"] = nc.dram_tensor("d_qts", [H, 4, 128, 512], BF16, kind="ExternalOutput").ap()
            dbg["d_xns"] = nc.dram_tensor("d_xns", [NCH, OWN_W, 128, 512], BF16, kind="ExternalOutput").ap()
        if upto == "c":
            dbg["d_ats"] = nc.dram_tensor("d_ats", [H, 4, 128, 512], BF16, kind="ExternalOutput").ap()
            dbg["d_pacc"] = nc.dram_tensor("d_pacc", [128, 2048], F32, kind="ExternalOutput").ap()
            dbg["d_linv"] = nc.dram_tensor("d_linv", [128, 512], F32, kind="ExternalOutput").ap()
            dbg["d_pg"] = nc.dram_tensor("d_pg", [128, 2048], BF16, kind="ExternalOutput").ap()
    xns = nc.dram_tensor("xns", [NCH, OWN_W, 128, 512], BF16, kind="Internal").ap()
    qts = nc.dram_tensor("qts", [H, 4, 128, 512], BF16, kind="Internal").ap()
    ats = nc.dram_tensor("ats", [H, 4, 128, 512], BF16, kind="Internal").ap()
    lts = nc.dram_tensor("lts", [H, 4, 512], F32, kind="Internal").ap()

    sch = Sched()
    E = sch.emit

    import contextlib
    with contextlib.ExitStack() as _st:
        AR1_t = _st.enter_context(nc.sbuf_tensor("AR1", [128, 65536], BF16))
        AR2_t = _st.enter_context(nc.sbuf_tensor("AR2", [128, 16384], BF16))
        AR3_t = _st.enter_context(nc.sbuf_tensor("AR3", [128, 5120], F32))
        kT_sb = _st.enter_context(nc.sbuf_tensor("kT_sb", [128, 4096], BF16))
        v_sb = _st.enter_context(nc.sbuf_tensor("v_sb", [128, 4096], BF16))
        mask_sb = _st.enter_context(nc.sbuf_tensor("mask_sb", [128, 4096], BF16))
        stats_sb = _st.enter_context(nc.sbuf_tensor("stats_sb", [128, 48], F32))
        aggr_sb = _st.enter_context(nc.sbuf_tensor("aggr_sb", [128, 2], F32))
        lncol = _st.enter_context(nc.sbuf_tensor("lncol", [128, 1], F32))
        rcol = _st.enter_context(nc.sbuf_tensor("rcol", [128, 1], F32))
        rmcol = _st.enter_context(nc.sbuf_tensor("rmcol", [128, 1], F32))
        qs_sb = _st.enter_context(nc.sbuf_tensor("qs_sb", [128, 32], F32))
        qtmp = _st.enter_context(nc.sbuf_tensor("qtmp", [128, 1], F32))
        brsb = _st.enter_context(nc.sbuf_tensor("brsb", [1, 8], F32))
        rsm = _st.enter_context(nc.sbuf_tensor("rsm", [1, 64], F32))
        routing_b = _st.enter_context(nc.sbuf_tensor("routing_b", [1, 8], BF16))
        rtcol_b = _st.enter_context(nc.sbuf_tensor("rtcol_b", [8, 1], BF16))
        rscale = _st.enter_context(nc.sbuf_tensor("rscale", [128, 1], F32))
        bq_sb = _st.enter_context(nc.sbuf_tensor("bq_sb", [128, 32], F32))
        bk_sb = _st.enter_context(nc.sbuf_tensor("bk_sb", [128, 1], F32))
        bv_sb = _st.enter_context(nc.sbuf_tensor("bv_sb", [128, 1], F32))
        bo_sb = _st.enter_context(nc.sbuf_tensor("bo_sb", [128, 32], F32))
        bf_sb = _st.enter_context(nc.sbuf_tensor("bf_sb", [128, 32], F32))
        bsum = _st.enter_context(nc.sbuf_tensor("bsum", [128, 32], F32))
        identf_sb = _st.enter_context(nc.sbuf_tensor("identf_sb", [128, 128], F32))
        onescol_sb = _st.enter_context(nc.sbuf_tensor("onescol_sb", [128, 1], F32))
        onescolb_sb = _st.enter_context(nc.sbuf_tensor("onescolb_sb", [128, 1], BF16))
        onesrow_sb = _st.enter_context(nc.sbuf_tensor("onesrow_sb", [1, 128], F32))
        onesrowb_sb = _st.enter_context(nc.sbuf_tensor("onesrowb_sb", [1, 128], BF16))
        ones1_sb = _st.enter_context(nc.sbuf_tensor("ones1_sb", [1, 1], BF16))
        sel_sb = _st.enter_context(nc.sbuf_tensor("sel_sb", [8, 128], BF16))
        epscol = _st.enter_context(nc.sbuf_tensor("epscol", [128, 1], F32))
        scr_act = _st.enter_context(nc.sbuf_tensor("scr_act", [1, 1], F32))
        scr_dve = _st.enter_context(nc.sbuf_tensor("scr_dve", [1, 1], F32))
        scr_pool = _st.enter_context(nc.sbuf_tensor("scr_pool", [1, 1], F32))
        ps_A = _st.enter_context(nc.psum_tensor([128, 2048], F32))
        ps_B = _st.enter_context(nc.psum_tensor([128, 2048], F32))
        qsems = {}
        for _qp in ("qsp", "qact", "qpool"):
            for _r in range(Sched.NQROT):
                qsems[f"{_qp}{_r}"] = _st.enter_context(
                    nc.semaphore(f"{_qp}{_r}"))
        sem_spe = _st.enter_context(nc.semaphore("spe"))
        sem_sact = _st.enter_context(nc.semaphore("sact"))
        sem_sdve = _st.enter_context(nc.semaphore("sdve"))
        sem_spool = _st.enter_context(nc.semaphore("spool"))
        block = _st.enter_context(nc.Block())
        sems = {"spe": sem_spe, "sact": sem_sact, "sdve": sem_sdve,
                "spool": sem_spool}
        sems.update(qsems)

        _phases_done = False
        AR1 = AR1_t[:]
        AR2 = AR2_t[:]
        AR3 = AR3_t[:]
        AR1f = AR1.bitcast(F32)
        AR2f = AR2.bitcast(F32)
        rowb = AR3_t[0:1, 4096:5120]
        lnl = AR2f[0:1, 5120:6144]
        wr_sb = AR2f[:, 6144:6400]

        def _emit_phases():
            tok_eps = E("dve", lambda e: e.memset(epscol[:], EPS), inc="sdve")
            t_idf = E("sp", lambda e: e.dma_start(identf_sb[:], identf), inc="qsp", inc_by=16)
            t_oc = E("sp", lambda e: e.dma_start(onescol_sb[:], onescol_f), inc="qsp", inc_by=16)
            t_ocb = E("sp", lambda e: e.dma_start(onescolb_sb[:], onescol_bf), inc="qsp", inc_by=16)
            t_or = E("sp", lambda e: e.dma_start(onesrow_sb[:], onesrow_f), inc="qsp", inc_by=16)
            t_orb = E("sp", lambda e: e.dma_start(onesrowb_sb[:], onesrow_b), inc="qsp", inc_by=16)
            t_o1 = E("sp", lambda e: e.dma_start(ones1_sb[:], ones1_b), inc="qsp", inc_by=16)
            t_sel = E("sp", lambda e: e.dma_start(sel_sb[:], sel_b), inc="qsp", inc_by=16)
            t_bk = E("sp", lambda e: e.dma_start(bk_sb[:], bk), inc="qsp", inc_by=16)
            t_bv = E("sp", lambda e: e.dma_start(bv_sb[:], bv), inc="qsp", inc_by=16)
            t_bq = E("sp", lambda e: e.dma_start(bq_sb[:], bq), inc="qsp", inc_by=16)
            t_wr = E("sp", lambda e: e.dma_start(
                wr_sb[:].rearrange("p (c e) -> p c e", c=NCH),
                wr.transpose([1, 0, 2])), inc="qsp", inc_by=16)
            t_br = E("sp", lambda e: e.dma_start(brsb[:], br), inc="qsp", inc_by=16)

            def a0_pass(src_ap, ntiles, rbc, rmbc, war):
                src_v = src_ap.rearrange("(n p) d -> n p d", p=128)
                stage = [AR1f[:, 0:4096], AR1f[:, 4096:8192]]
                t_stage_rd = [war.get("stage0"), war.get("stage1")]
                t_row_ev1 = war.get("row1")
                t_row_ev2 = war.get("row2")
                t_bc_ev_r = war.get("bcr")
                t_bc_ev_m = war.get("bcm")
                t_rc_read = war.get("rc")
                t_rm_read = war.get("rm")
                rc_mm = [None] * 4
                rm_mm = [None] * 4
                for tt in range(ntiles):
                    P = tt % 2
                    g, j = tt // 4, tt % 4
                    st = stage[P]
                    t_x = E("sp", lambda e, st=st, tt=tt: e.dma_start(st, src_v[tt]),
                            deps=[t_stage_rd[P]], inc="qsp", inc_by=16)
                    tok_bn = None
                    for i in range(8):
                        tok_bn = E("dve", lambda e, st=st, i=i: e.bn_stats(
                            stats_sb[:, 6 * i:6 * i + 6], st[:, 512 * i:512 * i + 512]),
                            deps=[t_x], inc="sdve" if i == 7 else None)
                    t_stage_rd[P] = tok_bn
                    tok_ag = E("dve", lambda e: e.bn_aggr(aggr_sb[:], stats_sb[:]),
                               deps=[tok_bn], inc="sdve")
                    tok_ln = E("act", lambda e: e.activation(
                        lncol[:], aggr_sb[:, 1:2], AF.Ln, bias=epscol[:]),
                        deps=[tok_ag, tok_eps], inc="sact")
                    tok_r = E("act", lambda e: e.activation(
                        rcol[:], lncol[:], AF.Exp, scale=-0.5),
                        deps=[tok_ln, t_rc_read], inc="sact")
                    tok_rm = E("dve", lambda e: e.tensor_tensor(
                        rmcol[:], aggr_sb[:, 0:1], rcol[:], ALU.mult),
                        deps=[tok_r, t_rm_read], inc="sdve")
                    rc_mm[j] = E("pe", lambda e, j=j: e.matmul(
                        ps_A[0:1, 128 * j:128 * j + 128], rcol[:], identf_sb[:],
                        start=True, stop=True),
                        deps=[tok_r, t_idf, t_row_ev1], inc="spe")
                    rm_mm[j] = E("pe", lambda e, j=j: e.matmul(
                        ps_A[0:1, 512 + 128 * j:512 + 128 * j + 128], rmcol[:],
                        identf_sb[:], start=True, stop=True),
                        deps=[tok_rm, t_row_ev2], inc="spe")
                    t_rc_read = rc_mm[j]
                    t_rm_read = rm_mm[j]
                    if j == 3:
                        t_row_ev1 = E("act", lambda e: e.activation(
                            rowb[:, 0:512], ps_A[0:1, 0:512], AF.Copy),
                            deps=rc_mm, inc="sact")
                        t_row_ev2 = E("act", lambda e: e.activation(
                            rowb[:, 512:1024], ps_A[0:1, 512:1024], AF.Copy),
                            deps=rm_mm, inc="sact")
                        bc1 = E("pe", lambda e: e.matmul(
                            ps_A[:, 1024:1536], onesrow_sb[:], rowb[:, 0:512],
                            start=True, stop=True),
                            deps=[t_row_ev1, t_or, t_bc_ev_r], inc="spe")
                        bc2 = E("pe", lambda e: e.matmul(
                            ps_A[:, 1536:2048], onesrow_sb[:], rowb[:, 512:1024],
                            start=True, stop=True),
                            deps=[t_row_ev2, t_bc_ev_m], inc="spe")
                        t_bc_ev_r = E("act", lambda e, g=g: e.activation(
                            rbc[:, 512 * g:512 * g + 512], ps_A[:, 1024:1536],
                            AF.Copy), deps=[bc1], inc="sact")
                        t_bc_ev_m = E("act", lambda e, g=g: e.activation(
                            rmbc[:, 512 * g:512 * g + 512], ps_A[:, 1536:2048],
                            AF.Copy), deps=[bc2], inc="sact")
                return {"stage0": t_stage_rd[0], "stage1": t_stage_rd[1],
                        "row1": t_row_ev1, "row2": t_row_ev2,
                        "bcr": t_bc_ev_r, "bcm": t_bc_ev_m,
                        "rc": t_rc_read, "rm": t_rm_read}

            r_bc_own = AR3[:, 0:2048]
            rm_bc_own = AR3[:, 2048:4096]
            w0 = a0_pass(xtok_own, 16, r_bc_own, rm_bc_own, {})
            tok_own_r, tok_own_m = w0["bcr"], w0["bcm"]
            r_bc_g = AR1f[:, 20480:24576]
            rm_bc_g = AR1f[:, 24576:28672]
            w1 = a0_pass(x_glob, 32, r_bc_g, rm_bc_g, w0)
            tok_g_r, tok_g_m = w1["bcr"], w1["bcm"]
            if upto == "a0":
                t1 = E("sp", lambda e: e.dma_start(dbg["d_own"], AR3[:, 0:4096]),
                       deps=[tok_own_r, tok_own_m], inc="qsp", inc_by=16)
                t2 = E("sp", lambda e: e.dma_start(dbg["d_glob"], AR1f[:, 20480:28672]),
                       deps=[tok_g_r, tok_g_m], inc="qsp", inc_by=16)
                E("sp", lambda e: None, deps=[t1, t2])
                return

            t_wk = E("pool", lambda e: e.dma_start(
                AR2[:, 0:4096].rearrange("p (c f) -> p c f", c=NCH),
                wk.transpose([1, 0, 2])), inc="qpool", inc_by=16)
            t_wv = E("pool", lambda e: e.dma_start(
                AR2[:, 4096:8192].rearrange("p (c f) -> p c f", c=NCH),
                wv.transpose([1, 0, 2])), inc="qpool", inc_by=16)

            xn_w = [AR1[:, 0:16384], AR1[:, 16384:32768]]
            xstage = [AR1f[:, 16384 + 512 * k:16384 + 512 * k + 512] for k in range(3)]
            tmpf = [AR1f[:, 28672:29184], AR1f[:, 29184:29696]]
            vtmp = AR1f[:, 29696:30208]

            tok_xn = {}
            tok_kv_stop = {}
            tok_qsum_last = None
            tok_kev = None
            tok_vev = None
            tok_vtr_ev = [None, None]
            tok_xstage_rd = [w1["stage0"], w1["stage1"], None]
            tok_qs_wr = None
            tok_v_done = None

            for w in range(NW):
                P = w % 2
                dep_war = [tok_kv_stop.get(w - 2)]
                for d in range(NCH):
                    k3 = (w * NCH + d) % 3
                    t_x = E("sp", lambda e, w=w, d=d, k3=k3: e.dma_start(
                        xstage[k3], xT_glob[w, d]),
                        deps=[tok_xstage_rd[k3]], inc="qsp", inc_by=16)
                    tp = d % 2
                    tok_t = E("dve", lambda e, w=w, tp=tp, k3=k3: e.tensor_tensor(
                        tmpf[tp], xstage[k3], r_bc_g[:, 512 * w:512 * w + 512],
                        ALU.mult),
                        deps=[t_x, tok_g_r] + dep_war, inc="sdve")
                    tok_xn[(w, d)] = E("dve",
                        lambda e, w=w, d=d, P=P, tp=tp: e.tensor_tensor(
                            xn_w[P][:, 512 * d:512 * d + 512], tmpf[tp],
                            rm_bc_g[:, 512 * w:512 * w + 512], ALU.subtract),
                        deps=[tok_t, tok_g_m], inc="sdve")
                    tok_xstage_rd[k3] = tok_t
                tok_k_mm = None
                for d in range(NCH):
                    tok_k_mm = E("pe", lambda e, d=d, P=P: e.matmul(
                        ps_B[:, 0:512], AR2[:, 128 * d:128 * d + 128],
                        xn_w[P][:, 512 * d:512 * d + 512],
                        start=(d == 0), stop=(d == NCH - 1)),
                        deps=[tok_xn[(w, d)], t_wk, tok_kev],
                        inc="spe" if d == NCH - 1 else None)
                tok_kev = E("act", lambda e, w=w: e.activation(
                    kT_sb[:, 512 * w:512 * w + 512], ps_B[:, 0:512], AF.Identity,
                    bias=bk_sb[:]), deps=[tok_k_mm, t_bk], inc="sact")
                tok_v_mm = None
                for d in range(NCH):
                    tok_v_mm = E("pe", lambda e, d=d, P=P: e.matmul(
                        ps_B[:, 512:1024],
                        AR2[:, 4096 + 128 * d:4096 + 128 * d + 128],
                        xn_w[P][:, 512 * d:512 * d + 512],
                        start=(d == 0), stop=(d == NCH - 1)),
                        deps=[tok_xn[(w, d)], t_wv, tok_vev],
                        inc="spe" if d == NCH - 1 else None)
                tok_vev = E("act", lambda e: e.activation(
                    vtmp, ps_B[:, 512:1024], AF.Identity, bias=bv_sb[:]),
                    deps=[tok_v_mm, t_bv], inc="sact")
                for j in range(4):
                    sl = 1024 + 512 * (j % 2)
                    tok_tr = E("pe", lambda e, j=j, sl=sl: e.transpose(
                        ps_B[:, sl:sl + 128],
                        vtmp[:, 128 * j:128 * j + 128], identf_sb[:]),
                        deps=[tok_vev, tok_vtr_ev[j % 2]], inc="spe")
                    tok_vtr_ev[j % 2] = E("act", lambda e, w=w, j=j, sl=sl: e.activation(
                        v_sb[:, (4 * w + j) * 128:(4 * w + j) * 128 + 128],
                        ps_B[:, sl:sl + 128], AF.Copy),
                        deps=[tok_tr], inc="sact")
                tok_v_done = tok_vtr_ev[1]
                tok_kv_stop[w] = tok_v_mm
                if w >= 1:
                    lo = max(0, Q_LO - 512 * w)
                    hi = min(512, Q_HI - 512 * w)
                    for d in range(NCH):
                        tq1 = E("dve", lambda e, d=d, P=P, lo=lo, hi=hi: e.reduce_sum(
                            qtmp[:], xn_w[P][:, 512 * d + lo:512 * d + hi], axis=AX.X),
                            deps=[tok_xn[(w, d)], tok_qs_wr], inc="sdve")
                        if w == 1:
                            tok_qs_wr = E("dve", lambda e, d=d: e.tensor_copy(
                                qs_sb[:, d:d + 1], qtmp[:]), deps=[tq1], inc="sdve")
                        else:
                            tok_qs_wr = E("dve", lambda e, d=d: e.tensor_tensor(
                                qs_sb[:, d:d + 1], qs_sb[:, d:d + 1], qtmp[:],
                                ALU.add), deps=[tq1], inc="sdve")
            tok_qsum_last = tok_qs_wr
            tok_kT_done = tok_kev
            if upto == "a1":
                t1 = E("sp", lambda e: e.dma_start(dbg["d_kT"], kT_sb[:]),
                       deps=[tok_kT_done], inc="qsp", inc_by=16)
                t2 = E("sp", lambda e: e.dma_start(dbg["d_v"], v_sb[:]),
                       deps=[tok_v_done], inc="qsp", inc_by=16)
                t3 = E("sp", lambda e: e.dma_start(dbg["d_qs"], qs_sb[:]),
                       deps=[tok_qsum_last], inc="qsp", inc_by=16)
                E("sp", lambda e: None, deps=[t1, t2, t3])
                return

            tok_lps = None
            for d in range(NCH):
                tok_lps = E("pe", lambda e, d=d: e.matmul(
                    ps_A[0:1, 0:8], qs_sb[:, d:d + 1], wr_sb[:, 8 * d:8 * d + 8],
                    start=(d == 0), stop=(d == NCH - 1)),
                    deps=[tok_qsum_last, t_wr, w1["row1"]],
                    inc="spe" if d == NCH - 1 else None)
            tok_lg = E("act", lambda e: e.activation(
                rsm[0:1, 0:8], ps_A[0:1, 0:8], AF.Copy, scale=1.0 / NTOK_R),
                deps=[tok_lps], inc="sact")
            tok_lg2 = E("dve", lambda e: e.tensor_tensor(
                rsm[0:1, 8:16], rsm[0:1, 0:8], brsb[:], ALU.add),
                deps=[tok_lg, t_br], inc="sdve")
            tok_le = E("act", lambda e: e.activation(
                rsm[0:1, 16:24], rsm[0:1, 8:16], AF.Exp), deps=[tok_lg2], inc="sact")
            tok_ls = E("dve", lambda e: e.reduce_sum(
                rsm[0:1, 24:25], rsm[0:1, 16:24], axis=AX.X), deps=[tok_le], inc="sdve")
            tok_lr = E("dve", lambda e: e.reciprocal(
                rsm[0:1, 25:26], rsm[0:1, 24:25]), deps=[tok_ls], inc="sdve")
            tok_rt = E("dve", lambda e: e.tensor_scalar(
                rsm[0:1, 32:40], rsm[0:1, 16:24], rsm[0:1, 25:26], None, ALU.mult),
                deps=[tok_lr], inc="sdve")
            tok_rtb = E("dve", lambda e: e.tensor_copy(
                routing_b[:], rsm[0:1, 32:40]), deps=[tok_rt], inc="sdve")
            tok_rt1 = E("pe", lambda e: e.matmul(
                ps_A[0:8, 512:513], routing_b[:], ones1_sb[:], start=True, stop=True),
                deps=[tok_rtb, t_o1, w1["row2"]], inc="spe")
            tok_rtc = E("act", lambda e: e.activation(
                rtcol_b[:], ps_A[0:8, 512:513], AF.Copy), deps=[tok_rt1], inc="sact")
            tok_rsc = E("pe", lambda e: e.matmul(
                ps_A[:, 1024:1025], sel_sb[:], rtcol_b[:], start=True, stop=True),
                deps=[tok_rtc, t_sel, w1["bcr"]], inc="spe")
            tok_rscale = E("act", lambda e: e.activation(
                rscale[:], ps_A[:, 1024:1025], AF.Copy), deps=[tok_rsc], inc="sact")
            if upto == "rt":
                t1 = E("sp", lambda e: e.dma_start(dbg["d_rsc"], rscale[:]),
                       deps=[tok_rscale], inc="qsp", inc_by=16)
                t2 = E("sp", lambda e: e.dma_start(dbg["d_rsm"], rsm[:]),
                       deps=[tok_rtb], inc="qsp", inc_by=16)
                E("sp", lambda e: None, deps=[t1, t2])
                return

            bstage = [AR2f[:, 4096 + 512 * k:4096 + 512 * k + 512] for k in range(2)]
            btmp = [AR2f[:, 5120 + 512 * k:5120 + 512 * k + 512] for k in range(2)]
            dep_b_war = [tok_kv_stop[NW - 1], tok_kv_stop[NW - 2], tok_v_done,
                         tok_qsum_last]
            tok_bstage_rd = [None, None]
            tok_xn_own = {}
            tok_xns = {}
            for d in range(NCH):
                for wi in range(OWN_W):
                    k2 = (d * OWN_W + wi) % 2
                    t_x = E("sp", lambda e, d=d, wi=wi, k2=k2: e.dma_start(
                        bstage[k2], xT_own[wi, d]),
                        deps=[tok_bstage_rd[k2]], inc="qsp", inc_by=16)
                    tok_t = E("dve", lambda e, wi=wi, k2=k2: e.tensor_tensor(
                        btmp[k2], bstage[k2], r_bc_own[:, 512 * wi:512 * wi + 512],
                        ALU.mult),
                        deps=[t_x, tok_own_r] + dep_b_war, inc="sdve")
                    tok_xn_own[(d, wi)] = E("dve",
                        lambda e, d=d, wi=wi, k2=k2: e.tensor_tensor(
                            AR1[:, 2048 * d + 512 * wi:2048 * d + 512 * wi + 512],
                            btmp[k2], rm_bc_own[:, 512 * wi:512 * wi + 512],
                            ALU.subtract),
                        deps=[tok_t, tok_own_m], inc="sdve")
                    tok_bstage_rd[k2] = tok_t
                tok_xns[d] = E("act", lambda e, d=d: e.dma_start(
                    xns[d].transpose([1, 0, 2]),
                    AR1[:, 2048 * d:2048 * d + 2048].rearrange(
                        "p (s f) -> p s f", s=4)),
                    deps=[tok_xn_own[(d, wi2)] for wi2 in range(OWN_W)],
                    inc="qact", inc_by=16)
            dep_xn_own = [tok_xn_own[(d, OWN_W - 1)] for d in range(NCH)]
            tok_xns_done = tok_xns[NCH - 1]

            wq_cb = [AR2[:, 0:4096], AR2[:, 4096:8192]]
            qevac = [AR2[:, 12288:14336], AR2[:, 14336:16384]]
            tok_q_stop = {}
            tok_q_ev = {}
            tok_q_spill = {}
            for hd in range(H):
                P = hd % 2
                qps = ps_A if P == 0 else ps_B
                t_w = E("pool", lambda e, hd=hd, P=P: e.dma_start(
                    wq_cb[P].rearrange("p (c f) -> p c f", c=NCH),
                    wq[hd].transpose([1, 0, 2])),
                    deps=[tok_q_stop.get(hd - 2),
                          (tok_kv_stop[NW - 1] if hd <= 1 else None)],
                    inc="qpool", inc_by=16)
                extra = []
                if hd == 0:
                    extra = [tok_lg, tok_rtc, tok_rscale, w1["row1"], w1["row2"],
                             w1["bcr"], w1["bcm"]]
                if hd == 1:
                    extra = [tok_kev, tok_vev, tok_v_done]
                tok_mm = None
                for d in range(NCH):
                    for tb in range(4):
                        tok_mm = E("pe",
                            lambda e, P=P, d=d, tb=tb, qps=qps: e.matmul(
                                qps[:, 512 * tb:512 * tb + 512],
                                wq_cb[P][:, 128 * d:128 * d + 128],
                                AR1[:, 2048 * d + 512 * tb:2048 * d + 512 * tb + 512],
                                start=(d == 0), stop=(d == NCH - 1)),
                            deps=[t_w, tok_q_ev.get(hd - 2)] + extra +
                                 (dep_xn_own if hd == 0 else []),
                            inc="spe" if (d == NCH - 1 and tb == 3) else None)
                tok_q_stop[hd] = tok_mm
                tok_q_ev[hd] = E("act", lambda e, hd=hd, P=P, qps=qps: e.activation(
                    qevac[P], qps[:], AF.Identity, bias=bq_sb[:, hd:hd + 1]),
                    deps=[tok_mm, t_bq, tok_q_spill.get(hd - 2)], inc="sact")
                tok_q_spill[hd] = E("act", lambda e, hd=hd, P=P: e.dma_start(
                    qts[hd].transpose([1, 0, 2]),
                    qevac[P].rearrange("p (s f) -> p s f", s=4)),
                    deps=[tok_q_ev[hd]], inc="qact", inc_by=16)

            if upto == "b":
                t1 = E("sp", lambda e: e.dma_start(dbg["d_qts"], qts),
                       deps=[tok_q_spill[H - 1]], inc="qsp", inc_by=16)
                t2 = E("sp", lambda e: e.dma_start(dbg["d_xns"], xns),
                       deps=[tok_xns_done], inc="qsp", inc_by=16)
                E("sp", lambda e: None, deps=[t1, t2] + sch.qbar("qact"))
                return

            qtb = [AR2[:, 0:1024], AR2[:, 1024:2048]]
            pg = [AR2[:, 2048 + 1024 * i:2048 + 1024 * i + 1024] for i in range(4)]
            aev = [AR2[:, 6144 + 512 * j:6144 + 512 * j + 512] for j in range(2)]
            lrow = AR2f[0:1, 3584:4608]
            sT = [ps_A[:, 0:1024], ps_A[:, 1024:2048]]
            accs_c = [ps_B[:, 0:512], ps_B[:, 512:1024]]
            lacc = [ps_B[0:1, 1024:1536], ps_B[0:1, 1536:2048]]

            dep_c_ar2 = [tok_q_stop[H - 1], tok_q_stop[H - 2],
                         tok_q_spill[H - 1], tok_q_spill[H - 2]]

            tok_mask = None
            tok_mask_use = None
            tok_qtb = {}
            tok_qk_last = {}
            tok_exp_tok = {}
            tok_pg_free = {}
            for _pp in range(4):
                tok_pg_free[_pp] = list(dep_c_ar2)
            tok_sT_free = [tok_q_ev[H - 2], tok_q_ev[H - 2]]
            tok_aev_dma = {}
            tok_lts_dma = {}
            tok_lev = None
            tok_aev_ev = [tok_q_ev[H - 1], tok_q_ev[H - 1]]
            tok_lacc_ev = [tok_q_ev[H - 1], tok_q_ev[H - 1]]

            for s in range(4):
                L = LIMS[s]
                tok_mask = E("sp", lambda e, s=s: e.dma_start(
                    mask_sb[:].rearrange("p (j f) -> p j f", j=8),
                    masks[s].transpose([1, 0, 2])),
                    deps=[tok_mask_use] + (dep_c_ar2 if s == 0 else []),
                    inc="qsp", inc_by=16)
                for hp in range(16):
                    h0 = hp * 2
                    Pq = hp % 2
                    gi = s * 16 + hp
                    tok_qtb[gi] = E("sp", lambda e, h0=h0, s=s, Pq=Pq: e.dma_start(
                        qtb[Pq].rearrange("p (h f) -> p h f", h=2),
                        qts[h0:h0 + 2, s].transpose([1, 0, 2])),
                        deps=[tok_q_spill[h0 + 1], tok_qk_last.get(gi - 2)] +
                             (dep_c_ar2 if gi < 2 else []),
                        inc="qsp", inc_by=16)
                    npairs = (L + 1) // 2
                    tok_av = {}
                    tok_masked = {}
                    for pr in range(npairs + 1):
                        if pr < npairs:
                            for kc in (2 * pr, 2 * pr + 1):
                                if kc >= L:
                                    continue
                                Ps = kc % 2
                                qk_tok = None
                                for h in range(2):
                                    qk_tok = E("pe",
                                        lambda e, kc=kc, h=h, Ps=Ps, Pq=Pq: e.matmul(
                                            sT[Ps][:, 512 * h:512 * h + 512],
                                            kT_sb[:, 128 * kc:128 * kc + 128],
                                            qtb[Pq][:, 512 * h:512 * h + 512],
                                            start=True, stop=True),
                                        deps=[tok_qtb[gi], tok_kT_done,
                                              tok_sT_free[Ps]],
                                        inc="spe" if h == 1 else None)
                                Pp = kc % 4
                                texp = E("act", lambda e, Pp=Pp, Ps=Ps: e.activation(
                                    pg[Pp][:], sT[Ps][:], AF.Exp, scale=SM_SCALE),
                                    deps=[qk_tok] + tok_pg_free.get(Pp, []),
                                    inc="sact" if (kc % 2 == 1 or kc == L - 1)
                                    else None)
                                if kc % 2 == 1 or kc == L - 1:
                                    tok_exp_tok[pr] = texp
                                    tok_sT_free[0] = texp
                                    tok_sT_free[1] = texp
                        if pr >= 1:
                            for kc in (2 * (pr - 1), 2 * (pr - 1) + 1):
                                if kc >= L:
                                    continue
                                Pp = kc % 4
                                if kc >= L - 8:
                                    j = kc - (L - 8)
                                    tmm = None
                                    for h in range(2):
                                        tmm = E("dve",
                                            lambda e, h=h, j=j, Pp=Pp: e.tensor_tensor(
                                                pg[Pp][:, 512 * h:512 * h + 512],
                                                pg[Pp][:, 512 * h:512 * h + 512],
                                                mask_sb[:, 512 * j:512 * j + 512],
                                                ALU.mult),
                                            deps=[tok_exp_tok[pr - 1], tok_mask],
                                            inc="sdve" if h == 1 else None)
                                    tok_masked[kc] = tmm
                                    tok_mask_use = tmm
                                dep_p = [tok_exp_tok[pr - 1], tok_masked.get(kc)]
                                for h in range(2):
                                    E("pe", lambda e, h=h, kc=kc, Pp=Pp, L=L: e.matmul(
                                        lacc[h], onescolb_sb[:],
                                        pg[Pp][:, 512 * h:512 * h + 512],
                                        start=(kc == 0), stop=(kc == L - 1)),
                                        deps=dep_p + [t_ocb, tok_lacc_ev[h]],
                                        inc=None)
                                for h in range(2):
                                    tok_av[(kc, h)] = E("pe",
                                        lambda e, h=h, kc=kc, Pp=Pp, L=L: e.matmul(
                                            accs_c[h],
                                            v_sb[:, 128 * kc:128 * kc + 128],
                                            pg[Pp][:, 512 * h:512 * h + 512],
                                            start=(kc == 0), stop=(kc == L - 1)),
                                        deps=dep_p + [tok_v_done, tok_aev_ev[h]],
                                        inc="spe")
                                tok_pg_free[Pp] = [tok_av[(kc, 1)]]
                    tok_qk_last[gi] = tok_exp_tok[npairs - 1]
                    for h in range(2):
                        E("dve", lambda e, h=h: e.tensor_copy(
                            aev[h], accs_c[h]),
                            deps=[tok_av[(L - 1, h)],
                                  tok_aev_dma.get((gi - 1, h))], inc=None)
                    for h in range(2):
                        tok_lev = E("dve", lambda e, h=h: e.tensor_copy(
                            lrow[:, 512 * h:512 * h + 512], lacc[h]),
                            deps=[tok_av[(L - 1, 1)], tok_lts_dma.get(gi - 1)],
                            inc="sdve" if h == 1 else None)
                    for h in range(2):
                        tok_aev_ev[h] = tok_lev
                        tok_lacc_ev[h] = tok_lev
                        tok_aev_dma[(gi, h)] = E("sp",
                            lambda e, h=h, h0=h0, s=s: e.dma_start(
                                ats[h0 + h, s], aev[h]),
                            deps=[tok_lev], inc="qsp", inc_by=16)
                    tok_lts_dma[gi] = E("sp", lambda e, h0=h0, s=s: e.dma_start(
                        lts[h0:h0 + 2, s].unsqueeze(0),
                        lrow[:].rearrange("p (s2 f) -> p s2 f", s2=2)),
                        deps=[tok_lev], inc="qsp", inc_by=16)

            t_bo = E("sp", lambda e: e.dma_start(bo_sb[:], bo), inc="qsp", inc_by=16)
            t_bf = E("sp", lambda e: e.dma_start(bf_sb[:], bfb), inc="qsp", inc_by=16)
            tok_bsum = E("dve", lambda e: e.tensor_tensor(
                bsum[:], bo_sb[:], bf_sb[:], ALU.add), deps=[t_bo, t_bf], inc="sdve")

            bar_pe = sch.bar("spe")
            bar_act = sch.bar("sact")
            bar_dve = sch.bar("sdve")
            bar_pool = sch.bar("spool")
            bar_qact_l = sch.qbar("qact")
            bar_qsp_l = sch.qbar("qsp")

            wcb = [AR2[:, 0:4096], AR2[:, 4096:8192], AR2[:, 8192:12288]]
            wB_t = [AR2[:, 12288:12416], AR2[:, 12416:12544]]
            tp_h = AR2[:, 12544:13568]
            linv_bc = AR2[:, 13568:14592]
            lrowD = AR3[0:1, 4096:5120]
            linv_row = AR2[0:1, 14592:15616]
            xst = [AR3[:, 0:1024], AR3[:, 1024:2048]]
            acts = [AR3[:, 2048:3072], AR3[:, 3072:4096]]
            accs = [ps_A[:, 0:1024], ps_A[:, 1024:2048], ps_B[:, 0:1024]]
            t_ps = ps_B[:, 1024:2048]

            tok_ath = {}
            tok_lath = {}
            tok_xrh = {}
            tok_wcb = {}
            tok_wBt = {}
            tok_acc_stop = {}
            tok_acts = {}
            tok_dadd = {}
            tok_out = {}
            tok_tp_ev = None
            tok_tp_stop = None

            dep_d_ps = [tok_lev, tok_sT_free[0]]

            for hh in range(2):
                dep_half = [tok_acc_stop.get((hh - 1, NCH - 1))] if hh > 0 else \
                    [tok_q_stop[H - 1], tok_xns_done]
                t_wA = E("pool", lambda e: e.dma_start(
                    AR2[:, 0:4096].rearrange("p (c f) -> p c f", c=NCH),
                    wA.transpose([1, 0, 2])),
                    deps=dep_half + ([bar_pe, bar_act, bar_dve] + bar_qact_l +
                                     bar_qsp_l if hh == 0 else []),
                    inc="qpool", inc_by=16)
                for h in range(H):
                    tok_ath[(hh, h)] = E("sp", lambda e, hh=hh, h=h: e.dma_start(
                        AR1[:, 1024 * h:1024 * h + 1024].rearrange(
                            "p (s f) -> p s f", s=2),
                        ats[h, 2 * hh:2 * hh + 2].transpose([1, 0, 2])),
                        deps=[tok_aev_dma[(2 * hh * 16 + h // 2, h % 2)],
                              tok_aev_dma[((2 * hh + 1) * 16 + h // 2, h % 2)],
                              tok_lath.get((hh - 1, h))] +
                             dep_half,
                        inc="qsp", inc_by=16)
                for d in range(NCH):
                    tok_xrh[(hh, d)] = E("sp", lambda e, hh=hh, d=d: e.dma_start(
                        AR1[:, 32768 + 1024 * d:32768 + 1024 * d + 1024].rearrange(
                            "p (s f) -> p s f", s=2),
                        xns[d, 2 * hh:2 * hh + 2].transpose([1, 0, 2])),
                        deps=[tok_xns_done] + dep_half, inc="qsp", inc_by=16)

                tok_expl = None
                tok_bc_ev = dep_d_ps
                for h in range(H):
                    t_lr = E("sp", lambda e, hh=hh, h=h: e.dma_start(
                        lrowD.rearrange("p (s f) -> p s f", s=2),
                        lts[h, 2 * hh:2 * hh + 2].unsqueeze(0)),
                        deps=[tok_lts_dma[(2 * hh) * 16 + h // 2],
                              tok_lts_dma[(2 * hh + 1) * 16 + h // 2],
                              tok_expl] + dep_half,
                        inc="qsp", inc_by=16)
                    tok_ln_l = E("act", lambda e: e.activation(
                        lrowD, lrowD, AF.Ln),
                        deps=[t_lr], inc=None)
                    tok_expl = E("act", lambda e: e.activation(
                        linv_row, lrowD, AF.Exp, scale=-1.0),
                        deps=[], inc="sact")
                    tok_bc = None
                    for tb in range(2):
                        tok_bc = E("pe", lambda e, tb=tb: e.matmul(
                            t_ps[:, 512 * tb:512 * tb + 512],
                            onesrowb_sb[:],
                            linv_row[:, 512 * tb:512 * tb + 512],
                            start=True, stop=True),
                            deps=[tok_expl, t_orb] + tok_bc_ev + dep_d_ps,
                            inc="spe" if tb == 1 else None)
                    tok_ev = E("dve", lambda e: e.tensor_copy(
                        linv_bc[:], t_ps[:]),
                        deps=[tok_bc], inc=None)
                    tok_lath[(hh, h)] = E("dve", lambda e, h=h: e.tensor_tensor(
                        AR1[:, 1024 * h:1024 * h + 1024],
                        AR1[:, 1024 * h:1024 * h + 1024], linv_bc[:], ALU.mult),
                        deps=[tok_ath[(hh, h)]], inc="sdve")
                    tok_bc_ev = [tok_lath[(hh, h)]]

                for d in range(NCH):
                    for tb in range(2):
                        tok_tp_stop = E("pe", lambda e, d=d, tb=tb: e.matmul(
                            t_ps[:, 512 * tb:512 * tb + 512],
                            AR2[:, 128 * d:128 * d + 128],
                            AR1[:, 32768 + 1024 * d + 512 * tb:
                                32768 + 1024 * d + 512 * tb + 512],
                            start=(d == 0), stop=(d == NCH - 1)),
                            deps=[t_wA, tok_xrh[(hh, d)], tok_tp_ev,
                                  tok_bc_ev[0]] + dep_d_ps,
                            inc="spe" if (d == NCH - 1 and tb == 1) else None)
                tok_tp_ev = E("dve", lambda e: e.tensor_scalar(
                    tp_h[:], t_ps[:], rscale[:], None, ALU.mult),
                    deps=[tok_tp_stop, tok_rscale,
                          *(bar_qact_l if hh == 0 else [None])], inc="sdve")

                for o in range(NCH):
                    u0 = (2 * o) % 3
                    u1 = (2 * o + 1) % 3
                    dep_w0 = [tok_acc_stop.get((hh, o - 2))]
                    if hh == 0 and o == 0:
                        dep_w0 += [tok_tp_stop]
                    if hh == 1 and o == 0:
                        dep_w0 += [tok_tp_stop, tok_acc_stop.get((0, NCH - 1))]
                    if hh == 0 and o == 1:
                        dep_w0 += bar_qact_l + [bar_dve]
                    tok_wcb[(hh, o, 0)] = E("pool", lambda e, o=o, u0=u0: e.dma_start(
                        wcb[u0].rearrange("p (c f) -> p c f", c=NCH),
                        wo[o].transpose([1, 0, 2])),
                        deps=dep_w0, inc="qpool", inc_by=16)
                    tok_wcb[(hh, o, 1)] = E("pool", lambda e, o=o, u1=u1: e.dma_start(
                        wcb[u1].rearrange("p (c f) -> p c f", c=NCH),
                        wW[o].transpose([1, 0, 2])),
                        deps=[tok_acc_stop.get((hh, o - 1)),
                              (tok_acc_stop.get((0, NCH - 1))
                               if (hh == 1 and o == 0) else None)] +
                             ([bar_pe, bar_dve, bar_pool]
                              if (hh == 0 and o == 0) else []),
                        inc="qpool", inc_by=16)
                    tok_wBt[(hh, o)] = E("pool", lambda e, o=o: e.dma_start(
                        wB_t[o % 2], wB[o]),
                        deps=[tok_acc_stop.get((hh, o - 2)),
                              *(bar_qact_l if (hh == 0 and o <= 1) else [None]),
                              (tok_acc_stop.get((0, 30 + o))
                               if (hh == 1 and o < 2) else None)],
                        inc="qpool", inc_by=16)
                    acc = accs[o % 3]
                    tok_mm = None
                    for h in range(H):
                        for tb in range(2):
                            tok_mm = E("pe",
                                lambda e, h=h, tb=tb, u0=u0, acc=acc: e.matmul(
                                    acc[:, 512 * tb:512 * tb + 512],
                                    wcb[u0][:, 128 * h:128 * h + 128],
                                    AR1[:, 1024 * h + 512 * tb:
                                        1024 * h + 512 * tb + 512],
                                    start=(h == 0), stop=False),
                                deps=[tok_wcb[(hh, o, 0)], tok_lath[(hh, h)],
                                      tok_acts.get((hh, o - 3)),
                                      (tok_acts.get((hh - 1, [30, 31, 29][o]))
                                       if (hh > 0 and o < 3) else None)],
                                inc=None)
                    for d in range(NCH):
                        for tb in range(2):
                            tok_mm = E("pe",
                                lambda e, d=d, tb=tb, u1=u1, acc=acc: e.matmul(
                                    acc[:, 512 * tb:512 * tb + 512],
                                    wcb[u1][:, 128 * d:128 * d + 128],
                                    AR1[:, 32768 + 1024 * d + 512 * tb:
                                        32768 + 1024 * d + 512 * tb + 512],
                                    start=False, stop=False),
                                deps=[tok_wcb[(hh, o, 1)], tok_xrh[(hh, d)]],
                                inc=None)
                    for tb in range(2):
                        tok_mm = E("pe", lambda e, tb=tb, o=o, acc=acc: e.matmul(
                            acc[:, 512 * tb:512 * tb + 512], wB_t[o % 2][:],
                            tp_h[:, 512 * tb:512 * tb + 512],
                            start=False, stop=True),
                            deps=[tok_wBt[(hh, o)], tok_tp_ev],
                            inc="spe" if tb == 1 else None)
                    tok_acc_stop[(hh, o)] = tok_mm
                    Po = o % 2
                    prev_out = tok_out.get((hh, o - 2)) if o >= 2 else (
                        tok_out.get((hh - 1, NCH - 2 + o)) if hh > 0 else None)
                    tok_acts[(hh, o)] = E("act",
                        lambda e, o=o, Po=Po, acc=acc: e.activation(
                            acts[Po], acc[:], AF.Identity, bias=bsum[:, o:o + 1]),
                        deps=[tok_mm, tok_bsum, prev_out,
                              (bar_dve if (hh == 0 and o < 2) else None)],
                        inc="sact")
                    prev_xst = tok_dadd.get((hh, o - 2)) if o >= 2 else (
                        tok_dadd.get((hh - 1, NCH - 2 + o)) if hh > 0 else
                        (bar_pe if o < 2 else None))
                    t_xs = E("sp", lambda e, hh=hh, o=o, Po=Po: e.dma_start(
                        xst[Po].rearrange("p (s f) -> p s f", s=2),
                        xT_own[2 * hh:2 * hh + 2, o].transpose([1, 0, 2])),
                        deps=[prev_xst,
                              (sch.bar("spe") if (hh == 0 and o < 2) else None)],
                        inc="qsp", inc_by=16)
                    tok_dadd[(hh, o)] = E("dve", lambda e, Po=Po: e.tensor_tensor(
                        acts[Po], acts[Po], xst[Po], ALU.add),
                        deps=[tok_acts[(hh, o)], t_xs], inc="sdve")
                    tok_out[(hh, o)] = E("act", lambda e, hh=hh, o=o, Po=Po: e.dma_start(
                        out_t[o, 2 * hh:2 * hh + 2].transpose([1, 0, 2]),
                        acts[Po].rearrange("p (s f) -> p s f", s=2)),
                        deps=[tok_dadd[(hh, o)]], inc="qact", inc_by=16)

            fin = sch.qbar("qact") + sch.qbar("qsp") + sch.qbar("qpool")
            E("sp", lambda e: None, deps=fin)

        _emit_phases()

        _flush_op = {
            "act": lambda e: e.copy(scr_act[:], scr_act[:]),
            "dve": lambda e: e.tensor_copy(scr_dve[:], scr_dve[:]),
            "pool": lambda e: e.tensor_copy(scr_pool[:], scr_pool[:]),
        }
        _flush_sems = ("sact", "sdve", "spool")

        def run_stream(name, eng):
            for wl, fn, inc, inc_by in sch.streams[name]:
                for snm, v in wl:
                    eng.wait_ge(sems[snm], v)
                ins = fn(eng)
                if inc is not None:
                    if name in _flush_op and inc in _flush_sems:
                        eng.drain(fusable=False)
                        _flush_op[name](eng).then_inc(sems[inc], inc_by)
                    else:
                        ins.then_inc(sems[inc], inc_by)

        @block.tensor
        def _(eng):
            run_stream("pe", eng)

        @block.scalar
        def _(eng):
            run_stream("act", eng)

        @block.vector
        def _(eng):
            run_stream("dve", eng)

        @block.gpsimd
        def _(eng):
            run_stream("pool", eng)

        @block.sync
        def _(eng):
            run_stream("sp", eng)

    return nc



_NC_CACHE = {}


def _get_program():
    if "nc" not in _NC_CACHE:
        _NC_CACHE["nc"] = build_program()
    return _NC_CACHE["nc"]


def _pack_weights(inp):
    f32 = np.float32
    wq_w = np.asarray(inp["L0_attn_wq_w"], f32)
    wk_w = np.asarray(inp["L0_attn_wk_w"], f32)
    wv_w = np.asarray(inp["L0_attn_wv_w"], f32)
    wo_w = np.asarray(inp["L0_attn_wo_w"], f32)
    W = np.asarray(inp["L0_ffn_W"], f32)
    A = np.asarray(inp["L0_ffn_A"], f32)
    Bm = np.asarray(inp["L0_ffn_B"], f32)
    rW = np.asarray(inp["L0_ffn_router_W"], f32)

    def colblocks(w):
        wt = np.ascontiguousarray(w.T)
        return np.ascontiguousarray(
            wt.reshape(NCH, 128, NCH, 128).transpose(2, 0, 1, 3))

    def dblocks(w):
        wt = np.ascontiguousarray(w.T)
        return np.ascontiguousarray(wt.reshape(NCH, 128, w.shape[0]))

    d = {}
    d["wq"] = colblocks(wq_w)
    d["wo"] = colblocks(wo_w)
    d["wW"] = colblocks(W)
    d["wk"] = dblocks(wk_w)
    d["wv"] = dblocks(wv_w)
    A_all = A.reshape(128, D)
    d["wA"] = dblocks(A_all)
    B_all = Bm.transpose(0, 2, 1).reshape(128, D)
    d["wB"] = np.ascontiguousarray(
        B_all.reshape(128, NCH, 128).transpose(1, 0, 2))
    d["wr"] = dblocks(rW)
    d["bq"] = np.ascontiguousarray(
        np.asarray(inp["L0_attn_wq_b"], f32).reshape(NCH, 128).T)
    d["bk"] = np.asarray(inp["L0_attn_wk_b"], f32).reshape(128, 1)
    d["bv"] = np.asarray(inp["L0_attn_wv_b"], f32).reshape(128, 1)
    d["bo"] = np.ascontiguousarray(
        np.asarray(inp["L0_attn_wo_b"], f32).reshape(NCH, 128).T)
    d["bfb"] = np.ascontiguousarray(
        np.asarray(inp["L0_ffn_b"], f32).reshape(NCH, 128).T)
    d["br"] = np.asarray(inp["L0_ffn_router_b"], f32).reshape(1, 8)
    d["identf"] = np.eye(128, dtype=f32)
    d["onescol_f"] = np.ones((128, 1), f32)
    d["onescol_bf"] = np.ones((128, 1), ml_dtypes.bfloat16)
    d["onesrow_f"] = np.ones((1, 128), f32)
    d["onesrow_b"] = np.ones((1, 128), ml_dtypes.bfloat16)
    d["ones1_b"] = np.ones((1, 1), ml_dtypes.bfloat16)
    sel = np.zeros((8, 128), f32)
    for e_ in range(8):
        sel[e_, e_ * 16:(e_ + 1) * 16] = 2.0
    d["sel_b"] = sel.astype(ml_dtypes.bfloat16)
    return d


def _build_masks(own_supers):
    m = np.zeros((4, 8, 128, 512), np.float32)
    tri = np.triu(np.ones((128, 128), np.float32))
    for s, g in enumerate(own_supers):
        L = LIMS[s]
        for j in range(8):
            kc = L - 8 + j
            for qt in range(4):
                gq = 4 * g + qt
                if kc < gq:
                    m[s, j, :, qt * 128:(qt + 1) * 128] = 1.0
                elif kc == gq:
                    m[s, j, :, qt * 128:(qt + 1) * 128] = tri
    return m.astype(ml_dtypes.bfloat16)


def kernel(**inputs):
    x = np.asarray(inputs["L0_attn_x"], np.float32)
    wd = _pack_weights(inputs)
    masks_by_p = {p: _build_masks(OWN_SUPER[p]) for p in (0, 1)}

    in_maps = []
    for c in range(8):
        b, p = c // 2, c % 2
        owns = OWN_SUPER[p]
        xb = np.ascontiguousarray(x[b])
        xTb = np.ascontiguousarray(xb.T)
        xT_blocks = np.ascontiguousarray(
            xTb.reshape(NCH, 128, NW, 512).transpose(2, 0, 1, 3))
        own_rows = np.concatenate(
            [np.arange(512 * g, 512 * (g + 1)) for g in owns])
        xtok_own = np.ascontiguousarray(xb[own_rows])
        xT_own = np.ascontiguousarray(xT_blocks[owns])
        m = {"x_glob": xb, "xtok_own": xtok_own, "xT_glob": xT_blocks,
             "xT_own": xT_own, "masks": masks_by_p[p]}
        m.update(wd)
        in_maps.append(m)

    nc = _get_program()
    res = run_bass_kernel_spmd(nc, in_maps, core_ids=list(range(8)))

    out = np.empty((4, 4096, 4096), np.float32)
    for c in range(8):
        b, p = c // 2, c % 2
        owns = OWN_SUPER[p]
        r = res.results[c]["out"]
        arr = np.ascontiguousarray(r.transpose(1, 3, 0, 2)).reshape(4, 512, 4096)
        for wi, g in enumerate(owns):
            out[b, 512 * g:512 * (g + 1), :] = arr[wi]
    return out


if __name__ == "__main__":
    import reference
    inp = reference.setup_inputs()
    o = kernel(**{k: np.asarray(v) for k, v in inp.items()})
    ref = np.asarray(reference.reference(**inp))
    err = np.abs(o - ref)
    rel = np.sqrt(((o - ref) ** 2).mean() / (ref ** 2).mean())
    print("max abs err:", err.max(), "rel l2:", rel)


# version 04
# speedup vs baseline: 1.2235x; 1.1348x over previous
import numpy as np
import ml_dtypes
import concourse.bass as bass
import concourse.mybir as mybir
from concourse.bass_utils import run_bass_kernel_spmd

F32 = mybir.dt.float32
BF16 = mybir.dt.bfloat16
AF = mybir.ActivationFunctionType
ALU = mybir.AluOpType
AX = mybir.AxisListType

D = 4096
T = 4096
HD = 128
H = 32
NCH = 32
NW = 8
OWN_W = 4
OT = 2048
LIMS = [8, 16, 24, 32]
OWN_SUPER = {0: [0, 3, 4, 7], 1: [1, 2, 5, 6]}
EPS = 1e-5
SM_SCALE = float(1.0 / np.sqrt(128.0))
Q_LO, Q_HI = 611, 4095
NTOK_R = Q_HI - Q_LO


class Sched:

    ENGINES = ("pe", "act", "dve", "pool", "sp")

    NQROT = 8

    def __init__(self):
        self.streams = {e: [] for e in self.ENGINES}
        self.cnt = {}
        self.lastw = {}
        self.qrot = {}

    def emit(self, eng, fn, deps=(), inc=None, inc_by=1):
        wl = []
        for tok in deps:
            if tok is None:
                continue
            s, v = tok
            if v > self.lastw.get((eng, s), 0):
                wl.append((s, v))
                self.lastw[(eng, s)] = v
        tok = None
        if inc is not None:
            if inc in ("qsp", "qact", "qpool"):
                r = self.qrot.get(inc, 0)
                self.qrot[inc] = (r + 1) % self.NQROT
                inc = f"{inc}{r}"
            self.cnt[inc] = self.cnt.get(inc, 0) + inc_by
            tok = (inc, self.cnt[inc])
        self.streams[eng].append((wl, fn, inc, inc_by))
        return tok

    def qbar(self, prefix):
        return [(f"{prefix}{r}", self.cnt.get(f"{prefix}{r}", 0))
                for r in range(self.NQROT)]

    def bar(self, name):
        return (name, self.cnt.get(name, 0))


class _Stop(Exception):
    pass


def build_program(upto="d"):
    nc = bass.Bass("TRN2", target_bir_lowering=False)

    def din(name, shape, dt=F32):
        return nc.dram_tensor(name, list(shape), dt, kind="ExternalInput").ap()

    x_glob = din("x_glob", (T, D))
    xtok_own = din("xtok_own", (OT, D))
    xT_glob = din("xT_glob", (NW, NCH, 128, 512))
    xT_own = din("xT_own", (OWN_W, NCH, 128, 512))
    wq = din("wq", (NCH, NCH, 128, 128))
    wk = din("wk", (NCH, 128, 128))
    wv = din("wv", (NCH, 128, 128))
    wo = din("wo", (NCH, NCH, 128, 128))
    wW = din("wW", (NCH, NCH, 128, 128))
    wA = din("wA", (NCH, 128, 128))
    wB = din("wB", (NCH, 128, 128))
    wr = din("wr", (NCH, 128, 8))
    bq = din("bq", (128, 32))
    bk = din("bk", (128, 1))
    bv = din("bv", (128, 1))
    bo = din("bo", (128, 32))
    bfb = din("bfb", (128, 32))
    br = din("br", (1, 8))
    identf = din("identf", (128, 128))
    onescol_f = din("onescol_f", (128, 1))
    onescol_bf = din("onescol_bf", (128, 1), BF16)
    onesrow_f = din("onesrow_f", (1, 128))
    onesrow_b = din("onesrow_b", (1, 128), BF16)
    ones1_b = din("ones1_b", (1, 1), BF16)
    sel_b = din("sel_b", (8, 128), BF16)
    masks = din("masks", (4, 8, 128, 512), BF16)

    out_t = nc.dram_tensor("out", [NCH, OWN_W, 128, 512], F32,
                           kind="ExternalOutput").ap()
    dbg = {}
    if upto != "d":
        if upto == "a0":
            dbg["d_own"] = nc.dram_tensor("d_own", [128, 4096], F32, kind="ExternalOutput").ap()
            dbg["d_glob"] = nc.dram_tensor("d_glob", [128, 8192], F32, kind="ExternalOutput").ap()
        if upto == "a1":
            dbg["d_kT"] = nc.dram_tensor("d_kT", [128, 4096], BF16, kind="ExternalOutput").ap()
            dbg["d_v"] = nc.dram_tensor("d_v", [128, 4096], BF16, kind="ExternalOutput").ap()
            dbg["d_qs"] = nc.dram_tensor("d_qs", [128, 32], F32, kind="ExternalOutput").ap()
        if upto == "rt":
            dbg["d_rsc"] = nc.dram_tensor("d_rsc", [128, 1], F32, kind="ExternalOutput").ap()
            dbg["d_rsm"] = nc.dram_tensor("d_rsm", [1, 64], F32, kind="ExternalOutput").ap()
        if upto == "b":
            dbg["d_qts"] = nc.dram_tensor("d_qts", [H, 4, 128, 512], BF16, kind="ExternalOutput").ap()
            dbg["d_xns"] = nc.dram_tensor("d_xns", [NCH, OWN_W, 128, 512], BF16, kind="ExternalOutput").ap()
        if upto == "c":
            dbg["d_ats"] = nc.dram_tensor("d_ats", [H, 4, 128, 512], BF16, kind="ExternalOutput").ap()
            dbg["d_pacc"] = nc.dram_tensor("d_pacc", [128, 2048], F32, kind="ExternalOutput").ap()
            dbg["d_linv"] = nc.dram_tensor("d_linv", [128, 512], F32, kind="ExternalOutput").ap()
            dbg["d_pg"] = nc.dram_tensor("d_pg", [128, 2048], BF16, kind="ExternalOutput").ap()
    xns = nc.dram_tensor("xns", [NCH, OWN_W, 128, 512], BF16, kind="Internal").ap()
    qts = nc.dram_tensor("qts", [H, 4, 128, 512], BF16, kind="Internal").ap()
    ats = nc.dram_tensor("ats", [H, 4, 128, 512], BF16, kind="Internal").ap()
    lts = nc.dram_tensor("lts", [H, 4, 512], F32, kind="Internal").ap()

    sch = Sched()
    E = sch.emit

    import contextlib
    with contextlib.ExitStack() as _st:
        AR1_t = _st.enter_context(nc.sbuf_tensor("AR1", [128, 65536], BF16))
        AR2_t = _st.enter_context(nc.sbuf_tensor("AR2", [128, 16384], BF16))
        AR3_t = _st.enter_context(nc.sbuf_tensor("AR3", [128, 5120], F32))
        kT_sb = _st.enter_context(nc.sbuf_tensor("kT_sb", [128, 4096], BF16))
        v_sb = _st.enter_context(nc.sbuf_tensor("v_sb", [128, 4096], BF16))
        mask_sb = _st.enter_context(nc.sbuf_tensor("mask_sb", [128, 4096], BF16))
        stats_sb = _st.enter_context(nc.sbuf_tensor("stats_sb", [128, 48], F32))
        aggr_sb = _st.enter_context(nc.sbuf_tensor("aggr_sb", [128, 2], F32))
        lncol = _st.enter_context(nc.sbuf_tensor("lncol", [128, 1], F32))
        rcol = _st.enter_context(nc.sbuf_tensor("rcol", [128, 1], F32))
        rmcol = _st.enter_context(nc.sbuf_tensor("rmcol", [128, 1], F32))
        qs_sb = _st.enter_context(nc.sbuf_tensor("qs_sb", [128, 32], F32))
        qtmp = _st.enter_context(nc.sbuf_tensor("qtmp", [128, 4], F32))
        brsb = _st.enter_context(nc.sbuf_tensor("brsb", [1, 8], F32))
        rsm = _st.enter_context(nc.sbuf_tensor("rsm", [1, 64], F32))
        routing_b = _st.enter_context(nc.sbuf_tensor("routing_b", [1, 8], BF16))
        rtcol_b = _st.enter_context(nc.sbuf_tensor("rtcol_b", [8, 1], BF16))
        rscale = _st.enter_context(nc.sbuf_tensor("rscale", [128, 1], F32))
        bq_sb = _st.enter_context(nc.sbuf_tensor("bq_sb", [128, 32], F32))
        bk_sb = _st.enter_context(nc.sbuf_tensor("bk_sb", [128, 1], F32))
        bv_sb = _st.enter_context(nc.sbuf_tensor("bv_sb", [128, 1], F32))
        bo_sb = _st.enter_context(nc.sbuf_tensor("bo_sb", [128, 32], F32))
        bf_sb = _st.enter_context(nc.sbuf_tensor("bf_sb", [128, 32], F32))
        bsum = _st.enter_context(nc.sbuf_tensor("bsum", [128, 32], F32))
        identf_sb = _st.enter_context(nc.sbuf_tensor("identf_sb", [128, 128], F32))
        onescol_sb = _st.enter_context(nc.sbuf_tensor("onescol_sb", [128, 1], F32))
        onescolb_sb = _st.enter_context(nc.sbuf_tensor("onescolb_sb", [128, 1], BF16))
        onesrow_sb = _st.enter_context(nc.sbuf_tensor("onesrow_sb", [1, 128], F32))
        onesrowb_sb = _st.enter_context(nc.sbuf_tensor("onesrowb_sb", [1, 128], BF16))
        ones1_sb = _st.enter_context(nc.sbuf_tensor("ones1_sb", [1, 1], BF16))
        sel_sb = _st.enter_context(nc.sbuf_tensor("sel_sb", [8, 128], BF16))
        epscol = _st.enter_context(nc.sbuf_tensor("epscol", [128, 1], F32))
        scr_act = _st.enter_context(nc.sbuf_tensor("scr_act", [1, 1], F32))
        scr_dve = _st.enter_context(nc.sbuf_tensor("scr_dve", [1, 1], F32))
        scr_pool = _st.enter_context(nc.sbuf_tensor("scr_pool", [1, 1], F32))
        ps_A = _st.enter_context(nc.psum_tensor([128, 2048], F32))
        ps_B = _st.enter_context(nc.psum_tensor([128, 2048], F32))
        qsems = {}
        for _qp in ("qsp", "qact", "qpool"):
            for _r in range(Sched.NQROT):
                qsems[f"{_qp}{_r}"] = _st.enter_context(
                    nc.semaphore(f"{_qp}{_r}"))
        sem_spe = _st.enter_context(nc.semaphore("spe"))
        sem_sact = _st.enter_context(nc.semaphore("sact"))
        sem_sdve = _st.enter_context(nc.semaphore("sdve"))
        sem_spool = _st.enter_context(nc.semaphore("spool"))
        block = _st.enter_context(nc.Block())
        sems = {"spe": sem_spe, "sact": sem_sact, "sdve": sem_sdve,
                "spool": sem_spool}
        sems.update(qsems)

        _phases_done = False
        AR1 = AR1_t[:]
        AR2 = AR2_t[:]
        AR3 = AR3_t[:]
        AR1f = AR1.bitcast(F32)
        AR2f = AR2.bitcast(F32)
        rowb = AR3_t[0:1, 4096:5120]
        lnl = AR2f[0:1, 5120:6144]
        wr_sb = AR2f[:, 6144:6400]

        def _emit_phases():
            tok_eps = E("dve", lambda e: e.memset(epscol[:], EPS), inc="sdve")
            t_idf = E("sp", lambda e: e.dma_start(identf_sb[:], identf), inc="qsp", inc_by=16)
            t_oc = E("sp", lambda e: e.dma_start(onescol_sb[:], onescol_f), inc="qsp", inc_by=16)
            t_ocb = E("sp", lambda e: e.dma_start(onescolb_sb[:], onescol_bf), inc="qsp", inc_by=16)
            t_or = E("sp", lambda e: e.dma_start(onesrow_sb[:], onesrow_f), inc="qsp", inc_by=16)
            t_orb = E("sp", lambda e: e.dma_start(onesrowb_sb[:], onesrow_b), inc="qsp", inc_by=16)
            t_o1 = E("sp", lambda e: e.dma_start(ones1_sb[:], ones1_b), inc="qsp", inc_by=16)
            t_sel = E("sp", lambda e: e.dma_start(sel_sb[:], sel_b), inc="qsp", inc_by=16)
            t_bk = E("sp", lambda e: e.dma_start(bk_sb[:], bk), inc="qsp", inc_by=16)
            t_bv = E("sp", lambda e: e.dma_start(bv_sb[:], bv), inc="qsp", inc_by=16)
            t_bq = E("sp", lambda e: e.dma_start(bq_sb[:], bq), inc="qsp", inc_by=16)
            t_wr = E("sp", lambda e: e.dma_start(
                wr_sb[:].rearrange("p (c e) -> p c e", c=NCH),
                wr.transpose([1, 0, 2])), inc="qsp", inc_by=16)
            t_br = E("sp", lambda e: e.dma_start(brsb[:], br), inc="qsp", inc_by=16)

            def a0_pass(src_ap, ntiles, rbc, rmbc, war):
                src_v = src_ap.rearrange("(n p) d -> n p d", p=128)
                stage = [AR1f[:, 0:4096], AR1f[:, 4096:8192]]
                t_stage_rd = [war.get("stage0"), war.get("stage1")]
                t_row_ev1 = war.get("row1")
                t_row_ev2 = war.get("row2")
                t_bc_ev_r = war.get("bcr")
                t_bc_ev_m = war.get("bcm")
                t_rc_read = war.get("rc")
                t_rm_read = war.get("rm")
                rc_mm = [None] * 4
                rm_mm = [None] * 4
                for tt in range(ntiles):
                    P = tt % 2
                    g, j = tt // 4, tt % 4
                    st = stage[P]
                    t_x = E("sp", lambda e, st=st, tt=tt: e.dma_start(st, src_v[tt]),
                            deps=[t_stage_rd[P]], inc="qsp", inc_by=16)
                    tok_bn = None
                    for i in range(8):
                        tok_bn = E("dve", lambda e, st=st, i=i: e.bn_stats(
                            stats_sb[:, 6 * i:6 * i + 6], st[:, 512 * i:512 * i + 512]),
                            deps=[t_x], inc="sdve" if i == 7 else None)
                    t_stage_rd[P] = tok_bn
                    tok_ag = E("dve", lambda e: e.bn_aggr(aggr_sb[:], stats_sb[:]),
                               deps=[tok_bn], inc="sdve")
                    tok_ln = E("act", lambda e: e.activation(
                        lncol[:], aggr_sb[:, 1:2], AF.Ln, bias=epscol[:]),
                        deps=[tok_ag, tok_eps], inc="sact")
                    tok_r = E("act", lambda e: e.activation(
                        rcol[:], lncol[:], AF.Exp, scale=-0.5),
                        deps=[tok_ln, t_rc_read], inc="sact")
                    tok_rm = E("dve", lambda e: e.tensor_tensor(
                        rmcol[:], aggr_sb[:, 0:1], rcol[:], ALU.mult),
                        deps=[tok_r, t_rm_read], inc="sdve")
                    rc_mm[j] = E("pe", lambda e, j=j: e.matmul(
                        ps_A[0:1, 128 * j:128 * j + 128], rcol[:], identf_sb[:],
                        start=True, stop=True),
                        deps=[tok_r, t_idf, t_row_ev1], inc="spe")
                    rm_mm[j] = E("pe", lambda e, j=j: e.matmul(
                        ps_A[0:1, 512 + 128 * j:512 + 128 * j + 128], rmcol[:],
                        identf_sb[:], start=True, stop=True),
                        deps=[tok_rm, t_row_ev2], inc="spe")
                    t_rc_read = rc_mm[j]
                    t_rm_read = rm_mm[j]
                    if j == 3:
                        t_row_ev1 = E("act", lambda e: e.activation(
                            rowb[:, 0:512], ps_A[0:1, 0:512], AF.Copy),
                            deps=rc_mm, inc="sact")
                        t_row_ev2 = E("act", lambda e: e.activation(
                            rowb[:, 512:1024], ps_A[0:1, 512:1024], AF.Copy),
                            deps=rm_mm, inc="sact")
                        bc1 = E("pe", lambda e: e.matmul(
                            ps_A[:, 1024:1536], onesrow_sb[:], rowb[:, 0:512],
                            start=True, stop=True),
                            deps=[t_row_ev1, t_or, t_bc_ev_r], inc="spe")
                        bc2 = E("pe", lambda e: e.matmul(
                            ps_A[:, 1536:2048], onesrow_sb[:], rowb[:, 512:1024],
                            start=True, stop=True),
                            deps=[t_row_ev2, t_bc_ev_m], inc="spe")
                        t_bc_ev_r = E("act", lambda e, g=g: e.activation(
                            rbc[:, 512 * g:512 * g + 512], ps_A[:, 1024:1536],
                            AF.Copy), deps=[bc1], inc="sact")
                        t_bc_ev_m = E("act", lambda e, g=g: e.activation(
                            rmbc[:, 512 * g:512 * g + 512], ps_A[:, 1536:2048],
                            AF.Copy), deps=[bc2], inc="sact")
                return {"stage0": t_stage_rd[0], "stage1": t_stage_rd[1],
                        "row1": t_row_ev1, "row2": t_row_ev2,
                        "bcr": t_bc_ev_r, "bcm": t_bc_ev_m,
                        "rc": t_rc_read, "rm": t_rm_read}

            r_bc_own = AR3[:, 0:2048]
            rm_bc_own = AR3[:, 2048:4096]
            w0 = a0_pass(xtok_own, 16, r_bc_own, rm_bc_own, {})
            tok_own_r, tok_own_m = w0["bcr"], w0["bcm"]
            r_bc_g = AR1f[:, 20480:24576]
            rm_bc_g = AR1f[:, 24576:28672]
            w1 = a0_pass(x_glob, 32, r_bc_g, rm_bc_g, w0)
            tok_g_r, tok_g_m = w1["bcr"], w1["bcm"]
            if upto == "a0":
                t1 = E("sp", lambda e: e.dma_start(dbg["d_own"], AR3[:, 0:4096]),
                       deps=[tok_own_r, tok_own_m], inc="qsp", inc_by=16)
                t2 = E("sp", lambda e: e.dma_start(dbg["d_glob"], AR1f[:, 20480:28672]),
                       deps=[tok_g_r, tok_g_m], inc="qsp", inc_by=16)
                E("sp", lambda e: None, deps=[t1, t2])
                return

            t_wk = E("pool", lambda e: e.dma_start(
                AR2[:, 0:4096].rearrange("p (c f) -> p c f", c=NCH),
                wk.transpose([1, 0, 2])), inc="qpool", inc_by=16)
            t_wv = E("pool", lambda e: e.dma_start(
                AR2[:, 4096:8192].rearrange("p (c f) -> p c f", c=NCH),
                wv.transpose([1, 0, 2])), inc="qpool", inc_by=16)

            xn_w = [AR1[:, 0:16384], AR1[:, 16384:32768]]
            xstage = [AR1f[:, 16384 + 512 * k:16384 + 512 * k + 512] for k in range(6)]
            tmpf = [AR1f[:, 28672:29184], AR1f[:, 29184:29696]]
            vtmp = AR1f[:, 29696:30208]

            tok_blk = {}
            tok_kv_stop = {}
            tok_qsum_last = None
            tok_kev = None
            tok_vev = None
            tok_vtr_ev = [None, None]
            tok_xstage_rd = [w1["stage0"], w1["stage1"], None, None, None, None]
            blk_order = []
            tok_qs_wr = None
            tok_v_done = None

            tok_qsum_w = {}
            for w in range(NW):
                P = w % 2
                dep_war = [tok_kv_stop.get(w - 2), tok_qsum_w.get(w - 2)]
                pend = {"dve": [], "pool": []}
                for d in range(NCH):
                    i_lin = w * NCH + d
                    k3 = i_lin % 6
                    eng = "dve" if d % 2 == 0 else "pool"
                    war6 = tok_blk.get(blk_order[i_lin - 6]) if i_lin >= 6 else \
                        tok_xstage_rd[k3]
                    blk_order.append((w, d))
                    t_x = E("sp", lambda e, w=w, d=d, k3=k3: e.dma_start(
                        xstage[k3], xT_glob[w, d]),
                        deps=[war6], inc="qsp", inc_by=16)
                    tp = d % 2
                    E(eng, lambda e, w=w, tp=tp, k3=k3: e.tensor_tensor(
                        tmpf[tp], xstage[k3], r_bc_g[:, 512 * w:512 * w + 512],
                        ALU.mult),
                        deps=[t_x, tok_g_r] + dep_war, inc=None)
                    do_inc = (d >= NCH - 2) or (d % 4 in (2, 3))
                    tok = E(eng, lambda e, w=w, d=d, P=P, tp=tp: e.tensor_tensor(
                        xn_w[P][:, 512 * d:512 * d + 512], tmpf[tp],
                        rm_bc_g[:, 512 * w:512 * w + 512], ALU.subtract),
                        deps=[tok_g_m],
                        inc=("sdve" if eng == "dve" else "spool") if do_inc else None)
                    if do_inc:
                        for dd in pend[eng] + [d]:
                            tok_blk[(w, dd)] = tok
                        pend[eng] = []
                    else:
                        pend[eng].append(d)
                tok_k_mm = None
                for d in range(NCH):
                    tok_k_mm = E("pe", lambda e, d=d, P=P: e.matmul(
                        ps_B[:, 0:512], AR2[:, 128 * d:128 * d + 128],
                        xn_w[P][:, 512 * d:512 * d + 512],
                        start=(d == 0), stop=(d == NCH - 1)),
                        deps=[tok_blk[(w, d)], t_wk, tok_kev],
                        inc="spe" if d == NCH - 1 else None)
                tok_kev = E("act", lambda e, w=w: e.activation(
                    kT_sb[:, 512 * w:512 * w + 512], ps_B[:, 0:512], AF.Identity,
                    bias=bk_sb[:]), deps=[tok_k_mm, t_bk], inc="sact")
                tok_v_mm = None
                for d in range(NCH):
                    tok_v_mm = E("pe", lambda e, d=d, P=P: e.matmul(
                        ps_B[:, 512:1024],
                        AR2[:, 4096 + 128 * d:4096 + 128 * d + 128],
                        xn_w[P][:, 512 * d:512 * d + 512],
                        start=(d == 0), stop=(d == NCH - 1)),
                        deps=[tok_blk[(w, d)], t_wv, tok_vev],
                        inc="spe" if d == NCH - 1 else None)
                tok_vev = E("act", lambda e: e.activation(
                    vtmp, ps_B[:, 512:1024], AF.Identity, bias=bv_sb[:]),
                    deps=[tok_v_mm, t_bv], inc="sact")
                for j in range(4):
                    sl = 1024 + 512 * (j % 2)
                    tok_tr = E("pe", lambda e, j=j, sl=sl: e.transpose(
                        ps_B[:, sl:sl + 128],
                        vtmp[:, 128 * j:128 * j + 128], identf_sb[:]),
                        deps=[tok_vev, tok_vtr_ev[j % 2]], inc="spe")
                    tok_vtr_ev[j % 2] = E("act", lambda e, w=w, j=j, sl=sl: e.activation(
                        v_sb[:, (4 * w + j) * 128:(4 * w + j) * 128 + 128],
                        ps_B[:, sl:sl + 128], AF.Copy),
                        deps=[tok_tr], inc="sact")
                tok_v_done = tok_vtr_ev[1]
                tok_kv_stop[w] = tok_v_mm
                if w >= 1:
                    lo = max(0, Q_LO - 512 * w)
                    hi = min(512, Q_HI - 512 * w)

                    def q_add(d, w=w, last=False):
                        return E("dve", lambda e, d=d, w=w: (
                            e.tensor_copy(qs_sb[:, d:d + 1], qtmp[:, d % 4:d % 4 + 1])
                            if w == 1 else
                            e.tensor_tensor(qs_sb[:, d:d + 1], qs_sb[:, d:d + 1],
                                            qtmp[:, d % 4:d % 4 + 1], ALU.add)),
                            deps=[], inc="sdve" if last else None)

                    for d in range(NCH):
                        E("dve", lambda e, d=d, P=P, lo=lo, hi=hi: e.reduce_sum(
                            qtmp[:, d % 4:d % 4 + 1],
                            xn_w[P][:, 512 * d + lo:512 * d + hi], axis=AX.X),
                            deps=[tok_blk[(w, d)]], inc=None)
                        if d >= 2:
                            q_add(d - 2)
                    q_add(NCH - 2)
                    tok_qs_wr = q_add(NCH - 1, last=True)
                    tok_qsum_w[w] = tok_qs_wr
            tok_qsum_last = tok_qs_wr
            tok_kT_done = tok_kev
            if upto == "a1":
                t1 = E("sp", lambda e: e.dma_start(dbg["d_kT"], kT_sb[:]),
                       deps=[tok_kT_done], inc="qsp", inc_by=16)
                t2 = E("sp", lambda e: e.dma_start(dbg["d_v"], v_sb[:]),
                       deps=[tok_v_done], inc="qsp", inc_by=16)
                t3 = E("sp", lambda e: e.dma_start(dbg["d_qs"], qs_sb[:]),
                       deps=[tok_qsum_last], inc="qsp", inc_by=16)
                E("sp", lambda e: None, deps=[t1, t2, t3])
                return

            tok_lps = None
            for d in range(NCH):
                tok_lps = E("pe", lambda e, d=d: e.matmul(
                    ps_A[0:1, 0:8], qs_sb[:, d:d + 1], wr_sb[:, 8 * d:8 * d + 8],
                    start=(d == 0), stop=(d == NCH - 1)),
                    deps=[tok_qsum_last, t_wr, w1["row1"]],
                    inc="spe" if d == NCH - 1 else None)
            tok_lg = E("act", lambda e: e.activation(
                rsm[0:1, 0:8], ps_A[0:1, 0:8], AF.Copy, scale=1.0 / NTOK_R),
                deps=[tok_lps], inc="sact")
            tok_lg2 = E("dve", lambda e: e.tensor_tensor(
                rsm[0:1, 8:16], rsm[0:1, 0:8], brsb[:], ALU.add),
                deps=[tok_lg, t_br], inc="sdve")
            tok_le = E("act", lambda e: e.activation(
                rsm[0:1, 16:24], rsm[0:1, 8:16], AF.Exp), deps=[tok_lg2], inc="sact")
            tok_ls = E("dve", lambda e: e.reduce_sum(
                rsm[0:1, 24:25], rsm[0:1, 16:24], axis=AX.X), deps=[tok_le], inc="sdve")
            tok_lr = E("dve", lambda e: e.reciprocal(
                rsm[0:1, 25:26], rsm[0:1, 24:25]), deps=[tok_ls], inc="sdve")
            tok_rt = E("dve", lambda e: e.tensor_scalar(
                rsm[0:1, 32:40], rsm[0:1, 16:24], rsm[0:1, 25:26], None, ALU.mult),
                deps=[tok_lr], inc="sdve")
            tok_rtb = E("dve", lambda e: e.tensor_copy(
                routing_b[:], rsm[0:1, 32:40]), deps=[tok_rt], inc="sdve")
            tok_rt1 = E("pe", lambda e: e.matmul(
                ps_A[0:8, 512:513], routing_b[:], ones1_sb[:], start=True, stop=True),
                deps=[tok_rtb, t_o1, w1["row2"]], inc="spe")
            tok_rtc = E("act", lambda e: e.activation(
                rtcol_b[:], ps_A[0:8, 512:513], AF.Copy), deps=[tok_rt1], inc="sact")
            tok_rsc = E("pe", lambda e: e.matmul(
                ps_A[:, 1024:1025], sel_sb[:], rtcol_b[:], start=True, stop=True),
                deps=[tok_rtc, t_sel, w1["bcr"]], inc="spe")
            tok_rscale = E("act", lambda e: e.activation(
                rscale[:], ps_A[:, 1024:1025], AF.Copy), deps=[tok_rsc], inc="sact")
            if upto == "rt":
                t1 = E("sp", lambda e: e.dma_start(dbg["d_rsc"], rscale[:]),
                       deps=[tok_rscale], inc="qsp", inc_by=16)
                t2 = E("sp", lambda e: e.dma_start(dbg["d_rsm"], rsm[:]),
                       deps=[tok_rtb], inc="qsp", inc_by=16)
                E("sp", lambda e: None, deps=[t1, t2])
                return

            bstage = [AR2f[:, 4096 + 512 * k:4096 + 512 * k + 512] for k in range(2)]
            btmp = [AR2f[:, 5120 + 512 * k:5120 + 512 * k + 512] for k in range(2)]
            dep_b_war = [tok_kv_stop[NW - 1], tok_kv_stop[NW - 2], tok_v_done,
                         tok_qsum_last]
            tok_bstage_rd = [None, None]
            tok_xn_own = {}
            tok_xns = {}
            for d in range(NCH):
                for wi in range(OWN_W):
                    k2 = wi % 2
                    t_x = E("sp", lambda e, d=d, wi=wi, k2=k2: e.dma_start(
                        bstage[k2], xT_own[wi, d]),
                        deps=[tok_bstage_rd[k2]], inc="qsp", inc_by=16)
                    E("dve", lambda e, wi=wi, k2=k2: e.tensor_tensor(
                        btmp[k2], bstage[k2], r_bc_own[:, 512 * wi:512 * wi + 512],
                        ALU.mult),
                        deps=[t_x, tok_own_r] + dep_b_war, inc=None)
                    tok = E("dve", lambda e, d=d, wi=wi, k2=k2: e.tensor_tensor(
                        AR1[:, 2048 * d + 512 * wi:2048 * d + 512 * wi + 512],
                        btmp[k2], rm_bc_own[:, 512 * wi:512 * wi + 512],
                        ALU.subtract),
                        deps=[tok_own_m], inc="sdve")
                    tok_bstage_rd[k2] = tok
                tok_xn_own[d] = tok
                tok_xns[d] = E("act", lambda e, d=d: e.dma_start(
                    xns[d].transpose([1, 0, 2]),
                    AR1[:, 2048 * d:2048 * d + 2048].rearrange(
                        "p (s f) -> p s f", s=4)),
                    deps=[tok_xn_own[d]], inc="qact", inc_by=16)
            dep_xn_own = [tok_xn_own[d] for d in range(NCH)]
            tok_xns_done = tok_xns[NCH - 1]

            wq_cb = [AR2[:, 0:4096], AR2[:, 4096:8192]]
            qevac = [AR2[:, 12288:14336], AR2[:, 14336:16384]]
            tok_q_stop = {}
            tok_q_ev = {}
            tok_q_spill = {}
            for hd in range(H):
                P = hd % 2
                qps = ps_A if P == 0 else ps_B
                t_w = E("pool", lambda e, hd=hd, P=P: e.dma_start(
                    wq_cb[P].rearrange("p (c f) -> p c f", c=NCH),
                    wq[hd].transpose([1, 0, 2])),
                    deps=[tok_q_stop.get(hd - 2),
                          (tok_kv_stop[NW - 1] if hd <= 1 else None)],
                    inc="qpool", inc_by=16)
                extra = []
                if hd == 0:
                    extra = [tok_lg, tok_rtc, tok_rscale, w1["row1"], w1["row2"],
                             w1["bcr"], w1["bcm"]]
                if hd == 1:
                    extra = [tok_kev, tok_vev, tok_v_done]
                tok_mm = None
                for d in range(NCH):
                    for tb in range(4):
                        tok_mm = E("pe",
                            lambda e, P=P, d=d, tb=tb, qps=qps: e.matmul(
                                qps[:, 512 * tb:512 * tb + 512],
                                wq_cb[P][:, 128 * d:128 * d + 128],
                                AR1[:, 2048 * d + 512 * tb:2048 * d + 512 * tb + 512],
                                start=(d == 0), stop=(d == NCH - 1)),
                            deps=[t_w, tok_q_ev.get(hd - 2)] + extra +
                                 (dep_xn_own if hd == 0 else []),
                            inc="spe" if (d == NCH - 1 and tb == 3) else None)
                tok_q_stop[hd] = tok_mm
                tok_q_ev[hd] = E("act", lambda e, hd=hd, P=P, qps=qps: e.activation(
                    qevac[P], qps[:], AF.Identity, bias=bq_sb[:, hd:hd + 1]),
                    deps=[tok_mm, t_bq, tok_q_spill.get(hd - 2)], inc="sact")
                tok_q_spill[hd] = E("act", lambda e, hd=hd, P=P: e.dma_start(
                    qts[hd].transpose([1, 0, 2]),
                    qevac[P].rearrange("p (s f) -> p s f", s=4)),
                    deps=[tok_q_ev[hd]], inc="qact", inc_by=16)

            if upto == "b":
                t1 = E("sp", lambda e: e.dma_start(dbg["d_qts"], qts),
                       deps=[tok_q_spill[H - 1]], inc="qsp", inc_by=16)
                t2 = E("sp", lambda e: e.dma_start(dbg["d_xns"], xns),
                       deps=[tok_xns_done], inc="qsp", inc_by=16)
                E("sp", lambda e: None, deps=[t1, t2] + sch.qbar("qact"))
                return

            qtb = [AR2[:, 0:1024], AR2[:, 1024:2048]]
            pg = [AR2[:, 2048 + 1024 * i:2048 + 1024 * i + 1024] for i in range(6)]
            aev = [AR2[:, 8192 + 512 * j:8192 + 512 * j + 512] for j in range(2)]
            lrow = AR2f[0:1, 4608:5632]
            sT = [ps_A[:, 0:1024], ps_A[:, 1024:2048]]
            accs_c = [ps_B[:, 0:512], ps_B[:, 512:1024]]
            lacc = [ps_B[0:1, 1024:1536], ps_B[0:1, 1536:2048]]

            dep_c_ar2 = [tok_q_stop[H - 1], tok_q_stop[H - 2],
                         tok_q_spill[H - 1], tok_q_spill[H - 2]]

            tok_mask = None
            tok_mask_use = None
            tok_qtb = {}
            tok_qk_last = {}
            tok_exp_tok = {}
            tok_pg_free = {}
            for _pp in range(6):
                tok_pg_free[_pp] = list(dep_c_ar2)
            tok_sT_free = [tok_q_ev[H - 2], tok_q_ev[H - 2]]
            tok_aev_dma = {}
            tok_lts_dma = {}
            tok_lev = None
            tok_aev_ev = [tok_q_ev[H - 1], tok_q_ev[H - 1]]
            tok_lacc_ev = [tok_q_ev[H - 1], tok_q_ev[H - 1]]

            for s in range(4):
                L = LIMS[s]
                tok_mask = E("sp", lambda e, s=s: e.dma_start(
                    mask_sb[:].rearrange("p (j f) -> p j f", j=8),
                    masks[s].transpose([1, 0, 2])),
                    deps=[tok_mask_use] + (dep_c_ar2 if s == 0 else []),
                    inc="qsp", inc_by=16)
                for hp in range(16):
                    h0 = hp * 2
                    Pq = hp % 2
                    gi = s * 16 + hp
                    tok_qtb[gi] = E("sp", lambda e, h0=h0, s=s, Pq=Pq: e.dma_start(
                        qtb[Pq].rearrange("p (h f) -> p h f", h=2),
                        qts[h0:h0 + 2, s].transpose([1, 0, 2])),
                        deps=[tok_q_spill[h0 + 1], tok_qk_last.get(gi - 2)] +
                             (dep_c_ar2 if gi < 2 else []),
                        inc="qsp", inc_by=16)
                    npairs = (L + 1) // 2
                    tok_av = {}
                    tok_masked = {}
                    LAG = 2
                    for pr in range(npairs + LAG):
                        if pr < npairs:
                            for kc in (2 * pr, 2 * pr + 1):
                                if kc >= L:
                                    continue
                                Ps = kc % 2
                                qk_tok = None
                                for h in range(2):
                                    qk_tok = E("pe",
                                        lambda e, kc=kc, h=h, Ps=Ps, Pq=Pq: e.matmul(
                                            sT[Ps][:, 512 * h:512 * h + 512],
                                            kT_sb[:, 128 * kc:128 * kc + 128],
                                            qtb[Pq][:, 512 * h:512 * h + 512],
                                            start=True, stop=True),
                                        deps=[tok_qtb[gi], tok_kT_done,
                                              tok_sT_free[Ps]],
                                        inc="spe" if h == 1 else None)
                                Pp = kc % 6
                                texp = E("act", lambda e, Pp=Pp, Ps=Ps: e.activation(
                                    pg[Pp][:], sT[Ps][:], AF.Exp, scale=SM_SCALE),
                                    deps=[qk_tok] + tok_pg_free.get(Pp, []),
                                    inc="sact" if (kc % 2 == 1 or kc == L - 1)
                                    else None)
                                if kc % 2 == 1 or kc == L - 1:
                                    tok_exp_tok[pr] = texp
                                    tok_sT_free[0] = texp
                                    tok_sT_free[1] = texp
                        if pr >= LAG:
                            for kc in (2 * (pr - LAG), 2 * (pr - LAG) + 1):
                                if kc >= L:
                                    continue
                                Pp = kc % 6
                                if kc >= L - 8:
                                    j = kc - (L - 8)
                                    tmm = None
                                    for h in range(2):
                                        tmm = E("dve",
                                            lambda e, h=h, j=j, Pp=Pp: e.tensor_tensor(
                                                pg[Pp][:, 512 * h:512 * h + 512],
                                                pg[Pp][:, 512 * h:512 * h + 512],
                                                mask_sb[:, 512 * j:512 * j + 512],
                                                ALU.mult),
                                            deps=[tok_exp_tok[pr - LAG], tok_mask],
                                            inc="sdve" if h == 1 else None)
                                    tok_masked[kc] = tmm
                                    tok_mask_use = tmm
                                dep_p = [tok_exp_tok[pr - LAG], tok_masked.get(kc)]
                                for h in range(2):
                                    E("pe", lambda e, h=h, kc=kc, Pp=Pp, L=L: e.matmul(
                                        lacc[h], onescolb_sb[:],
                                        pg[Pp][:, 512 * h:512 * h + 512],
                                        start=(kc == 0), stop=(kc == L - 1)),
                                        deps=dep_p + [t_ocb, tok_lacc_ev[h]],
                                        inc=None)
                                for h in range(2):
                                    tok_av[(kc, h)] = E("pe",
                                        lambda e, h=h, kc=kc, Pp=Pp, L=L: e.matmul(
                                            accs_c[h],
                                            v_sb[:, 128 * kc:128 * kc + 128],
                                            pg[Pp][:, 512 * h:512 * h + 512],
                                            start=(kc == 0), stop=(kc == L - 1)),
                                        deps=dep_p + [tok_v_done, tok_aev_ev[h]],
                                        inc="spe")
                                tok_pg_free[Pp] = [tok_av[(kc, 1)]]
                    tok_qk_last[gi] = tok_exp_tok[npairs - 1]
                    for h in range(2):
                        E("dve", lambda e, h=h: e.tensor_copy(
                            aev[h], accs_c[h]),
                            deps=[tok_av[(L - 1, h)],
                                  tok_aev_dma.get((gi - 1, h))], inc=None)
                    for h in range(2):
                        tok_lev = E("dve", lambda e, h=h: e.tensor_copy(
                            lrow[:, 512 * h:512 * h + 512], lacc[h]),
                            deps=[tok_av[(L - 1, 1)], tok_lts_dma.get(gi - 1)],
                            inc="sdve" if h == 1 else None)
                    for h in range(2):
                        tok_aev_ev[h] = tok_lev
                        tok_lacc_ev[h] = tok_lev
                        tok_aev_dma[(gi, h)] = E("sp",
                            lambda e, h=h, h0=h0, s=s: e.dma_start(
                                ats[h0 + h, s], aev[h]),
                            deps=[tok_lev], inc="qsp", inc_by=16)
                    tok_lts_dma[gi] = E("sp", lambda e, h0=h0, s=s: e.dma_start(
                        lts[h0:h0 + 2, s].unsqueeze(0),
                        lrow[:].rearrange("p (s2 f) -> p s2 f", s2=2)),
                        deps=[tok_lev], inc="qsp", inc_by=16)

            t_bo = E("sp", lambda e: e.dma_start(bo_sb[:], bo), inc="qsp", inc_by=16)
            t_bf = E("sp", lambda e: e.dma_start(bf_sb[:], bfb), inc="qsp", inc_by=16)
            tok_bsum = E("dve", lambda e: e.tensor_tensor(
                bsum[:], bo_sb[:], bf_sb[:], ALU.add), deps=[t_bo, t_bf], inc="sdve")

            bar_pe = sch.bar("spe")
            bar_act = sch.bar("sact")
            bar_dve = sch.bar("sdve")
            bar_pool = sch.bar("spool")
            bar_qact_l = sch.qbar("qact")
            bar_qsp_l = sch.qbar("qsp")

            wcb = [AR2[:, 0:4096], AR2[:, 4096:8192], AR2[:, 8192:12288]]
            wB_t = [AR2[:, 12288:12416], AR2[:, 12416:12544]]
            tp_h = AR2[:, 12544:13568]
            linv_bc = AR2[:, 13568:14592]
            lrowD = AR3[0:1, 4096:5120]
            linv_row = AR2[0:1, 14592:15616]
            xst = [AR3[:, 0:1024], AR3[:, 1024:2048]]
            acts = [AR3[:, 2048:3072], AR3[:, 3072:4096]]
            accs = [ps_A[:, 0:1024], ps_A[:, 1024:2048], ps_B[:, 0:1024]]
            t_ps = ps_B[:, 1024:2048]

            tok_ath = {}
            tok_lath = {}
            tok_xrh = {}
            tok_wcb = {}
            tok_wBt = {}
            tok_acc_stop = {}
            tok_acts = {}
            tok_dadd = {}
            tok_out = {}
            tok_tp_ev = None
            tok_tp_stop = None

            dep_d_ps = [tok_lev, tok_sT_free[0]]

            for hh in range(2):
                dep_half = [tok_acc_stop.get((hh - 1, NCH - 1))] if hh > 0 else \
                    [tok_q_stop[H - 1], tok_xns_done]
                t_wA = E("pool", lambda e: e.dma_start(
                    AR2[:, 0:4096].rearrange("p (c f) -> p c f", c=NCH),
                    wA.transpose([1, 0, 2])),
                    deps=dep_half + ([bar_pe, bar_act, bar_dve] + bar_qact_l +
                                     bar_qsp_l if hh == 0 else []),
                    inc="qpool", inc_by=16)
                for h in range(H):
                    tok_ath[(hh, h)] = E("sp", lambda e, hh=hh, h=h: e.dma_start(
                        AR1[:, 1024 * h:1024 * h + 1024].rearrange(
                            "p (s f) -> p s f", s=2),
                        ats[h, 2 * hh:2 * hh + 2].transpose([1, 0, 2])),
                        deps=[tok_aev_dma[(2 * hh * 16 + h // 2, h % 2)],
                              tok_aev_dma[((2 * hh + 1) * 16 + h // 2, h % 2)],
                              tok_lath.get((hh - 1, h))] +
                             dep_half,
                        inc="qsp", inc_by=16)
                for d in range(NCH):
                    tok_xrh[(hh, d)] = E("sp", lambda e, hh=hh, d=d: e.dma_start(
                        AR1[:, 32768 + 1024 * d:32768 + 1024 * d + 1024].rearrange(
                            "p (s f) -> p s f", s=2),
                        xns[d, 2 * hh:2 * hh + 2].transpose([1, 0, 2])),
                        deps=[tok_xns_done] + dep_half, inc="qsp", inc_by=16)

                tok_expl = None
                tok_bc_prev = None
                tok_bc_ev = dep_d_ps
                for h in range(H):
                    t_lr = E("sp", lambda e, hh=hh, h=h: e.dma_start(
                        lrowD.rearrange("p (s f) -> p s f", s=2),
                        lts[h, 2 * hh:2 * hh + 2].unsqueeze(0)),
                        deps=[tok_lts_dma[(2 * hh) * 16 + h // 2],
                              tok_lts_dma[(2 * hh + 1) * 16 + h // 2],
                              tok_expl] + dep_half,
                        inc="qsp", inc_by=16)
                    tok_ln_l = E("act", lambda e: e.activation(
                        lrowD, lrowD, AF.Ln),
                        deps=[t_lr], inc=None)
                    tok_expl = E("act", lambda e: e.activation(
                        linv_row, lrowD, AF.Exp, scale=-1.0),
                        deps=[tok_bc_prev], inc="sact")
                    tok_bc = None
                    for tb in range(2):
                        tok_bc = E("pe", lambda e, tb=tb: e.matmul(
                            t_ps[:, 512 * tb:512 * tb + 512],
                            onesrowb_sb[:],
                            linv_row[:, 512 * tb:512 * tb + 512],
                            start=True, stop=True),
                            deps=[tok_expl, t_orb] + tok_bc_ev + dep_d_ps,
                            inc="spe" if tb == 1 else None)
                    tok_bc_prev = tok_bc
                    tok_ev = E("dve", lambda e: e.tensor_copy(
                        linv_bc[:], t_ps[:]),
                        deps=[tok_bc], inc=None)
                    tok_lath[(hh, h)] = E("dve", lambda e, h=h: e.tensor_tensor(
                        AR1[:, 1024 * h:1024 * h + 1024],
                        AR1[:, 1024 * h:1024 * h + 1024], linv_bc[:], ALU.mult),
                        deps=[tok_ath[(hh, h)]], inc="sdve")
                    tok_bc_ev = [tok_lath[(hh, h)]]

                for d in range(NCH):
                    for tb in range(2):
                        tok_tp_stop = E("pe", lambda e, d=d, tb=tb: e.matmul(
                            t_ps[:, 512 * tb:512 * tb + 512],
                            AR2[:, 128 * d:128 * d + 128],
                            AR1[:, 32768 + 1024 * d + 512 * tb:
                                32768 + 1024 * d + 512 * tb + 512],
                            start=(d == 0), stop=(d == NCH - 1)),
                            deps=[t_wA, tok_xrh[(hh, d)], tok_tp_ev,
                                  tok_bc_ev[0]] + dep_d_ps,
                            inc="spe" if (d == NCH - 1 and tb == 1) else None)
                tok_tp_ev = E("dve", lambda e: e.tensor_scalar(
                    tp_h[:], t_ps[:], rscale[:], None, ALU.mult),
                    deps=[tok_tp_stop, tok_rscale,
                          *(bar_qact_l if hh == 0 else [None])], inc="sdve")

                for o in range(NCH):
                    u0 = (2 * o) % 3
                    u1 = (2 * o + 1) % 3
                    dep_w0 = [tok_acc_stop.get((hh, o - 2))]
                    if hh == 0 and o == 0:
                        dep_w0 += [tok_tp_stop]
                    if hh == 1 and o == 0:
                        dep_w0 += [tok_tp_stop, tok_acc_stop.get((0, NCH - 1))]
                    if hh == 0 and o == 1:
                        dep_w0 += bar_qact_l + [bar_dve]
                    tok_wcb[(hh, o, 0)] = E("pool", lambda e, o=o, u0=u0: e.dma_start(
                        wcb[u0].rearrange("p (c f) -> p c f", c=NCH),
                        wo[o].transpose([1, 0, 2])),
                        deps=dep_w0, inc="qpool", inc_by=16)
                    tok_wcb[(hh, o, 1)] = E("pool", lambda e, o=o, u1=u1: e.dma_start(
                        wcb[u1].rearrange("p (c f) -> p c f", c=NCH),
                        wW[o].transpose([1, 0, 2])),
                        deps=[tok_acc_stop.get((hh, o - 1)),
                              (tok_acc_stop.get((0, NCH - 1))
                               if (hh == 1 and o == 0) else None)] +
                             ([bar_pe, bar_dve, bar_pool]
                              if (hh == 0 and o == 0) else []),
                        inc="qpool", inc_by=16)
                    tok_wBt[(hh, o)] = E("pool", lambda e, o=o: e.dma_start(
                        wB_t[o % 2], wB[o]),
                        deps=[tok_acc_stop.get((hh, o - 2)),
                              *(bar_qact_l if (hh == 0 and o <= 1) else [None]),
                              (tok_acc_stop.get((0, 30 + o))
                               if (hh == 1 and o < 2) else None)],
                        inc="qpool", inc_by=16)
                    acc = accs[o % 3]
                    tok_mm = None
                    for h in range(H):
                        for tb in range(2):
                            tok_mm = E("pe",
                                lambda e, h=h, tb=tb, u0=u0, acc=acc: e.matmul(
                                    acc[:, 512 * tb:512 * tb + 512],
                                    wcb[u0][:, 128 * h:128 * h + 128],
                                    AR1[:, 1024 * h + 512 * tb:
                                        1024 * h + 512 * tb + 512],
                                    start=(h == 0), stop=False),
                                deps=[tok_wcb[(hh, o, 0)], tok_lath[(hh, h)],
                                      tok_acts.get((hh, o - 3)),
                                      (tok_acts.get((hh - 1, [30, 31, 29][o]))
                                       if (hh > 0 and o < 3) else None)],
                                inc=None)
                    for d in range(NCH):
                        for tb in range(2):
                            tok_mm = E("pe",
                                lambda e, d=d, tb=tb, u1=u1, acc=acc: e.matmul(
                                    acc[:, 512 * tb:512 * tb + 512],
                                    wcb[u1][:, 128 * d:128 * d + 128],
                                    AR1[:, 32768 + 1024 * d + 512 * tb:
                                        32768 + 1024 * d + 512 * tb + 512],
                                    start=False, stop=False),
                                deps=[tok_wcb[(hh, o, 1)], tok_xrh[(hh, d)]],
                                inc=None)
                    for tb in range(2):
                        tok_mm = E("pe", lambda e, tb=tb, o=o, acc=acc: e.matmul(
                            acc[:, 512 * tb:512 * tb + 512], wB_t[o % 2][:],
                            tp_h[:, 512 * tb:512 * tb + 512],
                            start=False, stop=True),
                            deps=[tok_wBt[(hh, o)], tok_tp_ev],
                            inc="spe" if tb == 1 else None)
                    tok_acc_stop[(hh, o)] = tok_mm
                    Po = o % 2
                    prev_out = tok_out.get((hh, o - 2)) if o >= 2 else (
                        tok_out.get((hh - 1, NCH - 2 + o)) if hh > 0 else None)
                    tok_acts[(hh, o)] = E("act",
                        lambda e, o=o, Po=Po, acc=acc: e.activation(
                            acts[Po], acc[:], AF.Identity, bias=bsum[:, o:o + 1]),
                        deps=[tok_mm, tok_bsum, prev_out,
                              (bar_dve if (hh == 0 and o < 2) else None)],
                        inc="sact")
                    prev_xst = tok_dadd.get((hh, o - 2)) if o >= 2 else (
                        tok_dadd.get((hh - 1, NCH - 2 + o)) if hh > 0 else
                        (bar_pe if o < 2 else None))
                    t_xs = E("sp", lambda e, hh=hh, o=o, Po=Po: e.dma_start(
                        xst[Po].rearrange("p (s f) -> p s f", s=2),
                        xT_own[2 * hh:2 * hh + 2, o].transpose([1, 0, 2])),
                        deps=[prev_xst,
                              (sch.bar("spe") if (hh == 0 and o < 2) else None)],
                        inc="qsp", inc_by=16)
                    tok_dadd[(hh, o)] = E("dve", lambda e, Po=Po: e.tensor_tensor(
                        acts[Po], acts[Po], xst[Po], ALU.add),
                        deps=[tok_acts[(hh, o)], t_xs], inc="sdve")
                    tok_out[(hh, o)] = E("act", lambda e, hh=hh, o=o, Po=Po: e.dma_start(
                        out_t[o, 2 * hh:2 * hh + 2].transpose([1, 0, 2]),
                        acts[Po].rearrange("p (s f) -> p s f", s=2)),
                        deps=[tok_dadd[(hh, o)]], inc="qact", inc_by=16)

            fin = sch.qbar("qact") + sch.qbar("qsp") + sch.qbar("qpool")
            E("sp", lambda e: None, deps=fin)

        _emit_phases()

        _flush_op = {
            "act": lambda e: e.copy(scr_act[:], scr_act[:]),
            "dve": lambda e: e.tensor_copy(scr_dve[:], scr_dve[:]),
            "pool": lambda e: e.tensor_copy(scr_pool[:], scr_pool[:]),
        }
        _flush_sems = ("sact", "sdve", "spool")

        def run_stream(name, eng):
            for wl, fn, inc, inc_by in sch.streams[name]:
                for snm, v in wl:
                    eng.wait_ge(sems[snm], v)
                ins = fn(eng)
                if inc is not None:
                    if name in _flush_op and inc in _flush_sems:
                        eng.drain(fusable=False)
                        _flush_op[name](eng).then_inc(sems[inc], inc_by)
                    else:
                        ins.then_inc(sems[inc], inc_by)

        @block.tensor
        def _(eng):
            run_stream("pe", eng)

        @block.scalar
        def _(eng):
            run_stream("act", eng)

        @block.vector
        def _(eng):
            run_stream("dve", eng)

        @block.gpsimd
        def _(eng):
            run_stream("pool", eng)

        @block.sync
        def _(eng):
            run_stream("sp", eng)

    return nc



_NC_CACHE = {}


def _get_program():
    if "nc" not in _NC_CACHE:
        _NC_CACHE["nc"] = build_program()
    return _NC_CACHE["nc"]


def _pack_weights(inp):
    f32 = np.float32
    wq_w = np.asarray(inp["L0_attn_wq_w"], f32)
    wk_w = np.asarray(inp["L0_attn_wk_w"], f32)
    wv_w = np.asarray(inp["L0_attn_wv_w"], f32)
    wo_w = np.asarray(inp["L0_attn_wo_w"], f32)
    W = np.asarray(inp["L0_ffn_W"], f32)
    A = np.asarray(inp["L0_ffn_A"], f32)
    Bm = np.asarray(inp["L0_ffn_B"], f32)
    rW = np.asarray(inp["L0_ffn_router_W"], f32)

    def colblocks(w):
        wt = np.ascontiguousarray(w.T)
        return np.ascontiguousarray(
            wt.reshape(NCH, 128, NCH, 128).transpose(2, 0, 1, 3))

    def dblocks(w):
        wt = np.ascontiguousarray(w.T)
        return np.ascontiguousarray(wt.reshape(NCH, 128, w.shape[0]))

    d = {}
    d["wq"] = colblocks(wq_w)
    d["wo"] = colblocks(wo_w)
    d["wW"] = colblocks(W)
    d["wk"] = dblocks(wk_w)
    d["wv"] = dblocks(wv_w)
    A_all = A.reshape(128, D)
    d["wA"] = dblocks(A_all)
    B_all = Bm.transpose(0, 2, 1).reshape(128, D)
    d["wB"] = np.ascontiguousarray(
        B_all.reshape(128, NCH, 128).transpose(1, 0, 2))
    d["wr"] = dblocks(rW)
    d["bq"] = np.ascontiguousarray(
        np.asarray(inp["L0_attn_wq_b"], f32).reshape(NCH, 128).T)
    d["bk"] = np.asarray(inp["L0_attn_wk_b"], f32).reshape(128, 1)
    d["bv"] = np.asarray(inp["L0_attn_wv_b"], f32).reshape(128, 1)
    d["bo"] = np.ascontiguousarray(
        np.asarray(inp["L0_attn_wo_b"], f32).reshape(NCH, 128).T)
    d["bfb"] = np.ascontiguousarray(
        np.asarray(inp["L0_ffn_b"], f32).reshape(NCH, 128).T)
    d["br"] = np.asarray(inp["L0_ffn_router_b"], f32).reshape(1, 8)
    d["identf"] = np.eye(128, dtype=f32)
    d["onescol_f"] = np.ones((128, 1), f32)
    d["onescol_bf"] = np.ones((128, 1), ml_dtypes.bfloat16)
    d["onesrow_f"] = np.ones((1, 128), f32)
    d["onesrow_b"] = np.ones((1, 128), ml_dtypes.bfloat16)
    d["ones1_b"] = np.ones((1, 1), ml_dtypes.bfloat16)
    sel = np.zeros((8, 128), f32)
    for e_ in range(8):
        sel[e_, e_ * 16:(e_ + 1) * 16] = 2.0
    d["sel_b"] = sel.astype(ml_dtypes.bfloat16)
    return d


def _build_masks(own_supers):
    m = np.zeros((4, 8, 128, 512), np.float32)
    tri = np.triu(np.ones((128, 128), np.float32))
    for s, g in enumerate(own_supers):
        L = LIMS[s]
        for j in range(8):
            kc = L - 8 + j
            for qt in range(4):
                gq = 4 * g + qt
                if kc < gq:
                    m[s, j, :, qt * 128:(qt + 1) * 128] = 1.0
                elif kc == gq:
                    m[s, j, :, qt * 128:(qt + 1) * 128] = tri
    return m.astype(ml_dtypes.bfloat16)


def kernel(**inputs):
    x = np.asarray(inputs["L0_attn_x"], np.float32)
    wd = _pack_weights(inputs)
    masks_by_p = {p: _build_masks(OWN_SUPER[p]) for p in (0, 1)}

    in_maps = []
    for c in range(8):
        b, p = c // 2, c % 2
        owns = OWN_SUPER[p]
        xb = np.ascontiguousarray(x[b])
        xTb = np.ascontiguousarray(xb.T)
        xT_blocks = np.ascontiguousarray(
            xTb.reshape(NCH, 128, NW, 512).transpose(2, 0, 1, 3))
        own_rows = np.concatenate(
            [np.arange(512 * g, 512 * (g + 1)) for g in owns])
        xtok_own = np.ascontiguousarray(xb[own_rows])
        xT_own = np.ascontiguousarray(xT_blocks[owns])
        m = {"x_glob": xb, "xtok_own": xtok_own, "xT_glob": xT_blocks,
             "xT_own": xT_own, "masks": masks_by_p[p]}
        m.update(wd)
        in_maps.append(m)

    nc = _get_program()
    res = run_bass_kernel_spmd(nc, in_maps, core_ids=list(range(8)))

    out = np.empty((4, 4096, 4096), np.float32)
    for c in range(8):
        b, p = c // 2, c % 2
        owns = OWN_SUPER[p]
        r = res.results[c]["out"]
        arr = np.ascontiguousarray(r.transpose(1, 3, 0, 2)).reshape(4, 512, 4096)
        for wi, g in enumerate(owns):
            out[b, 512 * g:512 * (g + 1), :] = arr[wi]
    return out


if __name__ == "__main__":
    import reference
    inp = reference.setup_inputs()
    o = kernel(**{k: np.asarray(v) for k, v in inp.items()})
    ref = np.asarray(reference.reference(**inp))
    err = np.abs(o - ref)
    rel = np.sqrt(((o - ref) ** 2).mean() / (ref ** 2).mean())
    print("max abs err:", err.max(), "rel l2:", rel)


# version 05 (final)
# speedup vs baseline: 1.2742x; 1.0414x over previous
import numpy as np
import ml_dtypes
import concourse.bass as bass
import concourse.mybir as mybir
from concourse.bass_utils import run_bass_kernel_spmd

F32 = mybir.dt.float32
BF16 = mybir.dt.bfloat16
AF = mybir.ActivationFunctionType
ALU = mybir.AluOpType
AX = mybir.AxisListType

D = 4096
T = 4096
HD = 128
H = 32
NCH = 32
NW = 8
OWN_W = 4
OT = 2048
LIMS = [8, 16, 24, 32]
OWN_SUPER = {0: [0, 3, 4, 7], 1: [1, 2, 5, 6]}
EPS = 1e-5
SM_SCALE = float(1.0 / np.sqrt(128.0))
Q_LO, Q_HI = 611, 4095
NTOK_R = Q_HI - Q_LO


class Sched:

    ENGINES = ("pe", "act", "dve", "pool", "sp")

    NQROT = 8

    def __init__(self):
        self.streams = {e: [] for e in self.ENGINES}
        self.cnt = {}
        self.lastw = {}
        self.qrot = {}

    def emit(self, eng, fn, deps=(), inc=None, inc_by=1):
        wl = []
        for tok in deps:
            if tok is None:
                continue
            s, v = tok
            if v > self.lastw.get((eng, s), 0):
                wl.append((s, v))
                self.lastw[(eng, s)] = v
        tok = None
        if inc is not None:
            if inc in ("qsp", "qact", "qpool"):
                r = self.qrot.get(inc, 0)
                self.qrot[inc] = (r + 1) % self.NQROT
                inc = f"{inc}{r}"
            self.cnt[inc] = self.cnt.get(inc, 0) + inc_by
            tok = (inc, self.cnt[inc])
        self.streams[eng].append((wl, fn, inc, inc_by))
        return tok

    def qbar(self, prefix):
        return [(f"{prefix}{r}", self.cnt.get(f"{prefix}{r}", 0))
                for r in range(self.NQROT)]

    def bar(self, name):
        return (name, self.cnt.get(name, 0))


class _Stop(Exception):
    pass


def build_program(upto="d"):
    nc = bass.Bass("TRN2", target_bir_lowering=False)

    def din(name, shape, dt=F32):
        return nc.dram_tensor(name, list(shape), dt, kind="ExternalInput").ap()

    x_glob = din("x_glob", (T, D))
    xtok_own = din("xtok_own", (OT, D))
    xT_glob = din("xT_glob", (NW, NCH, 128, 512))
    xT_own = din("xT_own", (OWN_W, NCH, 128, 512))
    wq = din("wq", (NCH, 128, NCH, 128))
    wk = din("wk", (128, NCH, 128))
    wv = din("wv", (128, NCH, 128))
    wo = din("wo", (NCH, 128, NCH, 128))
    wW = din("wW", (NCH, 128, NCH, 128))
    wA = din("wA", (128, NCH, 128))
    wB = din("wB", (NCH, 128, 128))
    wr = din("wr", (128, NCH, 8))
    bq = din("bq", (128, 32))
    bk = din("bk", (128, 1))
    bv = din("bv", (128, 1))
    bo = din("bo", (128, 32))
    bfb = din("bfb", (128, 32))
    br = din("br", (1, 8))
    identf = din("identf", (128, 128))
    onescol_f = din("onescol_f", (128, 1))
    onescol_bf = din("onescol_bf", (128, 1), BF16)
    onesrow_f = din("onesrow_f", (1, 128))
    onesrow_b = din("onesrow_b", (1, 128), BF16)
    ones1_b = din("ones1_b", (1, 1), BF16)
    sel_b = din("sel_b", (8, 128), BF16)
    masks = din("masks", (4, 128, 8, 512), BF16)

    out_t = nc.dram_tensor("out", [NCH, OWN_W, 128, 512], F32,
                           kind="ExternalOutput").ap()
    dbg = {}
    if upto != "d":
        if upto == "a0":
            dbg["d_own"] = nc.dram_tensor("d_own", [128, 4096], F32, kind="ExternalOutput").ap()
            dbg["d_glob"] = nc.dram_tensor("d_glob", [128, 8192], F32, kind="ExternalOutput").ap()
        if upto == "a1":
            dbg["d_kT"] = nc.dram_tensor("d_kT", [128, 4096], BF16, kind="ExternalOutput").ap()
            dbg["d_v"] = nc.dram_tensor("d_v", [128, 4096], BF16, kind="ExternalOutput").ap()
            dbg["d_qs"] = nc.dram_tensor("d_qs", [128, 32], F32, kind="ExternalOutput").ap()
        if upto == "rt":
            dbg["d_rsc"] = nc.dram_tensor("d_rsc", [128, 1], F32, kind="ExternalOutput").ap()
            dbg["d_rsm"] = nc.dram_tensor("d_rsm", [1, 64], F32, kind="ExternalOutput").ap()
        if upto == "b":
            dbg["d_qts"] = nc.dram_tensor("d_qts", [4, 16, 128, 1024], BF16, kind="ExternalOutput").ap()
            dbg["d_xns"] = nc.dram_tensor("d_xns", [NCH, OWN_W, 128, 512], BF16, kind="ExternalOutput").ap()
        if upto == "c":
            dbg["d_ats"] = nc.dram_tensor("d_ats", [H, 4, 128, 512], BF16, kind="ExternalOutput").ap()
            dbg["d_pacc"] = nc.dram_tensor("d_pacc", [128, 2048], F32, kind="ExternalOutput").ap()
            dbg["d_linv"] = nc.dram_tensor("d_linv", [128, 512], F32, kind="ExternalOutput").ap()
            dbg["d_pg"] = nc.dram_tensor("d_pg", [128, 2048], BF16, kind="ExternalOutput").ap()
    xns = nc.dram_tensor("xns", [NCH, OWN_W, 128, 512], BF16, kind="Internal").ap()
    qts2 = nc.dram_tensor("qts", [4, 16, 128, 1024], BF16, kind="Internal").ap()
    ats = nc.dram_tensor("ats", [H, 4, 128, 512], BF16, kind="Internal").ap()
    lts = nc.dram_tensor("lts", [H, 4, 512], F32, kind="Internal").ap()

    sch = Sched()
    E = sch.emit

    import contextlib
    with contextlib.ExitStack() as _st:
        AR1_t = _st.enter_context(nc.sbuf_tensor("AR1", [128, 65536], BF16))
        AR2_t = _st.enter_context(nc.sbuf_tensor("AR2", [128, 16384], BF16))
        AR3_t = _st.enter_context(nc.sbuf_tensor("AR3", [128, 5120], F32))
        kT_sb = _st.enter_context(nc.sbuf_tensor("kT_sb", [128, 4096], BF16))
        v_sb = _st.enter_context(nc.sbuf_tensor("v_sb", [128, 4096], BF16))
        mask_sb = _st.enter_context(nc.sbuf_tensor("mask_sb", [128, 4096], BF16))
        stats_sb = _st.enter_context(nc.sbuf_tensor("stats_sb", [128, 48], F32))
        aggr_sb = _st.enter_context(nc.sbuf_tensor("aggr_sb", [128, 2], F32))
        lncol = _st.enter_context(nc.sbuf_tensor("lncol", [128, 1], F32))
        rcol = _st.enter_context(nc.sbuf_tensor("rcol", [128, 1], F32))
        rmcol = _st.enter_context(nc.sbuf_tensor("rmcol", [128, 1], F32))
        qs_sb = _st.enter_context(nc.sbuf_tensor("qs_sb", [128, 32], F32))
        qtmp = _st.enter_context(nc.sbuf_tensor("qtmp", [128, 4], F32))
        brsb = _st.enter_context(nc.sbuf_tensor("brsb", [1, 8], F32))
        rsm = _st.enter_context(nc.sbuf_tensor("rsm", [1, 64], F32))
        routing_b = _st.enter_context(nc.sbuf_tensor("routing_b", [1, 8], BF16))
        rtcol_b = _st.enter_context(nc.sbuf_tensor("rtcol_b", [8, 1], BF16))
        rscale = _st.enter_context(nc.sbuf_tensor("rscale", [128, 1], F32))
        bq_sb = _st.enter_context(nc.sbuf_tensor("bq_sb", [128, 32], F32))
        bk_sb = _st.enter_context(nc.sbuf_tensor("bk_sb", [128, 1], F32))
        bv_sb = _st.enter_context(nc.sbuf_tensor("bv_sb", [128, 1], F32))
        bo_sb = _st.enter_context(nc.sbuf_tensor("bo_sb", [128, 32], F32))
        bf_sb = _st.enter_context(nc.sbuf_tensor("bf_sb", [128, 32], F32))
        bsum = _st.enter_context(nc.sbuf_tensor("bsum", [128, 32], F32))
        identf_sb = _st.enter_context(nc.sbuf_tensor("identf_sb", [128, 128], F32))
        onescol_sb = _st.enter_context(nc.sbuf_tensor("onescol_sb", [128, 1], F32))
        onescolb_sb = _st.enter_context(nc.sbuf_tensor("onescolb_sb", [128, 1], BF16))
        onesrow_sb = _st.enter_context(nc.sbuf_tensor("onesrow_sb", [1, 128], F32))
        onesrowb_sb = _st.enter_context(nc.sbuf_tensor("onesrowb_sb", [1, 128], BF16))
        ones1_sb = _st.enter_context(nc.sbuf_tensor("ones1_sb", [1, 1], BF16))
        sel_sb = _st.enter_context(nc.sbuf_tensor("sel_sb", [8, 128], BF16))
        epscol = _st.enter_context(nc.sbuf_tensor("epscol", [128, 1], F32))
        scr_act = _st.enter_context(nc.sbuf_tensor("scr_act", [1, 1], F32))
        scr_dve = _st.enter_context(nc.sbuf_tensor("scr_dve", [1, 1], F32))
        scr_pool = _st.enter_context(nc.sbuf_tensor("scr_pool", [1, 1], F32))
        ps_A = _st.enter_context(nc.psum_tensor([128, 2048], F32))
        ps_B = _st.enter_context(nc.psum_tensor([128, 2048], F32))
        qsems = {}
        for _qp in ("qsp", "qact", "qpool"):
            for _r in range(Sched.NQROT):
                qsems[f"{_qp}{_r}"] = _st.enter_context(
                    nc.semaphore(f"{_qp}{_r}"))
        sem_spe = _st.enter_context(nc.semaphore("spe"))
        sem_sact = _st.enter_context(nc.semaphore("sact"))
        sem_sdve = _st.enter_context(nc.semaphore("sdve"))
        sem_spool = _st.enter_context(nc.semaphore("spool"))
        block = _st.enter_context(nc.Block())
        sems = {"spe": sem_spe, "sact": sem_sact, "sdve": sem_sdve,
                "spool": sem_spool}
        sems.update(qsems)

        _phases_done = False
        AR1 = AR1_t[:]
        AR2 = AR2_t[:]
        AR3 = AR3_t[:]
        AR1f = AR1.bitcast(F32)
        AR2f = AR2.bitcast(F32)
        rowb = AR3_t[0:1, 4096:5120]
        lnl = AR2f[0:1, 5120:6144]
        wr_sb = AR2f[:, 6144:6400]

        def _emit_phases():
            tok_eps = E("dve", lambda e: e.memset(epscol[:], EPS), inc="sdve")
            t_idf = E("sp", lambda e: e.dma_start(identf_sb[:], identf), inc="qsp", inc_by=16)
            t_oc = E("sp", lambda e: e.dma_start(onescol_sb[:], onescol_f), inc="qsp", inc_by=16)
            t_ocb = E("sp", lambda e: e.dma_start(onescolb_sb[:], onescol_bf), inc="qsp", inc_by=16)
            t_or = E("sp", lambda e: e.dma_start(onesrow_sb[:], onesrow_f), inc="qsp", inc_by=16)
            t_orb = E("sp", lambda e: e.dma_start(onesrowb_sb[:], onesrow_b), inc="qsp", inc_by=16)
            t_o1 = E("sp", lambda e: e.dma_start(ones1_sb[:], ones1_b), inc="qsp", inc_by=16)
            t_sel = E("sp", lambda e: e.dma_start(sel_sb[:], sel_b), inc="qsp", inc_by=16)
            t_bk = E("sp", lambda e: e.dma_start(bk_sb[:], bk), inc="qsp", inc_by=16)
            t_bv = E("sp", lambda e: e.dma_start(bv_sb[:], bv), inc="qsp", inc_by=16)
            t_bq = E("sp", lambda e: e.dma_start(bq_sb[:], bq), inc="qsp", inc_by=16)
            t_wr = E("sp", lambda e: e.dma_start(
                wr_sb[:].rearrange("p (c e) -> p c e", c=NCH),
                wr), inc="qsp", inc_by=16)
            t_br = E("sp", lambda e: e.dma_start(brsb[:], br), inc="qsp", inc_by=16)

            def a0_pass(src_ap, ntiles, rbc, rmbc, war):
                src_v = src_ap.rearrange("(n p) d -> n p d", p=128)
                stage = [AR1f[:, 0:4096], AR1f[:, 4096:8192]]
                t_stage_rd = [war.get("stage0"), war.get("stage1")]
                t_row_ev1 = war.get("row1")
                t_row_ev2 = war.get("row2")
                t_bc_ev_r = war.get("bcr")
                t_bc_ev_m = war.get("bcm")
                t_rc_read = war.get("rc")
                t_rm_read = war.get("rm")
                rc_mm = [None] * 4
                rm_mm = [None] * 4
                for tt in range(ntiles):
                    P = tt % 2
                    g, j = tt // 4, tt % 4
                    st = stage[P]
                    t_x = E("sp", lambda e, st=st, tt=tt: e.dma_start(st, src_v[tt]),
                            deps=[t_stage_rd[P]], inc="qsp", inc_by=16)
                    tok_bn = None
                    for i in range(8):
                        tok_bn = E("dve", lambda e, st=st, i=i: e.bn_stats(
                            stats_sb[:, 6 * i:6 * i + 6], st[:, 512 * i:512 * i + 512]),
                            deps=[t_x], inc="sdve" if i == 7 else None)
                    t_stage_rd[P] = tok_bn
                    tok_ag = E("dve", lambda e: e.bn_aggr(aggr_sb[:], stats_sb[:]),
                               deps=[tok_bn], inc="sdve")
                    tok_ln = E("act", lambda e: e.activation(
                        lncol[:], aggr_sb[:, 1:2], AF.Ln, bias=epscol[:]),
                        deps=[tok_ag, tok_eps], inc="sact")
                    tok_r = E("act", lambda e: e.activation(
                        rcol[:], lncol[:], AF.Exp, scale=-0.5),
                        deps=[tok_ln, t_rc_read], inc="sact")
                    tok_rm = E("dve", lambda e: e.tensor_tensor(
                        rmcol[:], aggr_sb[:, 0:1], rcol[:], ALU.mult),
                        deps=[tok_r, t_rm_read], inc="sdve")
                    rc_mm[j] = E("pe", lambda e, j=j: e.matmul(
                        ps_A[0:1, 128 * j:128 * j + 128], rcol[:], identf_sb[:],
                        start=True, stop=True),
                        deps=[tok_r, t_idf, t_row_ev1], inc="spe")
                    rm_mm[j] = E("pe", lambda e, j=j: e.matmul(
                        ps_A[0:1, 512 + 128 * j:512 + 128 * j + 128], rmcol[:],
                        identf_sb[:], start=True, stop=True),
                        deps=[tok_rm, t_row_ev2], inc="spe")
                    t_rc_read = rc_mm[j]
                    t_rm_read = rm_mm[j]
                    if j == 3:
                        t_row_ev1 = E("act", lambda e: e.activation(
                            rowb[:, 0:512], ps_A[0:1, 0:512], AF.Copy),
                            deps=rc_mm, inc="sact")
                        t_row_ev2 = E("act", lambda e: e.activation(
                            rowb[:, 512:1024], ps_A[0:1, 512:1024], AF.Copy),
                            deps=rm_mm, inc="sact")
                        bc1 = E("pe", lambda e: e.matmul(
                            ps_A[:, 1024:1536], onesrow_sb[:], rowb[:, 0:512],
                            start=True, stop=True),
                            deps=[t_row_ev1, t_or, t_bc_ev_r], inc="spe")
                        bc2 = E("pe", lambda e: e.matmul(
                            ps_A[:, 1536:2048], onesrow_sb[:], rowb[:, 512:1024],
                            start=True, stop=True),
                            deps=[t_row_ev2, t_bc_ev_m], inc="spe")
                        t_bc_ev_r = E("act", lambda e, g=g: e.activation(
                            rbc[:, 512 * g:512 * g + 512], ps_A[:, 1024:1536],
                            AF.Copy), deps=[bc1], inc="sact")
                        t_bc_ev_m = E("act", lambda e, g=g: e.activation(
                            rmbc[:, 512 * g:512 * g + 512], ps_A[:, 1536:2048],
                            AF.Copy), deps=[bc2], inc="sact")
                return {"stage0": t_stage_rd[0], "stage1": t_stage_rd[1],
                        "row1": t_row_ev1, "row2": t_row_ev2,
                        "bcr": t_bc_ev_r, "bcm": t_bc_ev_m,
                        "rc": t_rc_read, "rm": t_rm_read}

            r_bc_own = AR3[:, 0:2048]
            rm_bc_own = AR3[:, 2048:4096]
            w0 = a0_pass(xtok_own, 16, r_bc_own, rm_bc_own, {})
            tok_own_r, tok_own_m = w0["bcr"], w0["bcm"]
            r_bc_g = AR1f[:, 20480:24576]
            rm_bc_g = AR1f[:, 24576:28672]
            w1 = a0_pass(x_glob, 32, r_bc_g, rm_bc_g, w0)
            tok_g_r, tok_g_m = w1["bcr"], w1["bcm"]
            if upto == "a0":
                t1 = E("sp", lambda e: e.dma_start(dbg["d_own"], AR3[:, 0:4096]),
                       deps=[tok_own_r, tok_own_m], inc="qsp", inc_by=16)
                t2 = E("sp", lambda e: e.dma_start(dbg["d_glob"], AR1f[:, 20480:28672]),
                       deps=[tok_g_r, tok_g_m], inc="qsp", inc_by=16)
                E("sp", lambda e: None, deps=[t1, t2])
                return

            t_wk = E("pool", lambda e: e.dma_start(
                AR2[:, 0:4096].rearrange("p (c f) -> p c f", c=NCH),
                wk), inc="qpool", inc_by=16)
            t_wv = E("pool", lambda e: e.dma_start(
                AR2[:, 4096:8192].rearrange("p (c f) -> p c f", c=NCH),
                wv), inc="qpool", inc_by=16)

            xn_w = [AR1[:, 0:16384], AR1[:, 16384:32768]]
            xstage = [AR1f[:, 16384 + 512 * k:16384 + 512 * k + 512] for k in range(6)]
            tmpf = [AR1f[:, 28672:29184], AR1f[:, 29184:29696]]
            vtmp = AR1f[:, 29696:30208]

            tok_blk = {}
            tok_kv_stop = {}
            tok_qsum_last = None
            tok_kev = None
            tok_vev = None
            tok_vtr_ev = [None, None]
            tok_xstage_rd = [w1["stage0"], w1["stage1"], None, None, None, None]
            blk_order = []
            tok_qs_wr = None
            tok_v_done = None

            tok_qsum_w = {}
            for w in range(NW):
                P = w % 2
                dep_war = [tok_kv_stop.get(w - 2), tok_qsum_w.get(w - 2)]
                pend = {"dve": [], "pool": []}
                for d in range(NCH):
                    i_lin = w * NCH + d
                    k3 = i_lin % 6
                    eng = "dve" if d % 2 == 0 else "pool"
                    war6 = tok_blk.get(blk_order[i_lin - 6]) if i_lin >= 6 else \
                        tok_xstage_rd[k3]
                    blk_order.append((w, d))
                    t_x = E("sp", lambda e, w=w, d=d, k3=k3: e.dma_start(
                        xstage[k3], xT_glob[w, d]),
                        deps=[war6], inc="qsp", inc_by=16)
                    tp = d % 2
                    E(eng, lambda e, w=w, tp=tp, k3=k3: e.tensor_tensor(
                        tmpf[tp], xstage[k3], r_bc_g[:, 512 * w:512 * w + 512],
                        ALU.mult),
                        deps=[t_x, tok_g_r] + dep_war, inc=None)
                    do_inc = (d >= NCH - 2) or (d % 4 in (2, 3))
                    tok = E(eng, lambda e, w=w, d=d, P=P, tp=tp: e.tensor_tensor(
                        xn_w[P][:, 512 * d:512 * d + 512], tmpf[tp],
                        rm_bc_g[:, 512 * w:512 * w + 512], ALU.subtract),
                        deps=[tok_g_m],
                        inc=("sdve" if eng == "dve" else "spool") if do_inc else None)
                    if do_inc:
                        for dd in pend[eng] + [d]:
                            tok_blk[(w, dd)] = tok
                        pend[eng] = []
                    else:
                        pend[eng].append(d)
                tok_k_mm = None
                for d in range(NCH):
                    tok_k_mm = E("pe", lambda e, d=d, P=P: e.matmul(
                        ps_B[:, 0:512], AR2[:, 128 * d:128 * d + 128],
                        xn_w[P][:, 512 * d:512 * d + 512],
                        start=(d == 0), stop=(d == NCH - 1)),
                        deps=[tok_blk[(w, d)], t_wk, tok_kev],
                        inc="spe" if d == NCH - 1 else None)
                tok_kev = E("act", lambda e, w=w: e.activation(
                    kT_sb[:, 512 * w:512 * w + 512], ps_B[:, 0:512], AF.Identity,
                    bias=bk_sb[:]), deps=[tok_k_mm, t_bk], inc="sact")
                tok_v_mm = None
                for d in range(NCH):
                    tok_v_mm = E("pe", lambda e, d=d, P=P: e.matmul(
                        ps_B[:, 512:1024],
                        AR2[:, 4096 + 128 * d:4096 + 128 * d + 128],
                        xn_w[P][:, 512 * d:512 * d + 512],
                        start=(d == 0), stop=(d == NCH - 1)),
                        deps=[tok_blk[(w, d)], t_wv, tok_vev],
                        inc="spe" if d == NCH - 1 else None)
                tok_vev = E("act", lambda e: e.activation(
                    vtmp, ps_B[:, 512:1024], AF.Identity, bias=bv_sb[:]),
                    deps=[tok_v_mm, t_bv], inc="sact")
                for j in range(4):
                    sl = 1024 + 512 * (j % 2)
                    tok_tr = E("pe", lambda e, j=j, sl=sl: e.transpose(
                        ps_B[:, sl:sl + 128],
                        vtmp[:, 128 * j:128 * j + 128], identf_sb[:]),
                        deps=[tok_vev, tok_vtr_ev[j % 2]], inc="spe")
                    tok_vtr_ev[j % 2] = E("act", lambda e, w=w, j=j, sl=sl: e.activation(
                        v_sb[:, (4 * w + j) * 128:(4 * w + j) * 128 + 128],
                        ps_B[:, sl:sl + 128], AF.Copy),
                        deps=[tok_tr], inc="sact")
                tok_v_done = tok_vtr_ev[1]
                tok_kv_stop[w] = tok_v_mm
                if w >= 1:
                    lo = max(0, Q_LO - 512 * w)
                    hi = min(512, Q_HI - 512 * w)

                    def q_add(d, w=w, last=False):
                        return E("dve", lambda e, d=d, w=w: (
                            e.tensor_copy(qs_sb[:, d:d + 1], qtmp[:, d % 4:d % 4 + 1])
                            if w == 1 else
                            e.tensor_tensor(qs_sb[:, d:d + 1], qs_sb[:, d:d + 1],
                                            qtmp[:, d % 4:d % 4 + 1], ALU.add)),
                            deps=[], inc="sdve" if last else None)

                    for d in range(NCH):
                        E("dve", lambda e, d=d, P=P, lo=lo, hi=hi: e.reduce_sum(
                            qtmp[:, d % 4:d % 4 + 1],
                            xn_w[P][:, 512 * d + lo:512 * d + hi], axis=AX.X),
                            deps=[tok_blk[(w, d)]], inc=None)
                        if d >= 2:
                            q_add(d - 2)
                    q_add(NCH - 2)
                    tok_qs_wr = q_add(NCH - 1, last=True)
                    tok_qsum_w[w] = tok_qs_wr
            tok_qsum_last = tok_qs_wr
            tok_kT_done = tok_kev
            if upto == "a1":
                t1 = E("sp", lambda e: e.dma_start(dbg["d_kT"], kT_sb[:]),
                       deps=[tok_kT_done], inc="qsp", inc_by=16)
                t2 = E("sp", lambda e: e.dma_start(dbg["d_v"], v_sb[:]),
                       deps=[tok_v_done], inc="qsp", inc_by=16)
                t3 = E("sp", lambda e: e.dma_start(dbg["d_qs"], qs_sb[:]),
                       deps=[tok_qsum_last], inc="qsp", inc_by=16)
                E("sp", lambda e: None, deps=[t1, t2, t3])
                return

            tok_lps = None
            for d in range(NCH):
                tok_lps = E("pe", lambda e, d=d: e.matmul(
                    ps_A[0:1, 0:8], qs_sb[:, d:d + 1], wr_sb[:, 8 * d:8 * d + 8],
                    start=(d == 0), stop=(d == NCH - 1)),
                    deps=[tok_qsum_last, t_wr, w1["row1"]],
                    inc="spe" if d == NCH - 1 else None)
            tok_lg = E("act", lambda e: e.activation(
                rsm[0:1, 0:8], ps_A[0:1, 0:8], AF.Copy, scale=1.0 / NTOK_R),
                deps=[tok_lps], inc="sact")
            tok_lg2 = E("dve", lambda e: e.tensor_tensor(
                rsm[0:1, 8:16], rsm[0:1, 0:8], brsb[:], ALU.add),
                deps=[tok_lg, t_br], inc="sdve")
            tok_le = E("act", lambda e: e.activation(
                rsm[0:1, 16:24], rsm[0:1, 8:16], AF.Exp), deps=[tok_lg2], inc="sact")
            tok_ls = E("dve", lambda e: e.reduce_sum(
                rsm[0:1, 24:25], rsm[0:1, 16:24], axis=AX.X), deps=[tok_le], inc="sdve")
            tok_lr = E("dve", lambda e: e.reciprocal(
                rsm[0:1, 25:26], rsm[0:1, 24:25]), deps=[tok_ls], inc="sdve")
            tok_rt = E("dve", lambda e: e.tensor_scalar(
                rsm[0:1, 32:40], rsm[0:1, 16:24], rsm[0:1, 25:26], None, ALU.mult),
                deps=[tok_lr], inc="sdve")
            tok_rtb = E("dve", lambda e: e.tensor_copy(
                routing_b[:], rsm[0:1, 32:40]), deps=[tok_rt], inc="sdve")
            tok_rt1 = E("pe", lambda e: e.matmul(
                ps_A[0:8, 512:513], routing_b[:], ones1_sb[:], start=True, stop=True),
                deps=[tok_rtb, t_o1, w1["row2"]], inc="spe")
            tok_rtc = E("act", lambda e: e.activation(
                rtcol_b[:], ps_A[0:8, 512:513], AF.Copy), deps=[tok_rt1], inc="sact")
            tok_rsc = E("pe", lambda e: e.matmul(
                ps_A[:, 1024:1025], sel_sb[:], rtcol_b[:], start=True, stop=True),
                deps=[tok_rtc, t_sel, w1["bcr"]], inc="spe")
            tok_rscale = E("act", lambda e: e.activation(
                rscale[:], ps_A[:, 1024:1025], AF.Copy), deps=[tok_rsc], inc="sact")
            if upto == "rt":
                t1 = E("sp", lambda e: e.dma_start(dbg["d_rsc"], rscale[:]),
                       deps=[tok_rscale], inc="qsp", inc_by=16)
                t2 = E("sp", lambda e: e.dma_start(dbg["d_rsm"], rsm[:]),
                       deps=[tok_rtb], inc="qsp", inc_by=16)
                E("sp", lambda e: None, deps=[t1, t2])
                return

            bstage = [AR2f[:, 4096 + 512 * k:4096 + 512 * k + 512] for k in range(2)]
            btmp = [AR2f[:, 5120 + 512 * k:5120 + 512 * k + 512] for k in range(2)]
            dep_b_war = [tok_kv_stop[NW - 1], tok_kv_stop[NW - 2], tok_v_done,
                         tok_qsum_last]
            tok_bstage_rd = [None, None]
            tok_xn_own = {}
            tok_xns = {}
            for d in range(NCH):
                for wi in range(OWN_W):
                    k2 = wi % 2
                    t_x = E("sp", lambda e, d=d, wi=wi, k2=k2: e.dma_start(
                        bstage[k2], xT_own[wi, d]),
                        deps=[tok_bstage_rd[k2]], inc="qsp", inc_by=16)
                    E("dve", lambda e, wi=wi, k2=k2: e.tensor_tensor(
                        btmp[k2], bstage[k2], r_bc_own[:, 512 * wi:512 * wi + 512],
                        ALU.mult),
                        deps=[t_x, tok_own_r] + dep_b_war, inc=None)
                    tok = E("dve", lambda e, d=d, wi=wi, k2=k2: e.tensor_tensor(
                        AR1[:, 2048 * d + 512 * wi:2048 * d + 512 * wi + 512],
                        btmp[k2], rm_bc_own[:, 512 * wi:512 * wi + 512],
                        ALU.subtract),
                        deps=[tok_own_m], inc="sdve")
                    tok_bstage_rd[k2] = tok
                tok_xn_own[d] = tok
                tok_xns[d] = E("act", lambda e, d=d: e.dma_start(
                    xns[d].transpose([1, 0, 2]),
                    AR1[:, 2048 * d:2048 * d + 2048].rearrange(
                        "p (s f) -> p s f", s=4)),
                    deps=[tok_xn_own[d]], inc="qact", inc_by=16)
            dep_xn_own = [tok_xn_own[d] for d in range(NCH)]
            tok_xns_done = tok_xns[NCH - 1]

            wq_cb = [AR2[:, 0:4096], AR2[:, 4096:8192]]
            qpair = AR2[:, 8192:12288]
            tok_q_stop = {}
            tok_q_ev = {}
            tok_q_spill = {}
            for hd in range(H):
                P = hd % 2
                qps = ps_A if P == 0 else ps_B
                t_w = E("pool", lambda e, hd=hd, P=P: e.dma_start(
                    wq_cb[P].rearrange("p (c f) -> p c f", c=NCH),
                    wq[hd]),
                    deps=[tok_q_stop.get(hd - 2),
                          (tok_kv_stop[NW - 1] if hd <= 1 else None)],
                    inc="qpool", inc_by=16)
                extra = []
                if hd == 0:
                    extra = [tok_lg, tok_rtc, tok_rscale, w1["row1"], w1["row2"],
                             w1["bcr"], w1["bcm"]]
                if hd == 1:
                    extra = [tok_kev, tok_vev, tok_v_done]
                tok_mm = None
                for d in range(NCH):
                    for tb in range(4):
                        tok_mm = E("pe",
                            lambda e, P=P, d=d, tb=tb, qps=qps: e.matmul(
                                qps[:, 512 * tb:512 * tb + 512],
                                wq_cb[P][:, 128 * d:128 * d + 128],
                                AR1[:, 2048 * d + 512 * tb:2048 * d + 512 * tb + 512],
                                start=(d == 0), stop=(d == NCH - 1)),
                            deps=[t_w, tok_q_ev.get(hd - 2)] + extra +
                                 (dep_xn_own if hd == 0 else []),
                            inc="spe" if (d == NCH - 1 and tb == 3) else None)
                tok_q_stop[hd] = tok_mm
                m = hd % 2
                war_pair = tok_q_spill.get(2 * (hd // 2) - 1)
                tev = None
                for s4 in range(4):
                    tev = E("act", lambda e, hd=hd, s4=s4, m=m, qps=qps: e.activation(
                        qpair[:, 1024 * s4 + 512 * m:1024 * s4 + 512 * m + 512],
                        qps[:, 512 * s4:512 * s4 + 512],
                        AF.Identity, bias=bq_sb[:, hd:hd + 1]),
                        deps=[tok_mm, t_bq, war_pair],
                        inc="sact" if s4 == 3 else None)
                tok_q_ev[hd] = tev
                if hd % 2 == 1:
                    tsp = None
                    for s4 in range(4):
                        tsp = E("act", lambda e, hd=hd, s4=s4: e.dma_start(
                            qts2[s4, hd // 2],
                            qpair[:, 1024 * s4:1024 * s4 + 1024]),
                            deps=[tok_q_ev[hd], tok_q_ev[hd - 1]],
                            inc="qact", inc_by=16)
                    tok_q_spill[hd] = tsp
                    tok_q_spill[hd - 1] = tsp

            if upto == "b":
                t1 = E("sp", lambda e: e.dma_start(dbg["d_qts"], qts2),
                       deps=[tok_q_spill[H - 1]], inc="qsp", inc_by=16)
                t2 = E("sp", lambda e: e.dma_start(dbg["d_xns"], xns),
                       deps=[tok_xns_done], inc="qsp", inc_by=16)
                E("sp", lambda e: None, deps=[t1, t2] + sch.qbar("qact"))
                return

            qtb = [AR2[:, 0:1024], AR2[:, 1024:2048]]
            pg = [AR2[:, 2048 + 1024 * i:2048 + 1024 * i + 1024] for i in range(6)]
            aev = [AR2[:, 8192 + 512 * j:8192 + 512 * j + 512] for j in range(2)]
            lrow = AR2f[0:1, 4608:5632]
            sT = [ps_A[:, 0:1024], ps_A[:, 1024:2048]]
            accs_c = [ps_B[:, 0:512], ps_B[:, 512:1024]]
            lacc = [ps_B[0:1, 1024:1536], ps_B[0:1, 1536:2048]]

            dep_c_ar2 = [tok_q_stop[H - 1], tok_q_stop[H - 2],
                         tok_q_spill[H - 1], tok_q_spill[H - 2]]

            tok_mask = None
            tok_mask_use = None
            tok_qtb = {}
            tok_qk_last = {}
            tok_exp_tok = {}
            tok_pg_free = {}
            for _pp in range(6):
                tok_pg_free[_pp] = list(dep_c_ar2)
            tok_sT_free = [tok_q_ev[H - 2], tok_q_ev[H - 2]]
            tok_aev_dma = {}
            tok_lts_dma = {}
            tok_lev = None
            tok_aev_ev = [tok_q_ev[H - 1], tok_q_ev[H - 1]]
            tok_lacc_ev = [tok_q_ev[H - 1], tok_q_ev[H - 1]]

            for s in range(4):
                L = LIMS[s]
                tok_mask = E("sp", lambda e, s=s: e.dma_start(
                    mask_sb[:].rearrange("p (j f) -> p j f", j=8),
                    masks[s]),
                    deps=[tok_mask_use] + (dep_c_ar2 if s == 0 else []),
                    inc="qsp", inc_by=16)
                for hp in range(16):
                    h0 = hp * 2
                    Pq = hp % 2
                    gi = s * 16 + hp
                    tok_qtb[gi] = E("sp", lambda e, hp=hp, s=s, Pq=Pq: e.dma_start(
                        qtb[Pq], qts2[s, hp]),
                        deps=[tok_q_spill[h0 + 1], tok_qk_last.get(gi - 2)] +
                             (dep_c_ar2 if gi < 2 else []),
                        inc="qsp", inc_by=16)
                    npairs = (L + 1) // 2
                    tok_av = {}
                    tok_masked = {}
                    LAG = 2
                    for pr in range(npairs + LAG):
                        if pr < npairs:
                            for kc in (2 * pr, 2 * pr + 1):
                                if kc >= L:
                                    continue
                                Ps = kc % 2
                                qk_tok = None
                                for h in range(2):
                                    qk_tok = E("pe",
                                        lambda e, kc=kc, h=h, Ps=Ps, Pq=Pq: e.matmul(
                                            sT[Ps][:, 512 * h:512 * h + 512],
                                            kT_sb[:, 128 * kc:128 * kc + 128],
                                            qtb[Pq][:, 512 * h:512 * h + 512],
                                            start=True, stop=True),
                                        deps=[tok_qtb[gi], tok_kT_done,
                                              tok_sT_free[Ps]],
                                        inc="spe" if h == 1 else None)
                                Pp = kc % 6
                                texp = E("act", lambda e, Pp=Pp, Ps=Ps: e.activation(
                                    pg[Pp][:], sT[Ps][:], AF.Exp, scale=SM_SCALE),
                                    deps=[qk_tok] + tok_pg_free.get(Pp, []),
                                    inc="sact" if (kc % 2 == 1 or kc == L - 1)
                                    else None)
                                if kc % 2 == 1 or kc == L - 1:
                                    tok_exp_tok[pr] = texp
                                    tok_sT_free[0] = texp
                                    tok_sT_free[1] = texp
                        if pr >= LAG:
                            for kc in (2 * (pr - LAG), 2 * (pr - LAG) + 1):
                                if kc >= L:
                                    continue
                                Pp = kc % 6
                                if kc >= L - 8:
                                    j = kc - (L - 8)
                                    tmm = None
                                    for h in range(2):
                                        tmm = E("dve",
                                            lambda e, h=h, j=j, Pp=Pp: e.tensor_tensor(
                                                pg[Pp][:, 512 * h:512 * h + 512],
                                                pg[Pp][:, 512 * h:512 * h + 512],
                                                mask_sb[:, 512 * j:512 * j + 512],
                                                ALU.mult),
                                            deps=[tok_exp_tok[pr - LAG], tok_mask],
                                            inc="sdve" if h == 1 else None)
                                    tok_masked[kc] = tmm
                                    tok_mask_use = tmm
                                dep_p = [tok_exp_tok[pr - LAG], tok_masked.get(kc)]
                                for h in range(2):
                                    E("pe", lambda e, h=h, kc=kc, Pp=Pp, L=L: e.matmul(
                                        lacc[h], onescolb_sb[:],
                                        pg[Pp][:, 512 * h:512 * h + 512],
                                        start=(kc == 0), stop=(kc == L - 1)),
                                        deps=dep_p + [t_ocb, tok_lacc_ev[h]],
                                        inc=None)
                                for h in range(2):
                                    tok_av[(kc, h)] = E("pe",
                                        lambda e, h=h, kc=kc, Pp=Pp, L=L: e.matmul(
                                            accs_c[h],
                                            v_sb[:, 128 * kc:128 * kc + 128],
                                            pg[Pp][:, 512 * h:512 * h + 512],
                                            start=(kc == 0), stop=(kc == L - 1)),
                                        deps=dep_p + [tok_v_done, tok_aev_ev[h]],
                                        inc="spe")
                                tok_pg_free[Pp] = [tok_av[(kc, 1)]]
                    tok_qk_last[gi] = tok_exp_tok[npairs - 1]
                    for h in range(2):
                        E("dve", lambda e, h=h: e.tensor_copy(
                            aev[h], accs_c[h]),
                            deps=[tok_av[(L - 1, h)],
                                  tok_aev_dma.get((gi - 1, h))], inc=None)
                    for h in range(2):
                        tok_lev = E("dve", lambda e, h=h: e.tensor_copy(
                            lrow[:, 512 * h:512 * h + 512], lacc[h]),
                            deps=[tok_av[(L - 1, 1)], tok_lts_dma.get(gi - 1)],
                            inc="sdve" if h == 1 else None)
                    for h in range(2):
                        tok_aev_ev[h] = tok_lev
                        tok_lacc_ev[h] = tok_lev
                        tok_aev_dma[(gi, h)] = E("sp",
                            lambda e, h=h, h0=h0, s=s: e.dma_start(
                                ats[h0 + h, s], aev[h]),
                            deps=[tok_lev], inc="qsp", inc_by=16)
                    tok_lts_dma[gi] = E("sp", lambda e, h0=h0, s=s: e.dma_start(
                        lts[h0:h0 + 2, s].unsqueeze(0),
                        lrow[:].rearrange("p (s2 f) -> p s2 f", s2=2)),
                        deps=[tok_lev], inc="qsp", inc_by=16)

            t_bo = E("sp", lambda e: e.dma_start(bo_sb[:], bo), inc="qsp", inc_by=16)
            t_bf = E("sp", lambda e: e.dma_start(bf_sb[:], bfb), inc="qsp", inc_by=16)
            tok_bsum = E("dve", lambda e: e.tensor_tensor(
                bsum[:], bo_sb[:], bf_sb[:], ALU.add), deps=[t_bo, t_bf], inc="sdve")

            bar_pe = sch.bar("spe")
            bar_act = sch.bar("sact")
            bar_dve = sch.bar("sdve")
            bar_pool = sch.bar("spool")
            bar_qact_l = sch.qbar("qact")
            bar_qsp_l = sch.qbar("qsp")

            wcb = [AR2[:, 0:4096], AR2[:, 4096:8192], AR2[:, 8192:12288]]
            wB_t = [AR2[:, 12288:12416], AR2[:, 12416:12544]]
            tp_h = AR2[:, 12544:13568]
            linv_bc = AR2[:, 13568:14592]
            lrowD = AR3[0:1, 4096:5120]
            linv_row = AR2[0:1, 14592:15616]
            xst = [AR3[:, 0:1024], AR3[:, 1024:2048]]
            acts = [AR3[:, 2048:3072], AR3[:, 3072:4096]]
            accs = [ps_A[:, 0:1024], ps_A[:, 1024:2048], ps_B[:, 0:1024]]
            t_ps = ps_B[:, 1024:2048]

            tok_ath = {}
            tok_lath = {}
            tok_xrh = {}
            tok_wcb = {}
            tok_wBt = {}
            tok_acc_stop = {}
            tok_acts = {}
            tok_dadd = {}
            tok_out = {}
            tok_tp_ev = None
            tok_tp_stop = None

            dep_d_ps = [tok_lev, tok_sT_free[0]]

            for hh in range(2):
                dep_half = [tok_acc_stop.get((hh - 1, NCH - 1))] if hh > 0 else \
                    [tok_q_stop[H - 1], tok_xns_done]
                t_wA = E("pool", lambda e: e.dma_start(
                    AR2[:, 0:4096].rearrange("p (c f) -> p c f", c=NCH),
                    wA),
                    deps=dep_half + ([bar_pe, bar_act, bar_dve] + bar_qact_l +
                                     bar_qsp_l if hh == 0 else []),
                    inc="qpool", inc_by=16)
                for h in range(H):
                    tok_ath[(hh, h)] = E("sp", lambda e, hh=hh, h=h: e.dma_start(
                        AR1[:, 1024 * h:1024 * h + 1024].rearrange(
                            "p (s f) -> p s f", s=2),
                        ats[h, 2 * hh:2 * hh + 2].transpose([1, 0, 2])),
                        deps=[tok_aev_dma[(2 * hh * 16 + h // 2, h % 2)],
                              tok_aev_dma[((2 * hh + 1) * 16 + h // 2, h % 2)],
                              tok_lath.get((hh - 1, h))] +
                             dep_half,
                        inc="qsp", inc_by=16)
                for d in range(NCH):
                    tok_xrh[(hh, d)] = E("sp", lambda e, hh=hh, d=d: e.dma_start(
                        AR1[:, 32768 + 1024 * d:32768 + 1024 * d + 1024].rearrange(
                            "p (s f) -> p s f", s=2),
                        xns[d, 2 * hh:2 * hh + 2].transpose([1, 0, 2])),
                        deps=[tok_xns_done] + dep_half, inc="qsp", inc_by=16)

                tok_expl = None
                tok_bc_prev = None
                tok_bc_ev = dep_d_ps
                for h in range(H):
                    t_lr = E("sp", lambda e, hh=hh, h=h: e.dma_start(
                        lrowD.rearrange("p (s f) -> p s f", s=2),
                        lts[h, 2 * hh:2 * hh + 2].unsqueeze(0)),
                        deps=[tok_lts_dma[(2 * hh) * 16 + h // 2],
                              tok_lts_dma[(2 * hh + 1) * 16 + h // 2],
                              tok_expl] + dep_half,
                        inc="qsp", inc_by=16)
                    tok_ln_l = E("act", lambda e: e.activation(
                        lrowD, lrowD, AF.Ln),
                        deps=[t_lr], inc=None)
                    tok_expl = E("act", lambda e: e.activation(
                        linv_row, lrowD, AF.Exp, scale=-1.0),
                        deps=[tok_bc_prev], inc="sact")
                    tok_bc = None
                    for tb in range(2):
                        tok_bc = E("pe", lambda e, tb=tb: e.matmul(
                            t_ps[:, 512 * tb:512 * tb + 512],
                            onesrowb_sb[:],
                            linv_row[:, 512 * tb:512 * tb + 512],
                            start=True, stop=True),
                            deps=[tok_expl, t_orb] + tok_bc_ev + dep_d_ps,
                            inc="spe" if tb == 1 else None)
                    tok_bc_prev = tok_bc
                    tok_ev = E("dve", lambda e: e.tensor_copy(
                        linv_bc[:], t_ps[:]),
                        deps=[tok_bc], inc=None)
                    tok_lath[(hh, h)] = E("dve", lambda e, h=h: e.tensor_tensor(
                        AR1[:, 1024 * h:1024 * h + 1024],
                        AR1[:, 1024 * h:1024 * h + 1024], linv_bc[:], ALU.mult),
                        deps=[tok_ath[(hh, h)]], inc="sdve")
                    tok_bc_ev = [tok_lath[(hh, h)]]

                for d in range(NCH):
                    for tb in range(2):
                        tok_tp_stop = E("pe", lambda e, d=d, tb=tb: e.matmul(
                            t_ps[:, 512 * tb:512 * tb + 512],
                            AR2[:, 128 * d:128 * d + 128],
                            AR1[:, 32768 + 1024 * d + 512 * tb:
                                32768 + 1024 * d + 512 * tb + 512],
                            start=(d == 0), stop=(d == NCH - 1)),
                            deps=[t_wA, tok_xrh[(hh, d)], tok_tp_ev,
                                  tok_bc_ev[0]] + dep_d_ps,
                            inc="spe" if (d == NCH - 1 and tb == 1) else None)
                tok_tp_ev = E("dve", lambda e: e.tensor_scalar(
                    tp_h[:], t_ps[:], rscale[:], None, ALU.mult),
                    deps=[tok_tp_stop, tok_rscale,
                          *(bar_qact_l if hh == 0 else [None])], inc="sdve")

                for o in range(NCH):
                    u0 = (2 * o) % 3
                    u1 = (2 * o + 1) % 3
                    dep_w0 = [tok_acc_stop.get((hh, o - 2))]
                    if hh == 0 and o == 0:
                        dep_w0 += [tok_tp_stop]
                    if hh == 1 and o == 0:
                        dep_w0 += [tok_tp_stop, tok_acc_stop.get((0, NCH - 1))]
                    if hh == 0 and o == 1:
                        dep_w0 += bar_qact_l + [bar_dve]
                    tok_wcb[(hh, o, 0)] = E("pool", lambda e, o=o, u0=u0: e.dma_start(
                        wcb[u0].rearrange("p (c f) -> p c f", c=NCH),
                        wo[o]),
                        deps=dep_w0, inc="qpool", inc_by=16)
                    tok_wcb[(hh, o, 1)] = E("pool", lambda e, o=o, u1=u1: e.dma_start(
                        wcb[u1].rearrange("p (c f) -> p c f", c=NCH),
                        wW[o]),
                        deps=[tok_acc_stop.get((hh, o - 1)),
                              (tok_acc_stop.get((0, NCH - 1))
                               if (hh == 1 and o == 0) else None)] +
                             ([bar_pe, bar_dve, bar_pool]
                              if (hh == 0 and o == 0) else []),
                        inc="qpool", inc_by=16)
                    tok_wBt[(hh, o)] = E("pool", lambda e, o=o: e.dma_start(
                        wB_t[o % 2], wB[o]),
                        deps=[tok_acc_stop.get((hh, o - 2)),
                              *(bar_qact_l if (hh == 0 and o <= 1) else [None]),
                              (tok_acc_stop.get((0, 30 + o))
                               if (hh == 1 and o < 2) else None)],
                        inc="qpool", inc_by=16)
                    acc = accs[o % 3]
                    tok_mm = None
                    for h in range(H):
                        for tb in range(2):
                            tok_mm = E("pe",
                                lambda e, h=h, tb=tb, u0=u0, acc=acc: e.matmul(
                                    acc[:, 512 * tb:512 * tb + 512],
                                    wcb[u0][:, 128 * h:128 * h + 128],
                                    AR1[:, 1024 * h + 512 * tb:
                                        1024 * h + 512 * tb + 512],
                                    start=(h == 0), stop=False),
                                deps=[tok_wcb[(hh, o, 0)], tok_lath[(hh, h)],
                                      tok_acts.get((hh, o - 3)),
                                      (tok_acts.get((hh - 1, [30, 31, 29][o]))
                                       if (hh > 0 and o < 3) else None)],
                                inc=None)
                    for d in range(NCH):
                        for tb in range(2):
                            tok_mm = E("pe",
                                lambda e, d=d, tb=tb, u1=u1, acc=acc: e.matmul(
                                    acc[:, 512 * tb:512 * tb + 512],
                                    wcb[u1][:, 128 * d:128 * d + 128],
                                    AR1[:, 32768 + 1024 * d + 512 * tb:
                                        32768 + 1024 * d + 512 * tb + 512],
                                    start=False, stop=False),
                                deps=[tok_wcb[(hh, o, 1)], tok_xrh[(hh, d)]],
                                inc=None)
                    for tb in range(2):
                        tok_mm = E("pe", lambda e, tb=tb, o=o, acc=acc: e.matmul(
                            acc[:, 512 * tb:512 * tb + 512], wB_t[o % 2][:],
                            tp_h[:, 512 * tb:512 * tb + 512],
                            start=False, stop=True),
                            deps=[tok_wBt[(hh, o)], tok_tp_ev],
                            inc="spe" if tb == 1 else None)
                    tok_acc_stop[(hh, o)] = tok_mm
                    Po = o % 2
                    prev_out = tok_out.get((hh, o - 2)) if o >= 2 else (
                        tok_out.get((hh - 1, NCH - 2 + o)) if hh > 0 else None)
                    tok_acts[(hh, o)] = E("act",
                        lambda e, o=o, Po=Po, acc=acc: e.activation(
                            acts[Po], acc[:], AF.Identity, bias=bsum[:, o:o + 1]),
                        deps=[tok_mm, tok_bsum, prev_out,
                              (bar_dve if (hh == 0 and o < 2) else None)],
                        inc="sact")
                    prev_xst = tok_dadd.get((hh, o - 2)) if o >= 2 else (
                        tok_dadd.get((hh - 1, NCH - 2 + o)) if hh > 0 else
                        (bar_pe if o < 2 else None))
                    t_xs = E("sp", lambda e, hh=hh, o=o, Po=Po: e.dma_start(
                        xst[Po].rearrange("p (s f) -> p s f", s=2),
                        xT_own[2 * hh:2 * hh + 2, o].transpose([1, 0, 2])),
                        deps=[prev_xst,
                              (sch.bar("spe") if (hh == 0 and o < 2) else None)],
                        inc="qsp", inc_by=16)
                    tok_dadd[(hh, o)] = E("dve", lambda e, Po=Po: e.tensor_tensor(
                        acts[Po], acts[Po], xst[Po], ALU.add),
                        deps=[tok_acts[(hh, o)], t_xs], inc="sdve")
                    tok_out[(hh, o)] = E("act", lambda e, hh=hh, o=o, Po=Po: e.dma_start(
                        out_t[o, 2 * hh:2 * hh + 2].transpose([1, 0, 2]),
                        acts[Po].rearrange("p (s f) -> p s f", s=2)),
                        deps=[tok_dadd[(hh, o)]], inc="qact", inc_by=16)

            fin = sch.qbar("qact") + sch.qbar("qsp") + sch.qbar("qpool")
            E("sp", lambda e: None, deps=fin)

        _emit_phases()
        build_program._last_sched = sch

        _flush_op = {
            "act": lambda e: e.copy(scr_act[:], scr_act[:]),
            "dve": lambda e: e.tensor_copy(scr_dve[:], scr_dve[:]),
            "pool": lambda e: e.tensor_copy(scr_pool[:], scr_pool[:]),
        }
        _flush_sems = ("sact", "sdve", "spool")

        def run_stream(name, eng):
            for wl, fn, inc, inc_by in sch.streams[name]:
                for snm, v in wl:
                    eng.wait_ge(sems[snm], v)
                ins = fn(eng)
                if inc is not None:
                    if name in _flush_op and inc in _flush_sems:
                        eng.drain(fusable=False)
                        _flush_op[name](eng).then_inc(sems[inc], inc_by)
                    else:
                        ins.then_inc(sems[inc], inc_by)

        @block.tensor
        def _(eng):
            run_stream("pe", eng)

        @block.scalar
        def _(eng):
            run_stream("act", eng)

        @block.vector
        def _(eng):
            run_stream("dve", eng)

        @block.gpsimd
        def _(eng):
            run_stream("pool", eng)

        @block.sync
        def _(eng):
            run_stream("sp", eng)

    return nc



_NC_CACHE = {}


def _get_program():
    if "nc" not in _NC_CACHE:
        _NC_CACHE["nc"] = build_program()
    return _NC_CACHE["nc"]


def _pack_weights(inp):
    f32 = np.float32
    wq_w = np.asarray(inp["L0_attn_wq_w"], f32)
    wk_w = np.asarray(inp["L0_attn_wk_w"], f32)
    wv_w = np.asarray(inp["L0_attn_wv_w"], f32)
    wo_w = np.asarray(inp["L0_attn_wo_w"], f32)
    W = np.asarray(inp["L0_ffn_W"], f32)
    A = np.asarray(inp["L0_ffn_A"], f32)
    Bm = np.asarray(inp["L0_ffn_B"], f32)
    rW = np.asarray(inp["L0_ffn_router_W"], f32)

    def colblocks(w):
        wt = np.ascontiguousarray(w.T)
        return np.ascontiguousarray(
            wt.reshape(NCH, 128, NCH, 128).transpose(2, 0, 1, 3))

    def dblocks(w):
        wt = np.ascontiguousarray(w.T)
        return np.ascontiguousarray(wt.reshape(NCH, 128, w.shape[0]))

    def cb_t(w):
        return np.ascontiguousarray(colblocks(w).transpose(0, 2, 1, 3))

    def db_t(w):
        return np.ascontiguousarray(dblocks(w).transpose(1, 0, 2))

    d = {}
    d["wq"] = cb_t(wq_w)
    d["wo"] = cb_t(wo_w)
    d["wW"] = cb_t(W)
    d["wk"] = db_t(wk_w)
    d["wv"] = db_t(wv_w)
    A_all = A.reshape(128, D)
    d["wA"] = db_t(A_all)
    B_all = Bm.transpose(0, 2, 1).reshape(128, D)
    d["wB"] = np.ascontiguousarray(
        B_all.reshape(128, NCH, 128).transpose(1, 0, 2))
    d["wr"] = db_t(rW)
    d["bq"] = np.ascontiguousarray(
        np.asarray(inp["L0_attn_wq_b"], f32).reshape(NCH, 128).T)
    d["bk"] = np.asarray(inp["L0_attn_wk_b"], f32).reshape(128, 1)
    d["bv"] = np.asarray(inp["L0_attn_wv_b"], f32).reshape(128, 1)
    d["bo"] = np.ascontiguousarray(
        np.asarray(inp["L0_attn_wo_b"], f32).reshape(NCH, 128).T)
    d["bfb"] = np.ascontiguousarray(
        np.asarray(inp["L0_ffn_b"], f32).reshape(NCH, 128).T)
    d["br"] = np.asarray(inp["L0_ffn_router_b"], f32).reshape(1, 8)
    d["identf"] = np.eye(128, dtype=f32)
    d["onescol_f"] = np.ones((128, 1), f32)
    d["onescol_bf"] = np.ones((128, 1), ml_dtypes.bfloat16)
    d["onesrow_f"] = np.ones((1, 128), f32)
    d["onesrow_b"] = np.ones((1, 128), ml_dtypes.bfloat16)
    d["ones1_b"] = np.ones((1, 1), ml_dtypes.bfloat16)
    sel = np.zeros((8, 128), f32)
    for e_ in range(8):
        sel[e_, e_ * 16:(e_ + 1) * 16] = 2.0
    d["sel_b"] = sel.astype(ml_dtypes.bfloat16)
    return d


def _build_masks(own_supers):
    m = np.zeros((4, 8, 128, 512), np.float32)
    tri = np.triu(np.ones((128, 128), np.float32))
    for s, g in enumerate(own_supers):
        L = LIMS[s]
        for j in range(8):
            kc = L - 8 + j
            for qt in range(4):
                gq = 4 * g + qt
                if kc < gq:
                    m[s, j, :, qt * 128:(qt + 1) * 128] = 1.0
                elif kc == gq:
                    m[s, j, :, qt * 128:(qt + 1) * 128] = tri
    return np.ascontiguousarray(m.transpose(0, 2, 1, 3)).astype(ml_dtypes.bfloat16)


def kernel(**inputs):
    x = np.asarray(inputs["L0_attn_x"], np.float32)
    wd = _pack_weights(inputs)
    masks_by_p = {p: _build_masks(OWN_SUPER[p]) for p in (0, 1)}

    in_maps = []
    for c in range(8):
        b, p = c // 2, c % 2
        owns = OWN_SUPER[p]
        xb = np.ascontiguousarray(x[b])
        xTb = np.ascontiguousarray(xb.T)
        xT_blocks = np.ascontiguousarray(
            xTb.reshape(NCH, 128, NW, 512).transpose(2, 0, 1, 3))
        own_rows = np.concatenate(
            [np.arange(512 * g, 512 * (g + 1)) for g in owns])
        xtok_own = np.ascontiguousarray(xb[own_rows])
        xT_own = np.ascontiguousarray(xT_blocks[owns])
        m = {"x_glob": xb, "xtok_own": xtok_own, "xT_glob": xT_blocks,
             "xT_own": xT_own, "masks": masks_by_p[p]}
        m.update(wd)
        in_maps.append(m)

    nc = _get_program()
    res = run_bass_kernel_spmd(nc, in_maps, core_ids=list(range(8)))

    out = np.empty((4, 4096, 4096), np.float32)
    for c in range(8):
        b, p = c // 2, c % 2
        owns = OWN_SUPER[p]
        r = res.results[c]["out"]
        arr = np.ascontiguousarray(r.transpose(1, 3, 0, 2)).reshape(4, 512, 4096)
        for wi, g in enumerate(owns):
            out[b, 512 * g:512 * (g + 1), :] = arr[wi]
    return out


if __name__ == "__main__":
    import reference
    inp = reference.setup_inputs()
    o = kernel(**{k: np.asarray(v) for k, v in inp.items()})
    ref = np.asarray(reference.reference(**inp))
    err = np.abs(o - ref)
    rel = np.sqrt(((o - ref) ** 2).mean() / (ref ** 2).mean())
    print("max abs err:", err.max(), "rel l2:", rel)


# version 07
# speedup vs baseline: 1.2843x; 1.0080x over previous
import numpy as np
import ml_dtypes
import concourse.bass as bass
import concourse.mybir as mybir
from concourse.bass_utils import run_bass_kernel_spmd

F32 = mybir.dt.float32
BF16 = mybir.dt.bfloat16
AF = mybir.ActivationFunctionType
ALU = mybir.AluOpType
AX = mybir.AxisListType

D = 4096
T = 4096
HD = 128
H = 32
NCH = 32
NW = 8
OWN_W = 4
OT = 2048
LIMS = [8, 16, 24, 32]
OWN_SUPER = {0: [0, 3, 4, 7], 1: [1, 2, 5, 6]}
EPS = 1e-5
SM_SCALE = float(1.0 / np.sqrt(128.0))
Q_LO, Q_HI = 611, 4095
NTOK_R = Q_HI - Q_LO


class Sched:

    ENGINES = ("pe", "act", "dve", "pool", "sp")

    NQROT = 8

    def __init__(self):
        self.streams = {e: [] for e in self.ENGINES}
        self.cnt = {}
        self.lastw = {}
        self.qrot = {}

    def emit(self, eng, fn, deps=(), inc=None, inc_by=1):
        wl = []
        for tok in deps:
            if tok is None:
                continue
            s, v = tok
            if v > self.lastw.get((eng, s), 0):
                wl.append((s, v))
                self.lastw[(eng, s)] = v
        tok = None
        if inc is not None:
            if inc in ("qsp", "qact", "qpool"):
                r = self.qrot.get(inc, 0)
                self.qrot[inc] = (r + 1) % self.NQROT
                inc = f"{inc}{r}"
            self.cnt[inc] = self.cnt.get(inc, 0) + inc_by
            tok = (inc, self.cnt[inc])
        self.streams[eng].append((wl, fn, inc, inc_by))
        return tok

    def qbar(self, prefix):
        return [(f"{prefix}{r}", self.cnt.get(f"{prefix}{r}", 0))
                for r in range(self.NQROT)]

    def bar(self, name):
        return (name, self.cnt.get(name, 0))


class _Stop(Exception):
    pass


def build_program(upto="d"):
    nc = bass.Bass("TRN2", target_bir_lowering=False)

    def din(name, shape, dt=F32):
        return nc.dram_tensor(name, list(shape), dt, kind="ExternalInput").ap()

    x_glob = din("x_glob", (T, D))
    xtok_own = din("xtok_own", (OT, D))
    xT_glob = din("xT_glob", (NW, NCH, 128, 512))
    xT_own = din("xT_own", (OWN_W, NCH, 128, 512))
    wq = din("wq", (NCH, 128, NCH, 128))
    wk = din("wk", (128, NCH, 128))
    wv = din("wv", (128, NCH, 128))
    wo = din("wo", (NCH, 128, NCH, 128))
    wW = din("wW", (NCH, 128, NCH, 128))
    wA = din("wA", (128, NCH, 128))
    wB = din("wB", (NCH, 128, 128))
    wr = din("wr", (128, NCH, 8))
    bq = din("bq", (128, 32))
    bk = din("bk", (128, 1))
    bv = din("bv", (128, 1))
    bo = din("bo", (128, 32))
    bfb = din("bfb", (128, 32))
    br = din("br", (1, 8))
    identf = din("identf", (128, 128))
    onescol_f = din("onescol_f", (128, 1))
    onescol_bf = din("onescol_bf", (128, 1), BF16)
    onesrow_f = din("onesrow_f", (1, 128))
    onesrow_b = din("onesrow_b", (1, 128), BF16)
    ones1_b = din("ones1_b", (1, 1), BF16)
    sel_b = din("sel_b", (8, 128), BF16)
    masks = din("masks", (4, 128, 8, 512), BF16)

    out_t = nc.dram_tensor("out", [NCH, OWN_W, 128, 512], F32,
                           kind="ExternalOutput").ap()
    dbg = {}
    if upto != "d":
        if upto == "a0":
            dbg["d_own"] = nc.dram_tensor("d_own", [128, 4096], F32, kind="ExternalOutput").ap()
            dbg["d_glob"] = nc.dram_tensor("d_glob", [128, 8192], F32, kind="ExternalOutput").ap()
        if upto == "a1":
            dbg["d_kT"] = nc.dram_tensor("d_kT", [128, 4096], BF16, kind="ExternalOutput").ap()
            dbg["d_v"] = nc.dram_tensor("d_v", [128, 4096], BF16, kind="ExternalOutput").ap()
            dbg["d_qs"] = nc.dram_tensor("d_qs", [128, 32], F32, kind="ExternalOutput").ap()
        if upto == "rt":
            dbg["d_rsc"] = nc.dram_tensor("d_rsc", [128, 1], F32, kind="ExternalOutput").ap()
            dbg["d_rsm"] = nc.dram_tensor("d_rsm", [1, 64], F32, kind="ExternalOutput").ap()
        if upto == "b":
            dbg["d_qts"] = nc.dram_tensor("d_qts", [4, 16, 128, 1024], BF16, kind="ExternalOutput").ap()
            dbg["d_xns"] = nc.dram_tensor("d_xns", [NCH, OWN_W, 128, 512], BF16, kind="ExternalOutput").ap()
        if upto == "c":
            dbg["d_ats"] = nc.dram_tensor("d_ats", [H, 4, 128, 512], BF16, kind="ExternalOutput").ap()
            dbg["d_pacc"] = nc.dram_tensor("d_pacc", [128, 2048], F32, kind="ExternalOutput").ap()
            dbg["d_linv"] = nc.dram_tensor("d_linv", [128, 512], F32, kind="ExternalOutput").ap()
            dbg["d_pg"] = nc.dram_tensor("d_pg", [128, 2048], BF16, kind="ExternalOutput").ap()
    xns = nc.dram_tensor("xns", [NCH, OWN_W, 128, 512], BF16, kind="Internal").ap()
    qts2 = nc.dram_tensor("qts", [4, 16, 128, 1024], BF16, kind="Internal").ap()
    ats = nc.dram_tensor("ats", [H, 4, 128, 512], BF16, kind="Internal").ap()
    lts = nc.dram_tensor("lts", [H, 4, 512], F32, kind="Internal").ap()

    sch = Sched()
    E = sch.emit

    import contextlib
    with contextlib.ExitStack() as _st:
        AR1_t = _st.enter_context(nc.sbuf_tensor("AR1", [128, 65536], BF16))
        AR2_t = _st.enter_context(nc.sbuf_tensor("AR2", [128, 16384], BF16))
        AR3_t = _st.enter_context(nc.sbuf_tensor("AR3", [128, 5120], F32))
        kT_sb = _st.enter_context(nc.sbuf_tensor("kT_sb", [128, 4096], BF16))
        v_sb = _st.enter_context(nc.sbuf_tensor("v_sb", [128, 4096], BF16))
        mask_sb = _st.enter_context(nc.sbuf_tensor("mask_sb", [128, 4096], BF16))
        stats_sb = _st.enter_context(nc.sbuf_tensor("stats_sb", [128, 48], F32))
        aggr_sb = _st.enter_context(nc.sbuf_tensor("aggr_sb", [128, 2], F32))
        lncol = _st.enter_context(nc.sbuf_tensor("lncol", [128, 1], F32))
        rcol = _st.enter_context(nc.sbuf_tensor("rcol", [128, 1], F32))
        rmcol = _st.enter_context(nc.sbuf_tensor("rmcol", [128, 1], F32))
        qs_sb = _st.enter_context(nc.sbuf_tensor("qs_sb", [128, 32], F32))
        qtmp = _st.enter_context(nc.sbuf_tensor("qtmp", [128, 4], F32))
        brsb = _st.enter_context(nc.sbuf_tensor("brsb", [1, 8], F32))
        rsm = _st.enter_context(nc.sbuf_tensor("rsm", [1, 64], F32))
        routing_b = _st.enter_context(nc.sbuf_tensor("routing_b", [1, 8], BF16))
        rtcol_b = _st.enter_context(nc.sbuf_tensor("rtcol_b", [8, 1], BF16))
        rscale = _st.enter_context(nc.sbuf_tensor("rscale", [128, 1], F32))
        bq_sb = _st.enter_context(nc.sbuf_tensor("bq_sb", [128, 32], F32))
        bk_sb = _st.enter_context(nc.sbuf_tensor("bk_sb", [128, 1], F32))
        bv_sb = _st.enter_context(nc.sbuf_tensor("bv_sb", [128, 1], F32))
        bo_sb = _st.enter_context(nc.sbuf_tensor("bo_sb", [128, 32], F32))
        bf_sb = _st.enter_context(nc.sbuf_tensor("bf_sb", [128, 32], F32))
        bsum = _st.enter_context(nc.sbuf_tensor("bsum", [128, 32], F32))
        identf_sb = _st.enter_context(nc.sbuf_tensor("identf_sb", [128, 128], F32))
        onescol_sb = _st.enter_context(nc.sbuf_tensor("onescol_sb", [128, 1], F32))
        onescolb_sb = _st.enter_context(nc.sbuf_tensor("onescolb_sb", [128, 1], BF16))
        onesrow_sb = _st.enter_context(nc.sbuf_tensor("onesrow_sb", [1, 128], F32))
        onesrowb_sb = _st.enter_context(nc.sbuf_tensor("onesrowb_sb", [1, 128], BF16))
        ones1_sb = _st.enter_context(nc.sbuf_tensor("ones1_sb", [1, 1], BF16))
        sel_sb = _st.enter_context(nc.sbuf_tensor("sel_sb", [8, 128], BF16))
        epscol = _st.enter_context(nc.sbuf_tensor("epscol", [128, 1], F32))
        scr_act = _st.enter_context(nc.sbuf_tensor("scr_act", [1, 1], F32))
        scr_dve = _st.enter_context(nc.sbuf_tensor("scr_dve", [1, 1], F32))
        scr_pool = _st.enter_context(nc.sbuf_tensor("scr_pool", [1, 1], F32))
        ps_A = _st.enter_context(nc.psum_tensor([128, 2048], F32))
        ps_B = _st.enter_context(nc.psum_tensor([128, 2048], F32))
        qsems = {}
        for _qp in ("qsp", "qact", "qpool"):
            for _r in range(Sched.NQROT):
                qsems[f"{_qp}{_r}"] = _st.enter_context(
                    nc.semaphore(f"{_qp}{_r}"))
        sem_spe = _st.enter_context(nc.semaphore("spe"))
        sem_sact = _st.enter_context(nc.semaphore("sact"))
        sem_sdve = _st.enter_context(nc.semaphore("sdve"))
        sem_spool = _st.enter_context(nc.semaphore("spool"))
        block = _st.enter_context(nc.Block())
        sems = {"spe": sem_spe, "sact": sem_sact, "sdve": sem_sdve,
                "spool": sem_spool}
        sems.update(qsems)

        _phases_done = False
        AR1 = AR1_t[:]
        AR2 = AR2_t[:]
        AR3 = AR3_t[:]
        AR1f = AR1.bitcast(F32)
        AR2f = AR2.bitcast(F32)
        rowb = AR3_t[0:1, 4096:5120]
        lnl = AR2f[0:1, 5120:6144]
        wr_sb = AR2f[:, 6144:6400]

        def _emit_phases():
            tok_eps = E("dve", lambda e: e.memset(epscol[:], EPS), inc="sdve")
            t_idf = E("sp", lambda e: e.dma_start(identf_sb[:], identf), inc="qsp", inc_by=16)
            t_oc = E("sp", lambda e: e.dma_start(onescol_sb[:], onescol_f), inc="qsp", inc_by=16)
            t_ocb = E("sp", lambda e: e.dma_start(onescolb_sb[:], onescol_bf), inc="qsp", inc_by=16)
            t_or = E("sp", lambda e: e.dma_start(onesrow_sb[:], onesrow_f), inc="qsp", inc_by=16)
            t_orb = E("sp", lambda e: e.dma_start(onesrowb_sb[:], onesrow_b), inc="qsp", inc_by=16)
            t_o1 = E("sp", lambda e: e.dma_start(ones1_sb[:], ones1_b), inc="qsp", inc_by=16)
            t_sel = E("sp", lambda e: e.dma_start(sel_sb[:], sel_b), inc="qsp", inc_by=16)
            t_bk = E("sp", lambda e: e.dma_start(bk_sb[:], bk), inc="qsp", inc_by=16)
            t_bv = E("sp", lambda e: e.dma_start(bv_sb[:], bv), inc="qsp", inc_by=16)
            t_bq = E("sp", lambda e: e.dma_start(bq_sb[:], bq), inc="qsp", inc_by=16)
            t_wr = E("sp", lambda e: e.dma_start(
                wr_sb[:].rearrange("p (c e) -> p c e", c=NCH),
                wr), inc="qsp", inc_by=16)
            t_br = E("sp", lambda e: e.dma_start(brsb[:], br), inc="qsp", inc_by=16)

            def a0_pass(src_ap, ntiles, rbc, rmbc, war):
                src_v = src_ap.rearrange("(n p) d -> n p d", p=128)
                stage = [AR1f[:, 0:4096], AR1f[:, 4096:8192]]
                t_stage_rd = [war.get("stage0"), war.get("stage1")]
                t_row_ev1 = war.get("row1")
                t_row_ev2 = war.get("row2")
                t_bc_ev_r = war.get("bcr")
                t_bc_ev_m = war.get("bcm")
                t_rc_read = war.get("rc")
                t_rm_read = war.get("rm")
                rc_mm = [None] * 4
                rm_mm = [None] * 4
                for tt in range(ntiles):
                    P = tt % 2
                    g, j = tt // 4, tt % 4
                    st = stage[P]
                    t_x = E("sp", lambda e, st=st, tt=tt: e.dma_start(st, src_v[tt]),
                            deps=[t_stage_rd[P]], inc="qsp", inc_by=16)
                    tok_bn = None
                    for i in range(8):
                        tok_bn = E("dve", lambda e, st=st, i=i: e.bn_stats(
                            stats_sb[:, 6 * i:6 * i + 6], st[:, 512 * i:512 * i + 512]),
                            deps=[t_x], inc="sdve" if i == 7 else None)
                    t_stage_rd[P] = tok_bn
                    tok_ag = E("dve", lambda e: e.bn_aggr(aggr_sb[:], stats_sb[:]),
                               deps=[tok_bn], inc="sdve")
                    tok_ln = E("act", lambda e: e.activation(
                        lncol[:], aggr_sb[:, 1:2], AF.Ln, bias=epscol[:]),
                        deps=[tok_ag, tok_eps], inc="sact")
                    tok_r = E("act", lambda e: e.activation(
                        rcol[:], lncol[:], AF.Exp, scale=-0.5),
                        deps=[tok_ln, t_rc_read], inc="sact")
                    tok_rm = E("dve", lambda e: e.tensor_tensor(
                        rmcol[:], aggr_sb[:, 0:1], rcol[:], ALU.mult),
                        deps=[tok_r, t_rm_read], inc="sdve")
                    rc_mm[j] = E("pe", lambda e, j=j: e.matmul(
                        ps_A[0:1, 128 * j:128 * j + 128], rcol[:], identf_sb[:],
                        start=True, stop=True),
                        deps=[tok_r, t_idf, t_row_ev1], inc="spe")
                    rm_mm[j] = E("pe", lambda e, j=j: e.matmul(
                        ps_A[0:1, 512 + 128 * j:512 + 128 * j + 128], rmcol[:],
                        identf_sb[:], start=True, stop=True),
                        deps=[tok_rm, t_row_ev2], inc="spe")
                    t_rc_read = rc_mm[j]
                    t_rm_read = rm_mm[j]
                    if j == 3:
                        t_row_ev1 = E("act", lambda e: e.activation(
                            rowb[:, 0:512], ps_A[0:1, 0:512], AF.Copy),
                            deps=rc_mm, inc="sact")
                        t_row_ev2 = E("act", lambda e: e.activation(
                            rowb[:, 512:1024], ps_A[0:1, 512:1024], AF.Copy),
                            deps=rm_mm, inc="sact")
                        bc1 = E("pe", lambda e: e.matmul(
                            ps_A[:, 1024:1536], onesrow_sb[:], rowb[:, 0:512],
                            start=True, stop=True),
                            deps=[t_row_ev1, t_or, t_bc_ev_r], inc="spe")
                        bc2 = E("pe", lambda e: e.matmul(
                            ps_A[:, 1536:2048], onesrow_sb[:], rowb[:, 512:1024],
                            start=True, stop=True),
                            deps=[t_row_ev2, t_bc_ev_m], inc="spe")
                        t_bc_ev_r = E("act", lambda e, g=g: e.activation(
                            rbc[:, 512 * g:512 * g + 512], ps_A[:, 1024:1536],
                            AF.Copy), deps=[bc1], inc="sact")
                        t_bc_ev_m = E("act", lambda e, g=g: e.activation(
                            rmbc[:, 512 * g:512 * g + 512], ps_A[:, 1536:2048],
                            AF.Copy), deps=[bc2], inc="sact")
                return {"stage0": t_stage_rd[0], "stage1": t_stage_rd[1],
                        "row1": t_row_ev1, "row2": t_row_ev2,
                        "bcr": t_bc_ev_r, "bcm": t_bc_ev_m,
                        "rc": t_rc_read, "rm": t_rm_read}

            r_bc_own = AR3[:, 0:2048]
            rm_bc_own = AR3[:, 2048:4096]
            w0 = a0_pass(xtok_own, 16, r_bc_own, rm_bc_own, {})
            tok_own_r, tok_own_m = w0["bcr"], w0["bcm"]
            r_bc_g = AR1f[:, 20480:24576]
            rm_bc_g = AR1f[:, 24576:28672]
            w1 = a0_pass(x_glob, 32, r_bc_g, rm_bc_g, w0)
            tok_g_r, tok_g_m = w1["bcr"], w1["bcm"]
            if upto == "a0":
                t1 = E("sp", lambda e: e.dma_start(dbg["d_own"], AR3[:, 0:4096]),
                       deps=[tok_own_r, tok_own_m], inc="qsp", inc_by=16)
                t2 = E("sp", lambda e: e.dma_start(dbg["d_glob"], AR1f[:, 20480:28672]),
                       deps=[tok_g_r, tok_g_m], inc="qsp", inc_by=16)
                E("sp", lambda e: None, deps=[t1, t2])
                return

            t_wk = E("pool", lambda e: e.dma_start(
                AR2[:, 0:4096].rearrange("p (c f) -> p c f", c=NCH),
                wk), inc="qpool", inc_by=16)
            t_wv = E("pool", lambda e: e.dma_start(
                AR2[:, 4096:8192].rearrange("p (c f) -> p c f", c=NCH),
                wv), inc="qpool", inc_by=16)

            xn_w = [AR1[:, 0:16384], AR1[:, 16384:32768]]
            xstage = [AR1f[:, 16384 + 512 * k:16384 + 512 * k + 512] for k in range(6)]
            tmpf = [AR1f[:, 28672:29184], AR1f[:, 29184:29696]]
            vtmp = AR1f[:, 29696:30208]

            tok_blk = {}
            tok_kv_stop = {}
            tok_qsum_last = None
            tok_kev = None
            tok_vev = None
            tok_vtr_ev = [None, None]
            tok_xstage_rd = [w1["stage0"], w1["stage1"], None, None, None, None]
            blk_order = []
            tok_qs_wr = None
            tok_v_done = None

            tok_qsum_w = {}
            for w in range(NW):
                P = w % 2
                dep_war = [tok_kv_stop.get(w - 2), tok_qsum_w.get(w - 2)]
                pend = {"dve": [], "pool": []}
                for d in range(NCH):
                    i_lin = w * NCH + d
                    k3 = i_lin % 6
                    eng = "pool"
                    war6 = tok_blk.get(blk_order[i_lin - 6]) if i_lin >= 6 else \
                        tok_xstage_rd[k3]
                    blk_order.append((w, d))
                    t_x = E("sp", lambda e, w=w, d=d, k3=k3: e.dma_start(
                        xstage[k3], xT_glob[w, d]),
                        deps=[war6], inc="qsp", inc_by=16)
                    tp = d % 2
                    E(eng, lambda e, w=w, tp=tp, k3=k3: e.tensor_tensor(
                        tmpf[tp], xstage[k3], r_bc_g[:, 512 * w:512 * w + 512],
                        ALU.mult),
                        deps=[t_x, tok_g_r] + dep_war, inc=None)
                    do_inc = (d >= NCH - 2) or (d % 4 in (2, 3))
                    tok = E(eng, lambda e, w=w, d=d, P=P, tp=tp: e.tensor_tensor(
                        xn_w[P][:, 512 * d:512 * d + 512], tmpf[tp],
                        rm_bc_g[:, 512 * w:512 * w + 512], ALU.subtract),
                        deps=[tok_g_m],
                        inc=("sdve" if eng == "dve" else "spool") if do_inc else None)
                    if do_inc:
                        for dd in pend[eng] + [d]:
                            tok_blk[(w, dd)] = tok
                        pend[eng] = []
                    else:
                        pend[eng].append(d)
                tok_k_mm = None
                for d in range(NCH):
                    tok_k_mm = E("pe", lambda e, d=d, P=P: e.matmul(
                        ps_B[:, 0:512], AR2[:, 128 * d:128 * d + 128],
                        xn_w[P][:, 512 * d:512 * d + 512],
                        start=(d == 0), stop=(d == NCH - 1)),
                        deps=[tok_blk[(w, d)], t_wk, tok_kev],
                        inc="spe" if d == NCH - 1 else None)
                tok_kev = E("act", lambda e, w=w: e.activation(
                    kT_sb[:, 512 * w:512 * w + 512], ps_B[:, 0:512], AF.Identity,
                    bias=bk_sb[:]), deps=[tok_k_mm, t_bk], inc="sact")
                tok_v_mm = None
                for d in range(NCH):
                    tok_v_mm = E("pe", lambda e, d=d, P=P: e.matmul(
                        ps_B[:, 512:1024],
                        AR2[:, 4096 + 128 * d:4096 + 128 * d + 128],
                        xn_w[P][:, 512 * d:512 * d + 512],
                        start=(d == 0), stop=(d == NCH - 1)),
                        deps=[tok_blk[(w, d)], t_wv, tok_vev],
                        inc="spe" if d == NCH - 1 else None)
                tok_vev = E("act", lambda e: e.activation(
                    vtmp, ps_B[:, 512:1024], AF.Identity, bias=bv_sb[:]),
                    deps=[tok_v_mm, t_bv], inc="sact")
                for j in range(4):
                    sl = 1024 + 512 * (j % 2)
                    tok_tr = E("pe", lambda e, j=j, sl=sl: e.transpose(
                        ps_B[:, sl:sl + 128],
                        vtmp[:, 128 * j:128 * j + 128], identf_sb[:]),
                        deps=[tok_vev, tok_vtr_ev[j % 2]], inc="spe")
                    tok_vtr_ev[j % 2] = E("act", lambda e, w=w, j=j, sl=sl: e.activation(
                        v_sb[:, (4 * w + j) * 128:(4 * w + j) * 128 + 128],
                        ps_B[:, sl:sl + 128], AF.Copy),
                        deps=[tok_tr], inc="sact")
                tok_v_done = tok_vtr_ev[1]
                tok_kv_stop[w] = tok_v_mm
                if w >= 1:
                    lo = max(0, Q_LO - 512 * w)
                    hi = min(512, Q_HI - 512 * w)

                    def q_add(d, w=w, last=False):
                        return E("dve", lambda e, d=d, w=w: (
                            e.tensor_copy(qs_sb[:, d:d + 1], qtmp[:, d % 4:d % 4 + 1])
                            if w == 1 else
                            e.tensor_tensor(qs_sb[:, d:d + 1], qs_sb[:, d:d + 1],
                                            qtmp[:, d % 4:d % 4 + 1], ALU.add)),
                            deps=[], inc="sdve" if last else None)

                    for d in range(NCH):
                        E("dve", lambda e, d=d, P=P, lo=lo, hi=hi: e.reduce_sum(
                            qtmp[:, d % 4:d % 4 + 1],
                            xn_w[P][:, 512 * d + lo:512 * d + hi], axis=AX.X),
                            deps=[tok_blk[(w, d)]], inc=None)
                        if d >= 2:
                            q_add(d - 2)
                    q_add(NCH - 2)
                    tok_qs_wr = q_add(NCH - 1, last=True)
                    tok_qsum_w[w] = tok_qs_wr
            tok_qsum_last = tok_qs_wr
            tok_kT_done = tok_kev
            if upto == "a1":
                t1 = E("sp", lambda e: e.dma_start(dbg["d_kT"], kT_sb[:]),
                       deps=[tok_kT_done], inc="qsp", inc_by=16)
                t2 = E("sp", lambda e: e.dma_start(dbg["d_v"], v_sb[:]),
                       deps=[tok_v_done], inc="qsp", inc_by=16)
                t3 = E("sp", lambda e: e.dma_start(dbg["d_qs"], qs_sb[:]),
                       deps=[tok_qsum_last], inc="qsp", inc_by=16)
                E("sp", lambda e: None, deps=[t1, t2, t3])
                return

            tok_lps = None
            for d in range(NCH):
                tok_lps = E("pe", lambda e, d=d: e.matmul(
                    ps_A[0:1, 0:8], qs_sb[:, d:d + 1], wr_sb[:, 8 * d:8 * d + 8],
                    start=(d == 0), stop=(d == NCH - 1)),
                    deps=[tok_qsum_last, t_wr, w1["row1"]],
                    inc="spe" if d == NCH - 1 else None)
            tok_lg = E("act", lambda e: e.activation(
                rsm[0:1, 0:8], ps_A[0:1, 0:8], AF.Copy, scale=1.0 / NTOK_R),
                deps=[tok_lps], inc="sact")
            tok_lg2 = E("dve", lambda e: e.tensor_tensor(
                rsm[0:1, 8:16], rsm[0:1, 0:8], brsb[:], ALU.add),
                deps=[tok_lg, t_br], inc="sdve")
            tok_le = E("act", lambda e: e.activation(
                rsm[0:1, 16:24], rsm[0:1, 8:16], AF.Exp), deps=[tok_lg2], inc="sact")
            tok_ls = E("dve", lambda e: e.reduce_sum(
                rsm[0:1, 24:25], rsm[0:1, 16:24], axis=AX.X), deps=[tok_le], inc="sdve")
            tok_lr = E("dve", lambda e: e.reciprocal(
                rsm[0:1, 25:26], rsm[0:1, 24:25]), deps=[tok_ls], inc="sdve")
            tok_rt = E("dve", lambda e: e.tensor_scalar(
                rsm[0:1, 32:40], rsm[0:1, 16:24], rsm[0:1, 25:26], None, ALU.mult),
                deps=[tok_lr], inc="sdve")
            tok_rtb = E("dve", lambda e: e.tensor_copy(
                routing_b[:], rsm[0:1, 32:40]), deps=[tok_rt], inc="sdve")
            tok_rt1 = E("pe", lambda e: e.matmul(
                ps_A[0:8, 512:513], routing_b[:], ones1_sb[:], start=True, stop=True),
                deps=[tok_rtb, t_o1, w1["row2"]], inc="spe")
            tok_rtc = E("act", lambda e: e.activation(
                rtcol_b[:], ps_A[0:8, 512:513], AF.Copy), deps=[tok_rt1], inc="sact")
            tok_rsc = E("pe", lambda e: e.matmul(
                ps_A[:, 1024:1025], sel_sb[:], rtcol_b[:], start=True, stop=True),
                deps=[tok_rtc, t_sel, w1["bcr"]], inc="spe")
            tok_rscale = E("act", lambda e: e.activation(
                rscale[:], ps_A[:, 1024:1025], AF.Copy), deps=[tok_rsc], inc="sact")
            if upto == "rt":
                t1 = E("sp", lambda e: e.dma_start(dbg["d_rsc"], rscale[:]),
                       deps=[tok_rscale], inc="qsp", inc_by=16)
                t2 = E("sp", lambda e: e.dma_start(dbg["d_rsm"], rsm[:]),
                       deps=[tok_rtb], inc="qsp", inc_by=16)
                E("sp", lambda e: None, deps=[t1, t2])
                return

            bstage = [AR2f[:, 4096 + 512 * k:4096 + 512 * k + 512] for k in range(2)]
            btmp = [AR2f[:, 5120 + 512 * k:5120 + 512 * k + 512] for k in range(2)]
            dep_b_war = [tok_kv_stop[NW - 1], tok_kv_stop[NW - 2], tok_v_done,
                         tok_qsum_last]
            tok_bstage_rd = [None, None]
            tok_xn_own = {}
            tok_xns = {}
            for d in range(NCH):
                for wi in range(OWN_W):
                    k2 = wi % 2
                    t_x = E("sp", lambda e, d=d, wi=wi, k2=k2: e.dma_start(
                        bstage[k2], xT_own[wi, d]),
                        deps=[tok_bstage_rd[k2]], inc="qsp", inc_by=16)
                    E("dve", lambda e, wi=wi, k2=k2: e.tensor_tensor(
                        btmp[k2], bstage[k2], r_bc_own[:, 512 * wi:512 * wi + 512],
                        ALU.mult),
                        deps=[t_x, tok_own_r] + dep_b_war, inc=None)
                    tok = E("dve", lambda e, d=d, wi=wi, k2=k2: e.tensor_tensor(
                        AR1[:, 2048 * d + 512 * wi:2048 * d + 512 * wi + 512],
                        btmp[k2], rm_bc_own[:, 512 * wi:512 * wi + 512],
                        ALU.subtract),
                        deps=[tok_own_m], inc="sdve")
                    tok_bstage_rd[k2] = tok
                tok_xn_own[d] = tok
                tok_xns[d] = E("act", lambda e, d=d: e.dma_start(
                    xns[d].transpose([1, 0, 2]),
                    AR1[:, 2048 * d:2048 * d + 2048].rearrange(
                        "p (s f) -> p s f", s=4)),
                    deps=[tok_xn_own[d]], inc="qact", inc_by=16)
            dep_xn_own = [tok_xn_own[d] for d in range(NCH)]
            tok_xns_done = tok_xns[NCH - 1]

            wq_cb = [AR2[:, 0:4096], AR2[:, 4096:8192]]
            qpair = AR2[:, 8192:12288]
            tok_q_stop = {}
            tok_q_ev = {}
            tok_q_spill = {}
            for hd in range(H):
                P = hd % 2
                qps = ps_A if P == 0 else ps_B
                t_w = E("pool", lambda e, hd=hd, P=P: e.dma_start(
                    wq_cb[P].rearrange("p (c f) -> p c f", c=NCH),
                    wq[hd]),
                    deps=[tok_q_stop.get(hd - 2),
                          (tok_kv_stop[NW - 1] if hd <= 1 else None)],
                    inc="qpool", inc_by=16)
                extra = []
                if hd == 0:
                    extra = [tok_lg, tok_rtc, tok_rscale, w1["row1"], w1["row2"],
                             w1["bcr"], w1["bcm"]]
                if hd == 1:
                    extra = [tok_kev, tok_vev, tok_v_done]
                tok_mm = None
                for d in range(NCH):
                    for tb in range(4):
                        tok_mm = E("pe",
                            lambda e, P=P, d=d, tb=tb, qps=qps: e.matmul(
                                qps[:, 512 * tb:512 * tb + 512],
                                wq_cb[P][:, 128 * d:128 * d + 128],
                                AR1[:, 2048 * d + 512 * tb:2048 * d + 512 * tb + 512],
                                start=(d == 0), stop=(d == NCH - 1)),
                            deps=[t_w, tok_q_ev.get(hd - 2)] + extra +
                                 (dep_xn_own if hd == 0 else []),
                            inc="spe" if (d == NCH - 1 and tb == 3) else None)
                tok_q_stop[hd] = tok_mm
                m = hd % 2
                war_pair = tok_q_spill.get(2 * (hd // 2) - 1)
                tev = None
                for s4 in range(4):
                    tev = E("act", lambda e, hd=hd, s4=s4, m=m, qps=qps: e.activation(
                        qpair[:, 1024 * s4 + 512 * m:1024 * s4 + 512 * m + 512],
                        qps[:, 512 * s4:512 * s4 + 512],
                        AF.Identity, bias=bq_sb[:, hd:hd + 1]),
                        deps=[tok_mm, t_bq, war_pair],
                        inc="sact" if s4 == 3 else None)
                tok_q_ev[hd] = tev
                if hd % 2 == 1:
                    tsp = None
                    for s4 in range(4):
                        tsp = E("act", lambda e, hd=hd, s4=s4: e.dma_start(
                            qts2[s4, hd // 2],
                            qpair[:, 1024 * s4:1024 * s4 + 1024]),
                            deps=[tok_q_ev[hd], tok_q_ev[hd - 1]],
                            inc="qact", inc_by=16)
                    tok_q_spill[hd] = tsp
                    tok_q_spill[hd - 1] = tsp

            if upto == "b":
                t1 = E("sp", lambda e: e.dma_start(dbg["d_qts"], qts2),
                       deps=[tok_q_spill[H - 1]], inc="qsp", inc_by=16)
                t2 = E("sp", lambda e: e.dma_start(dbg["d_xns"], xns),
                       deps=[tok_xns_done], inc="qsp", inc_by=16)
                E("sp", lambda e: None, deps=[t1, t2] + sch.qbar("qact"))
                return

            qtb = [AR2[:, 0:1024], AR2[:, 1024:2048]]
            pg = [AR2[:, 2048 + 1024 * i:2048 + 1024 * i + 1024] for i in range(6)]
            aev = [AR2[:, 8192 + 512 * j:8192 + 512 * j + 512] for j in range(2)]
            lrow = AR2f[0:1, 4608:5632]
            sT = [ps_A[:, 0:1024], ps_A[:, 1024:2048]]
            accs_c = [ps_B[:, 0:512], ps_B[:, 512:1024]]
            lacc = [ps_B[0:1, 1024:1536], ps_B[0:1, 1536:2048]]

            dep_c_ar2 = [tok_q_stop[H - 1], tok_q_stop[H - 2],
                         tok_q_spill[H - 1], tok_q_spill[H - 2]]

            tok_mask = None
            tok_mask_use = None
            tok_qtb = {}
            tok_qk_last = {}
            tok_exp_tok = {}
            tok_pg_free = {}
            for _pp in range(6):
                tok_pg_free[_pp] = list(dep_c_ar2)
            tok_sT_free = [tok_q_ev[H - 2], tok_q_ev[H - 2]]
            tok_aev_dma = {}
            tok_lts_dma = {}
            tok_lev = None
            tok_aev_ev = [tok_q_ev[H - 1], tok_q_ev[H - 1]]
            tok_lacc_ev = [tok_q_ev[H - 1], tok_q_ev[H - 1]]

            for s in range(4):
                L = LIMS[s]
                tok_mask = E("sp", lambda e, s=s: e.dma_start(
                    mask_sb[:].rearrange("p (j f) -> p j f", j=8),
                    masks[s]),
                    deps=[tok_mask_use] + (dep_c_ar2 if s == 0 else []),
                    inc="qsp", inc_by=16)
                for hp in range(16):
                    h0 = hp * 2
                    Pq = hp % 2
                    gi = s * 16 + hp
                    tok_qtb[gi] = E("sp", lambda e, hp=hp, s=s, Pq=Pq: e.dma_start(
                        qtb[Pq], qts2[s, hp]),
                        deps=[tok_q_spill[h0 + 1], tok_qk_last.get(gi - 2)] +
                             (dep_c_ar2 if gi < 2 else []),
                        inc="qsp", inc_by=16)
                    npairs = (L + 1) // 2
                    tok_av = {}
                    tok_masked = {}
                    LAG = 2
                    for pr in range(npairs + LAG):
                        if pr < npairs:
                            for kc in (2 * pr, 2 * pr + 1):
                                if kc >= L:
                                    continue
                                Ps = kc % 2
                                qk_tok = None
                                for h in range(2):
                                    qk_tok = E("pe",
                                        lambda e, kc=kc, h=h, Ps=Ps, Pq=Pq: e.matmul(
                                            sT[Ps][:, 512 * h:512 * h + 512],
                                            kT_sb[:, 128 * kc:128 * kc + 128],
                                            qtb[Pq][:, 512 * h:512 * h + 512],
                                            start=True, stop=True),
                                        deps=[tok_qtb[gi], tok_kT_done,
                                              tok_sT_free[Ps]],
                                        inc="spe" if h == 1 else None)
                                Pp = kc % 6
                                texp = E("act", lambda e, Pp=Pp, Ps=Ps: e.activation(
                                    pg[Pp][:], sT[Ps][:], AF.Exp, scale=SM_SCALE),
                                    deps=[qk_tok] + tok_pg_free.get(Pp, []),
                                    inc="sact" if (kc % 2 == 1 or kc == L - 1)
                                    else None)
                                if kc % 2 == 1 or kc == L - 1:
                                    tok_exp_tok[pr] = texp
                                    tok_sT_free[0] = texp
                                    tok_sT_free[1] = texp
                        if pr >= LAG:
                            for kc in (2 * (pr - LAG), 2 * (pr - LAG) + 1):
                                if kc >= L:
                                    continue
                                Pp = kc % 6
                                if kc >= L - 8:
                                    j = kc - (L - 8)
                                    tmm = None
                                    for h in range(2):
                                        tmm = E("dve",
                                            lambda e, h=h, j=j, Pp=Pp: e.tensor_tensor(
                                                pg[Pp][:, 512 * h:512 * h + 512],
                                                pg[Pp][:, 512 * h:512 * h + 512],
                                                mask_sb[:, 512 * j:512 * j + 512],
                                                ALU.mult),
                                            deps=[tok_exp_tok[pr - LAG], tok_mask],
                                            inc="sdve" if h == 1 else None)
                                    tok_masked[kc] = tmm
                                    tok_mask_use = tmm
                                dep_p = [tok_exp_tok[pr - LAG], tok_masked.get(kc)]
                                for h in range(2):
                                    E("pe", lambda e, h=h, kc=kc, Pp=Pp, L=L: e.matmul(
                                        lacc[h], onescolb_sb[:],
                                        pg[Pp][:, 512 * h:512 * h + 512],
                                        start=(kc == 0), stop=(kc == L - 1)),
                                        deps=dep_p + [t_ocb, tok_lacc_ev[h]],
                                        inc=None)
                                for h in range(2):
                                    tok_av[(kc, h)] = E("pe",
                                        lambda e, h=h, kc=kc, Pp=Pp, L=L: e.matmul(
                                            accs_c[h],
                                            v_sb[:, 128 * kc:128 * kc + 128],
                                            pg[Pp][:, 512 * h:512 * h + 512],
                                            start=(kc == 0), stop=(kc == L - 1)),
                                        deps=dep_p + [tok_v_done, tok_aev_ev[h]],
                                        inc="spe")
                                tok_pg_free[Pp] = [tok_av[(kc, 1)]]
                    tok_qk_last[gi] = tok_exp_tok[npairs - 1]
                    for h in range(2):
                        E("dve", lambda e, h=h: e.tensor_copy(
                            aev[h], accs_c[h]),
                            deps=[tok_av[(L - 1, h)],
                                  tok_aev_dma.get((gi - 1, h))], inc=None)
                    for h in range(2):
                        tok_lev = E("dve", lambda e, h=h: e.tensor_copy(
                            lrow[:, 512 * h:512 * h + 512], lacc[h]),
                            deps=[tok_av[(L - 1, 1)], tok_lts_dma.get(gi - 1)],
                            inc="sdve" if h == 1 else None)
                    for h in range(2):
                        tok_aev_ev[h] = tok_lev
                        tok_lacc_ev[h] = tok_lev
                        tok_aev_dma[(gi, h)] = E("sp",
                            lambda e, h=h, h0=h0, s=s: e.dma_start(
                                ats[h0 + h, s], aev[h]),
                            deps=[tok_lev], inc="qsp", inc_by=16)
                    tok_lts_dma[gi] = E("sp", lambda e, h0=h0, s=s: e.dma_start(
                        lts[h0:h0 + 2, s].unsqueeze(0),
                        lrow[:].rearrange("p (s2 f) -> p s2 f", s2=2)),
                        deps=[tok_lev], inc="qsp", inc_by=16)

            t_bo = E("sp", lambda e: e.dma_start(bo_sb[:], bo), inc="qsp", inc_by=16)
            t_bf = E("sp", lambda e: e.dma_start(bf_sb[:], bfb), inc="qsp", inc_by=16)
            tok_bsum = E("dve", lambda e: e.tensor_tensor(
                bsum[:], bo_sb[:], bf_sb[:], ALU.add), deps=[t_bo, t_bf], inc="sdve")

            bar_pe = sch.bar("spe")
            bar_act = sch.bar("sact")
            bar_dve = sch.bar("sdve")
            bar_pool = sch.bar("spool")
            bar_qact_l = sch.qbar("qact")
            bar_qsp_l = sch.qbar("qsp")

            wcb = [AR2[:, 0:4096], AR2[:, 4096:8192], AR2[:, 8192:12288]]
            wB_t = [AR2[:, 12288:12416], AR2[:, 12416:12544]]
            tp_h = AR2[:, 12544:13568]
            linv_bc = AR2[:, 13568:14592]
            lrowD = AR3[0:1, 4096:5120]
            linv_row = AR2[0:1, 14592:15616]
            xst = [AR3[:, 0:1024], AR3[:, 1024:2048]]
            acts = [AR3[:, 2048:3072], AR3[:, 3072:4096]]
            accs = [ps_A[:, 0:1024], ps_A[:, 1024:2048], ps_B[:, 0:1024]]
            t_ps = ps_B[:, 1024:2048]

            tok_ath = {}
            tok_lath = {}
            tok_xrh = {}
            tok_wcb = {}
            tok_wBt = {}
            tok_acc_stop = {}
            tok_acts = {}
            tok_dadd = {}
            tok_out = {}
            tok_tp_ev = None
            tok_tp_stop = None

            dep_d_ps = [tok_lev, tok_sT_free[0]]

            for hh in range(2):
                dep_half = [tok_acc_stop.get((hh - 1, NCH - 1))] if hh > 0 else \
                    [tok_q_stop[H - 1], tok_xns_done]
                t_wA = E("pool", lambda e: e.dma_start(
                    AR2[:, 0:4096].rearrange("p (c f) -> p c f", c=NCH),
                    wA),
                    deps=dep_half + ([bar_pe, bar_act, bar_dve] + bar_qact_l +
                                     bar_qsp_l if hh == 0 else []),
                    inc="qpool", inc_by=16)
                for h in range(H):
                    tok_ath[(hh, h)] = E("sp", lambda e, hh=hh, h=h: e.dma_start(
                        AR1[:, 1024 * h:1024 * h + 1024].rearrange(
                            "p (s f) -> p s f", s=2),
                        ats[h, 2 * hh:2 * hh + 2].transpose([1, 0, 2])),
                        deps=[tok_aev_dma[(2 * hh * 16 + h // 2, h % 2)],
                              tok_aev_dma[((2 * hh + 1) * 16 + h // 2, h % 2)],
                              tok_lath.get((hh - 1, h))] +
                             dep_half,
                        inc="qsp", inc_by=16)
                for d in range(NCH):
                    tok_xrh[(hh, d)] = E("sp", lambda e, hh=hh, d=d: e.dma_start(
                        AR1[:, 32768 + 1024 * d:32768 + 1024 * d + 1024].rearrange(
                            "p (s f) -> p s f", s=2),
                        xns[d, 2 * hh:2 * hh + 2].transpose([1, 0, 2])),
                        deps=[tok_xns_done] + dep_half, inc="qsp", inc_by=16)

                tok_expl = None
                tok_bc_prev = None
                tok_bc_ev = dep_d_ps
                for h in range(H):
                    t_lr = E("sp", lambda e, hh=hh, h=h: e.dma_start(
                        lrowD.rearrange("p (s f) -> p s f", s=2),
                        lts[h, 2 * hh:2 * hh + 2].unsqueeze(0)),
                        deps=[tok_lts_dma[(2 * hh) * 16 + h // 2],
                              tok_lts_dma[(2 * hh + 1) * 16 + h // 2],
                              tok_expl] + dep_half,
                        inc="qsp", inc_by=16)
                    tok_ln_l = E("act", lambda e: e.activation(
                        lrowD, lrowD, AF.Ln),
                        deps=[t_lr], inc=None)
                    tok_expl = E("act", lambda e: e.activation(
                        linv_row, lrowD, AF.Exp, scale=-1.0),
                        deps=[tok_bc_prev], inc="sact")
                    tok_bc = None
                    for tb in range(2):
                        tok_bc = E("pe", lambda e, tb=tb: e.matmul(
                            t_ps[:, 512 * tb:512 * tb + 512],
                            onesrowb_sb[:],
                            linv_row[:, 512 * tb:512 * tb + 512],
                            start=True, stop=True),
                            deps=[tok_expl, t_orb] + tok_bc_ev + dep_d_ps,
                            inc="spe" if tb == 1 else None)
                    tok_bc_prev = tok_bc
                    tok_ev = E("dve", lambda e: e.tensor_copy(
                        linv_bc[:], t_ps[:]),
                        deps=[tok_bc], inc=None)
                    tok_lath[(hh, h)] = E("dve", lambda e, h=h: e.tensor_tensor(
                        AR1[:, 1024 * h:1024 * h + 1024],
                        AR1[:, 1024 * h:1024 * h + 1024], linv_bc[:], ALU.mult),
                        deps=[tok_ath[(hh, h)]], inc="sdve")
                    tok_bc_ev = [tok_lath[(hh, h)]]

                for d in range(NCH):
                    for tb in range(2):
                        tok_tp_stop = E("pe", lambda e, d=d, tb=tb: e.matmul(
                            t_ps[:, 512 * tb:512 * tb + 512],
                            AR2[:, 128 * d:128 * d + 128],
                            AR1[:, 32768 + 1024 * d + 512 * tb:
                                32768 + 1024 * d + 512 * tb + 512],
                            start=(d == 0), stop=(d == NCH - 1)),
                            deps=[t_wA, tok_xrh[(hh, d)], tok_tp_ev,
                                  tok_bc_ev[0]] + dep_d_ps,
                            inc="spe" if (d == NCH - 1 and tb == 1) else None)
                tok_tp_ev = E("dve", lambda e: e.tensor_scalar(
                    tp_h[:], t_ps[:], rscale[:], None, ALU.mult),
                    deps=[tok_tp_stop, tok_rscale,
                          *(bar_qact_l if hh == 0 else [None])], inc="sdve")

                for o in range(NCH):
                    u0 = (2 * o) % 3
                    u1 = (2 * o + 1) % 3
                    dep_w0 = [tok_acc_stop.get((hh, o - 2))]
                    if hh == 0 and o == 0:
                        dep_w0 += [tok_tp_stop]
                    if hh == 1 and o == 0:
                        dep_w0 += [tok_tp_stop, tok_acc_stop.get((0, NCH - 1))]
                    if hh == 0 and o == 1:
                        dep_w0 += bar_qact_l + [bar_dve]
                    tok_wcb[(hh, o, 0)] = E("pool", lambda e, o=o, u0=u0: e.dma_start(
                        wcb[u0].rearrange("p (c f) -> p c f", c=NCH),
                        wo[o]),
                        deps=dep_w0, inc="qpool", inc_by=16)
                    tok_wcb[(hh, o, 1)] = E("pool", lambda e, o=o, u1=u1: e.dma_start(
                        wcb[u1].rearrange("p (c f) -> p c f", c=NCH),
                        wW[o]),
                        deps=[tok_acc_stop.get((hh, o - 1)),
                              (tok_acc_stop.get((0, NCH - 1))
                               if (hh == 1 and o == 0) else None)] +
                             ([bar_pe, bar_dve, bar_pool]
                              if (hh == 0 and o == 0) else []),
                        inc="qpool", inc_by=16)
                    tok_wBt[(hh, o)] = E("pool", lambda e, o=o: e.dma_start(
                        wB_t[o % 2], wB[o]),
                        deps=[tok_acc_stop.get((hh, o - 2)),
                              *(bar_qact_l if (hh == 0 and o <= 1) else [None]),
                              (tok_acc_stop.get((0, 30 + o))
                               if (hh == 1 and o < 2) else None)],
                        inc="qpool", inc_by=16)
                    acc = accs[o % 3]
                    tok_mm = None
                    for h in range(H):
                        for tb in range(2):
                            tok_mm = E("pe",
                                lambda e, h=h, tb=tb, u0=u0, acc=acc: e.matmul(
                                    acc[:, 512 * tb:512 * tb + 512],
                                    wcb[u0][:, 128 * h:128 * h + 128],
                                    AR1[:, 1024 * h + 512 * tb:
                                        1024 * h + 512 * tb + 512],
                                    start=(h == 0), stop=False),
                                deps=[tok_wcb[(hh, o, 0)], tok_lath[(hh, h)],
                                      tok_acts.get((hh, o - 3)),
                                      (tok_acts.get((hh - 1, [30, 31, 29][o]))
                                       if (hh > 0 and o < 3) else None)],
                                inc=None)
                    for d in range(NCH):
                        for tb in range(2):
                            tok_mm = E("pe",
                                lambda e, d=d, tb=tb, u1=u1, acc=acc: e.matmul(
                                    acc[:, 512 * tb:512 * tb + 512],
                                    wcb[u1][:, 128 * d:128 * d + 128],
                                    AR1[:, 32768 + 1024 * d + 512 * tb:
                                        32768 + 1024 * d + 512 * tb + 512],
                                    start=False, stop=False),
                                deps=[tok_wcb[(hh, o, 1)], tok_xrh[(hh, d)]],
                                inc=None)
                    for tb in range(2):
                        tok_mm = E("pe", lambda e, tb=tb, o=o, acc=acc: e.matmul(
                            acc[:, 512 * tb:512 * tb + 512], wB_t[o % 2][:],
                            tp_h[:, 512 * tb:512 * tb + 512],
                            start=False, stop=True),
                            deps=[tok_wBt[(hh, o)], tok_tp_ev],
                            inc="spe" if tb == 1 else None)
                    tok_acc_stop[(hh, o)] = tok_mm
                    Po = o % 2
                    prev_out = tok_out.get((hh, o - 2)) if o >= 2 else (
                        tok_out.get((hh - 1, NCH - 2 + o)) if hh > 0 else None)
                    tok_acts[(hh, o)] = E("act",
                        lambda e, o=o, Po=Po, acc=acc: e.activation(
                            acts[Po], acc[:], AF.Identity, bias=bsum[:, o:o + 1]),
                        deps=[tok_mm, tok_bsum, prev_out,
                              (bar_dve if (hh == 0 and o < 2) else None)],
                        inc="sact")
                    prev_xst = tok_dadd.get((hh, o - 2)) if o >= 2 else (
                        tok_dadd.get((hh - 1, NCH - 2 + o)) if hh > 0 else
                        (bar_pe if o < 2 else None))
                    t_xs = E("sp", lambda e, hh=hh, o=o, Po=Po: e.dma_start(
                        xst[Po].rearrange("p (s f) -> p s f", s=2),
                        xT_own[2 * hh:2 * hh + 2, o].transpose([1, 0, 2])),
                        deps=[prev_xst,
                              (sch.bar("spe") if (hh == 0 and o < 2) else None)],
                        inc="qsp", inc_by=16)
                    tok_dadd[(hh, o)] = E("dve", lambda e, Po=Po: e.tensor_tensor(
                        acts[Po], acts[Po], xst[Po], ALU.add),
                        deps=[tok_acts[(hh, o)], t_xs], inc="sdve")
                    tok_out[(hh, o)] = E("act", lambda e, hh=hh, o=o, Po=Po: e.dma_start(
                        out_t[o, 2 * hh:2 * hh + 2].transpose([1, 0, 2]),
                        acts[Po].rearrange("p (s f) -> p s f", s=2)),
                        deps=[tok_dadd[(hh, o)]], inc="qact", inc_by=16)

            fin = sch.qbar("qact") + sch.qbar("qsp") + sch.qbar("qpool")
            E("sp", lambda e: None, deps=fin)

        _emit_phases()
        build_program._last_sched = sch

        _flush_op = {
            "act": lambda e: e.copy(scr_act[:], scr_act[:]),
            "dve": lambda e: e.tensor_copy(scr_dve[:], scr_dve[:]),
            "pool": lambda e: e.tensor_copy(scr_pool[:], scr_pool[:]),
        }
        _flush_sems = ("sact", "sdve", "spool")

        def run_stream(name, eng):
            for wl, fn, inc, inc_by in sch.streams[name]:
                for snm, v in wl:
                    eng.wait_ge(sems[snm], v)
                ins = fn(eng)
                if inc is not None:
                    if name in _flush_op and inc in _flush_sems:
                        eng.drain(fusable=False)
                        _flush_op[name](eng).then_inc(sems[inc], inc_by)
                    else:
                        ins.then_inc(sems[inc], inc_by)

        @block.tensor
        def _(eng):
            run_stream("pe", eng)

        @block.scalar
        def _(eng):
            run_stream("act", eng)

        @block.vector
        def _(eng):
            run_stream("dve", eng)

        @block.gpsimd
        def _(eng):
            run_stream("pool", eng)

        @block.sync
        def _(eng):
            run_stream("sp", eng)

    return nc



_NC_CACHE = {}


def _get_program():
    if "nc" not in _NC_CACHE:
        _NC_CACHE["nc"] = build_program()
    return _NC_CACHE["nc"]


def _pack_weights(inp):
    f32 = np.float32
    wq_w = np.asarray(inp["L0_attn_wq_w"], f32)
    wk_w = np.asarray(inp["L0_attn_wk_w"], f32)
    wv_w = np.asarray(inp["L0_attn_wv_w"], f32)
    wo_w = np.asarray(inp["L0_attn_wo_w"], f32)
    W = np.asarray(inp["L0_ffn_W"], f32)
    A = np.asarray(inp["L0_ffn_A"], f32)
    Bm = np.asarray(inp["L0_ffn_B"], f32)
    rW = np.asarray(inp["L0_ffn_router_W"], f32)

    def colblocks(w):
        wt = np.ascontiguousarray(w.T)
        return np.ascontiguousarray(
            wt.reshape(NCH, 128, NCH, 128).transpose(2, 0, 1, 3))

    def dblocks(w):
        wt = np.ascontiguousarray(w.T)
        return np.ascontiguousarray(wt.reshape(NCH, 128, w.shape[0]))

    def cb_t(w):
        return np.ascontiguousarray(colblocks(w).transpose(0, 2, 1, 3))

    def db_t(w):
        return np.ascontiguousarray(dblocks(w).transpose(1, 0, 2))

    d = {}
    d["wq"] = cb_t(wq_w)
    d["wo"] = cb_t(wo_w)
    d["wW"] = cb_t(W)
    d["wk"] = db_t(wk_w)
    d["wv"] = db_t(wv_w)
    A_all = A.reshape(128, D)
    d["wA"] = db_t(A_all)
    B_all = Bm.transpose(0, 2, 1).reshape(128, D)
    d["wB"] = np.ascontiguousarray(
        B_all.reshape(128, NCH, 128).transpose(1, 0, 2))
    d["wr"] = db_t(rW)
    d["bq"] = np.ascontiguousarray(
        np.asarray(inp["L0_attn_wq_b"], f32).reshape(NCH, 128).T)
    d["bk"] = np.asarray(inp["L0_attn_wk_b"], f32).reshape(128, 1)
    d["bv"] = np.asarray(inp["L0_attn_wv_b"], f32).reshape(128, 1)
    d["bo"] = np.ascontiguousarray(
        np.asarray(inp["L0_attn_wo_b"], f32).reshape(NCH, 128).T)
    d["bfb"] = np.ascontiguousarray(
        np.asarray(inp["L0_ffn_b"], f32).reshape(NCH, 128).T)
    d["br"] = np.asarray(inp["L0_ffn_router_b"], f32).reshape(1, 8)
    d["identf"] = np.eye(128, dtype=f32)
    d["onescol_f"] = np.ones((128, 1), f32)
    d["onescol_bf"] = np.ones((128, 1), ml_dtypes.bfloat16)
    d["onesrow_f"] = np.ones((1, 128), f32)
    d["onesrow_b"] = np.ones((1, 128), ml_dtypes.bfloat16)
    d["ones1_b"] = np.ones((1, 1), ml_dtypes.bfloat16)
    sel = np.zeros((8, 128), f32)
    for e_ in range(8):
        sel[e_, e_ * 16:(e_ + 1) * 16] = 2.0
    d["sel_b"] = sel.astype(ml_dtypes.bfloat16)
    return d


def _build_masks(own_supers):
    m = np.zeros((4, 8, 128, 512), np.float32)
    tri = np.triu(np.ones((128, 128), np.float32))
    for s, g in enumerate(own_supers):
        L = LIMS[s]
        for j in range(8):
            kc = L - 8 + j
            for qt in range(4):
                gq = 4 * g + qt
                if kc < gq:
                    m[s, j, :, qt * 128:(qt + 1) * 128] = 1.0
                elif kc == gq:
                    m[s, j, :, qt * 128:(qt + 1) * 128] = tri
    return np.ascontiguousarray(m.transpose(0, 2, 1, 3)).astype(ml_dtypes.bfloat16)


def kernel(**inputs):
    x = np.asarray(inputs["L0_attn_x"], np.float32)
    wd = _pack_weights(inputs)
    masks_by_p = {p: _build_masks(OWN_SUPER[p]) for p in (0, 1)}

    in_maps = []
    for c in range(8):
        b, p = c // 2, c % 2
        owns = OWN_SUPER[p]
        xb = np.ascontiguousarray(x[b])
        xTb = np.ascontiguousarray(xb.T)
        xT_blocks = np.ascontiguousarray(
            xTb.reshape(NCH, 128, NW, 512).transpose(2, 0, 1, 3))
        own_rows = np.concatenate(
            [np.arange(512 * g, 512 * (g + 1)) for g in owns])
        xtok_own = np.ascontiguousarray(xb[own_rows])
        xT_own = np.ascontiguousarray(xT_blocks[owns])
        m = {"x_glob": xb, "xtok_own": xtok_own, "xT_glob": xT_blocks,
             "xT_own": xT_own, "masks": masks_by_p[p]}
        m.update(wd)
        in_maps.append(m)

    nc = _get_program()
    res = run_bass_kernel_spmd(nc, in_maps, core_ids=list(range(8)))

    out = np.empty((4, 4096, 4096), np.float32)
    for c in range(8):
        b, p = c // 2, c % 2
        owns = OWN_SUPER[p]
        r = res.results[c]["out"]
        arr = np.ascontiguousarray(r.transpose(1, 3, 0, 2)).reshape(4, 512, 4096)
        for wi, g in enumerate(owns):
            out[b, 512 * g:512 * (g + 1), :] = arr[wi]
    return out


if __name__ == "__main__":
    import reference
    inp = reference.setup_inputs()
    o = kernel(**{k: np.asarray(v) for k, v in inp.items()})
    ref = np.asarray(reference.reference(**inp))
    err = np.abs(o - ref)
    rel = np.sqrt(((o - ref) ** 2).mean() / (ref ** 2).mean())
    print("max abs err:", err.max(), "rel l2:", rel)
